# version 24
# baseline (speedup 1.0000x reference)
import contextlib
import numpy as np
import ml_dtypes
import concourse.bass as bass
import concourse.mybir as mybir
from concourse.bass_utils import run_bass_kernel_spmd
from concourse.bass import IndirectOffsetOnAxis

F32 = mybir.dt.float32
BF16 = mybir.dt.bfloat16
I32 = mybir.dt.int32
AF = mybir.ActivationFunctionType
ALU = mybir.AluOpType
AX = mybir.AxisListType

D = 1024
HID = 1408
ND = 8
NH = 11
NEXP = 8
CH = 512
KC = 1024
EPS = 1e-5
SM_SCALE = 1.0 / 32.0

ENGS = ("sync", "gpsimd", "vector", "scalar", "tensor")


def _names(aps):
    out = set()
    for a in aps:
        t = getattr(a, "tensor", None)
        out.add(t.name if t is not None else str(a))
    return out


class B:

    def __init__(self, nc):
        self.nc = nc
        self.th = {e: [] for e in ENGS}
        self.cnt = {}
        self.handles = {}
        self.dw = {"vector": set(), "scalar": set()}
        self.dr = {"vector": set(), "scalar": set()}

    def sem(self, name, stack):
        self.handles[name] = stack.enter_context(self.nc.semaphore(name))
        self.cnt[name] = 0

    def wait(self, eng, sem, val):
        if val is None or val <= 0:
            return
        h = self.handles[sem]
        self.th[eng].append(lambda e, h=h, v=val: e.wait_ge(h, v))

    def wait_g(self, eng, guard):
        if guard:
            self.wait(eng, guard[0], guard[1])

    def emit(self, eng, fn, r=(), w=(), inc=None, n=1):
        if eng in self.dw:
            nr, nw = _names(r), _names(w)
            dw, dr = self.dw[eng], self.dr[eng]
            if (nr & dw) or (nw & (dw | dr)):
                self.th[eng].append(lambda e: e.drain())
                dw.clear()
                dr.clear()
            dw |= nw
            dr |= nr
        if inc is not None:
            self.cnt[inc] += n
            mark = self.cnt[inc]
            h = self.handles[inc]
            self.th[eng].append(
                lambda e, fn=fn, h=h, k=n: fn(e).then_inc(h, k))
            return mark
        self.th[eng].append(fn)
        return None

    def drain_all(self):
        for e in self.dw:
            self.th[e].append(lambda eng: eng.drain())
            self.dw[e].clear()
            self.dr[e].clear()

    def barrier(self):
        self.drain_all()
        snap = dict(self.cnt)
        for eng in ENGS:
            for s, c in snap.items():
                self.wait(eng, s, c)


def build_nc(S=4096):
    NT = S // 128
    NCH = S // CH
    TPC = CH // 128
    NKC = S // KC
    NCNK = S // 256 + 8
    CAPR = NCNK * 512
    NGT = CAPR // 128

    nc = bass.Bass("TRN2", target_bir_lowering=False)

    dt_ = nc.dram_tensor
    x_d = dt_("x", [S, D], F32, kind="ExternalInput").ap()
    wqT_d = dt_("wqT", [D, D], F32, kind="ExternalInput").ap()
    wkT_d = dt_("wkT", [D, D], F32, kind="ExternalInput").ap()
    wve_d = dt_("wv_e", [D, D], F32, kind="ExternalInput").ap()
    wpT_d = dt_("wpT", [D, D], F32, kind="ExternalInput").ap()
    bq_d = dt_("bq_pe", [128, ND], F32, kind="ExternalInput").ap()
    bk_d = dt_("bk_pe", [128, ND], F32, kind="ExternalInput").ap()
    bv_d = dt_("bv_pe", [128, ND], F32, kind="ExternalInput").ap()
    bp_d = dt_("bp_row", [1, D], F32, kind="ExternalInput").ap()
    gw_d = dt_("gwT", [D, NEXP], F32, kind="ExternalInput").ap()
    eb_d = dt_("eb_row", [1, NEXP], F32, kind="ExternalInput").ap()
    wup_d = dt_("wupT", [9 * D, HID], F32, kind="ExternalInput").ap()
    wdn_d = dt_("wdnT", [9 * HID, D], F32, kind="ExternalInput").ap()
    idb_d = dt_("identb", [128, 128], BF16, kind="ExternalInput").ap()

    out_d = dt_("out", [S, D], F32, kind="ExternalOutput").ap()

    xnT_h = dt_("xnT_h", [NCH * 128, ND * CH], BF16, kind="Internal").ap()
    kT_h = dt_("kT_h", [NCH * 128, ND * CH], BF16, kind="Internal").ap()
    qT_h = dt_("qT_h", [NT * 128, ND * 128], BF16, kind="Internal").ap()
    attn_h = dt_("attn_h", [S, D], BF16, kind="Internal").ap()
    bprime_h = dt_("bprime_h", [1, D], F32, kind="Internal").ap()
    xn_h = dt_("xn_h", [S, D], BF16, kind="Internal").ap()
    wupb_h = dt_("wupb_h", [9 * 128, ND * HID], BF16, kind="Internal").ap()
    wdnb_h = dt_("wdnb_h", [9 * 128, NH * D], BF16, kind="Internal").ap()
    moe_h = dt_("moe_h", [S, D], F32, kind="Internal").ap()
    idx_h = dt_("idx_h", [NCNK * 512, 2], I32, kind="Internal").ap()
    scr_h = dt_("scr_h", [64, 256], F32, kind="Internal").ap()

    with contextlib.ExitStack() as top:
        b = B(nc)
        for s in ("s_x0", "s_x1", "s_x2", "s_x3", "s_sp0", "s_sp1", "s_kt", "s_qt0", "s_qt1",
                  "s_ast0", "s_ast1", "s_xnt0", "s_xnt1", "s_fin0", "s_fin1",
                  "s_out0", "s_out1", "s_c1", "s_bp", "g_wa", "g_wb", "g_wu",
                  "g_wd", "g_wc", "g_ix", "g_tg0", "g_tg1", "g_wg0", "g_wg1", "g_sa0", "g_sa1", "s_xh",
                  "s_sh", "s_ig", "s_mf0", "s_mf1", "g_ib", "gm", "tm", "vm", "sm"):
            b.sem(s, top)

        pstack = top.enter_context(contextlib.ExitStack())
        _p = pstack.enter_context
        idb = _p(nc.sbuf_tensor("idb", [128, 128], BF16))[:]
        bqc = _p(nc.sbuf_tensor("bqc", [128, ND], F32))[:]
        bqc32 = _p(nc.sbuf_tensor("bqc32", [128, ND], F32))[:]
        bkc = _p(nc.sbuf_tensor("bkc", [128, ND], F32))[:]
        bvc = _p(nc.sbuf_tensor("bvc", [128, ND], BF16))[:]
        bpr = _p(nc.sbuf_tensor("bpr", [1, D], F32))[:]
        ebc = _p(nc.sbuf_tensor("ebc", [128, NEXP], F32))[:]
        bprime = _p(nc.sbuf_tensor("bprime", [1, D], F32))[:]
        bpbc = _p(nc.sbuf_tensor("bpbc", [128, D], F32))[:]
        wc_sb = _p(nc.sbuf_tensor("wc_sb", [128, ND, D], BF16))[:]
        s2tok = _p(nc.sbuf_tensor("s2tok", [128, NT, NEXP], F32))[:]
        en_all = _p(nc.sbuf_tensor("en_all", [128, NT, NEXP], F32))[:]
        mask_all = _p(nc.sbuf_tensor("mask_all", [128, NT, NEXP], F32))[:]
        idxgw_i = _p(nc.sbuf_tensor("idxgw_i", [128, NCNK], I32))[:]

        psA = _p(nc.psum_tensor("psA", [128, 1024], F32))[:]
        psB = _p(nc.psum_tensor("psB", [128, 1024], F32))[:]
        psC = _p(nc.psum_tensor("psC", [128, 1024], F32))[:]
        pT = [_p(nc.psum_tensor(f"pT{i}", [128, 1024], BF16))[:]
              for i in range(2)]
        pairs = [psA, psB]

        def cst(dst, src):
            b.emit("sync", lambda e, d=dst, s=src: e.dma_start(out=d, in_=s),
                   inc="s_c1", n=16)
        cst(idb, idb_d)
        cst(bqc, bq_d)
        cst(bkc, bk_d)
        cst(bpr, bp_d)
        cst(ebc, eb_d.broadcast_to([128, NEXP]))
        c1_final = b.cnt["s_c1"]

        regs = {}
        b.emit("gpsimd", lambda e: regs.update(
            capr=e.to_reg(CAPR - 1), sm1=e.to_reg(S - 1)))

        def wcast(dst, src, sem="g_wa"):
            return b.emit("gpsimd",
                          lambda e, d=dst, s=src: e.dma_start(out=d, in_=s),
                          inc=sem, n=16)

        xn_stack = contextlib.ExitStack()
        xn = xn_stack.enter_context(nc.sbuf_tensor("xn", [128, NT, D], BF16))[:]

        ph0 = contextlib.ExitStack()
        wve_sb = ph0.enter_context(
            nc.sbuf_tensor("wve_sb", [128, ND, D], BF16))[:]
        wpt_sb = ph0.enter_context(
            nc.sbuf_tensor("wpt_sb", [128, ND, D], BF16))[:]
        wcast(wve_sb, wve_d.rearrange("(j p) e -> p j e", p=128), "g_wb")
        wcast(wpt_sb, wpT_d.rearrange("(j p) e -> p j e", p=128), "g_wb")
        wcast(bvc, bv_d, "g_wb")
        wv_final = b.cnt["g_wb"]
        ph0.close()

        pha = contextlib.ExitStack()
        _a = pha.enter_context
        qbuf = [_a(nc.sbuf_tensor(f"qbuf{i}", [128, TPC, ND * 128], BF16))[:]
                for i in range(2)]
        kbuf = [_a(nc.sbuf_tensor(f"kbuf{i}", [128, ND, CH], BF16))[:]
                for i in range(2)]
        xbuf = [_a(nc.sbuf_tensor(f"xbuf{i}", [128, D], F32))[:] for i in range(4)]
        stats = _a(nc.sbuf_tensor("stats", [128, 12], F32))[:]
        aggr = _a(nc.sbuf_tensor("aggr", [128, 2], F32))[:]
        lnv = [_a(nc.sbuf_tensor(f"lnv{i}", [128, 1], F32))[:] for i in range(2)]
        negm = [_a(nc.sbuf_tensor(f"negm{i}", [128, 1], F32))[:] for i in range(2)]
        rstd_s = _a(nc.sbuf_tensor("rstd_s", [128, 1], F32))[:]
        nmr_s = _a(nc.sbuf_tensor("nmr_s", [128, 1], F32))[:]
        xnTb = [_a(nc.sbuf_tensor(f"xnTb{i}", [128, ND, CH], BF16))[:]
                for i in range(2)]
        wk_sb = _a(nc.sbuf_tensor("wk_sb", [128, ND, D], BF16))[:]
        wq_sb = _a(nc.sbuf_tensor("wq_sb", [128, ND, D], BF16))[:]
        gw_sb = _a(nc.sbuf_tensor("gw_sb", [128, ND, NEXP], BF16))[:]
        sgt = _a(nc.sbuf_tensor("sgt", [128, NEXP], F32))[:]
        sg = _a(nc.sbuf_tensor("sg", [128, NEXP], F32))[:]
        bsd = _a(nc.sbuf_tensor("bsd", [128, NEXP], F32))[:]
        top8 = _a(nc.sbuf_tensor("top8", [128, NEXP], F32))[:]
        mskb = _a(nc.sbuf_tensor("mskb", [128, NEXP], F32))[:]
        smv = _a(nc.sbuf_tensor("smv", [128, NEXP], F32))[:]

        wcast(wk_sb, wkT_d.rearrange("(j p) e -> p j e", p=128))
        wcast(wq_sb, wqT_d.rearrange("(j p) e -> p j e", p=128))
        wcast(gw_sb, gw_d.rearrange("(j p) e -> p j e", p=128))
        gw_final = b.cnt["g_wa"]

        b.wait("vector", "s_c1", c1_final)
        b.emit("vector", lambda e: e.tensor_scalar_mul(bqc32[:], bqc[:], SM_SCALE),
               r=[bqc], w=[bqc32], inc="vm")

        kslice = [psA[:, 0:512], psA[:, 512:1024], psB[:, 0:512], psB[:, 512:1024]]
        qslice = [psC[:, 0:512], psC[:, 512:1024]]
        gslice = psC[:, 0:NEXP]

        bankg = {}
        trg = {}

        sm_xn = {}
        vm_ln = {}
        vm_xnt = {}
        tr_mm = {}
        gmm_d = {}
        en_ev = {}
        tm_smm = {}
        vm_fin = {}
        qmm_last = {}

        def stage_A(c):
            if c < 4:
                b.wait("sync", "s_c1", c1_final + 16 * c)
                b.emit("sync",
                       lambda e, d=xbuf[c % 4], s=x_d[c * 128:(c + 1) * 128, :]:
                       e.dma_start(out=d, in_=s), inc="s_c1", n=16)
            else:
                b.wait("gpsimd", "sm", sm_xn.get(c - 4))
                b.emit("gpsimd",
                       lambda e, d=xbuf[c % 4], s=x_d[c * 128:(c + 1) * 128, :]:
                       e.dma_start(out=d, in_=s), inc=f"s_x{c % 4}", n=16)

        def stage_B(c):
            if c < 4:
                b.wait("vector", "s_c1", c1_final + 16 * (c + 1))
            else:
                b.wait("vector", f"s_x{c % 4}", 16 * (c // 4))
            b.wait("vector", "sm", sm_xn.get(c - 2))
            b.emit("vector", lambda e, xb=xbuf[c % 4]:
                   e.bn_stats(stats[:, 0:6], xb[:, 0:512]),
                   r=[xbuf[c % 4]], w=[stats])
            b.emit("vector", lambda e, xb=xbuf[c % 4]:
                   e.bn_stats(stats[:, 6:12], xb[:, 512:1024]),
                   r=[xbuf[c % 4]], w=[stats])
            b.emit("vector", lambda e: e.bn_aggr(aggr[:], stats[:]),
                   r=[stats], w=[aggr])
            b.emit("vector", lambda e, d=lnv[c % 2]:
                   e.tensor_scalar_add(d, aggr[:, 1:2], EPS),
                   r=[aggr], w=[lnv[c % 2]])
            vm_ln[c] = b.emit("vector", lambda e, d=negm[c % 2]:
                              e.tensor_scalar_mul(d, aggr[:, 0:1], -1.0),
                              r=[aggr], w=[negm[c % 2]], inc="vm")

        def stage_C(c):
            b.wait("scalar", "vm", vm_ln[c])
            b.emit("scalar", lambda e, s=lnv[c % 2]:
                   e.activation(rstd_s[:], s, AF.Ln),
                   r=[lnv[c % 2]], w=[rstd_s])
            b.emit("scalar", lambda e: e.activation(rstd_s[:], rstd_s[:],
                                                    AF.Exp, scale=-0.5),
                   r=[rstd_s], w=[rstd_s])
            b.emit("scalar", lambda e, s=negm[c % 2]:
                   e.mul(nmr_s[:], s, rstd_s[:]),
                   r=[negm[c % 2], rstd_s], w=[nmr_s])
            sm_xn[c] = b.emit("scalar",
                              lambda e, xb=xbuf[c % 4], d=xn[:, c, :]:
                              e.activation(d, xb, AF.Identity, scale=rstd_s[:],
                                           bias=nmr_s[:]),
                              r=[xbuf[c % 4], rstd_s, nmr_s], w=[xn], inc="sm")

        def stage_D(c):
            b.wait("tensor", "s_c1", c1_final)
            b.wait("tensor", "sm", sm_xn[c])
            b.wait_g("tensor", trg.get(c % 2))
            for j in range(ND):
                fn = (lambda e, o=pT[c % 2][:, j * 128:(j + 1) * 128],
                      i_=xn[:, c, j * 128:(j + 1) * 128]:
                      e.transpose(o, i_, idb))
                if j == ND - 1:
                    tr_mm[c] = b.emit("tensor", fn, inc="tm")
                else:
                    b.emit("tensor", fn)

        def stage_E(c):
            ch, ci = c // TPC, c % TPC
            b.wait("vector", "tm", tr_mm[c])
            if ci == 0 and ch >= 2:
                b.wait("vector", f"s_sp{ch % 2}", 96 * ((ch - 2) // 2 + 1))
                b.wait("vector", "tm", qmm_last.get(ch - 2))
                b.wait("vector", "tm", gmm_d.get((ch - 2) * TPC + TPC - 1))
            m = b.emit("vector",
                       lambda e, d=xnTb[ch % 2][:, :, ci * 128:(ci + 1) * 128],
                       s=pT[c % 2].rearrange("p (j t) -> p j t", j=ND):
                       e.tensor_copy(d, s), w=[xnTb[ch % 2]], inc="vm")
            vm_xnt[c] = m
            trg[c % 2] = ("vm", m)

        def stage_F(c):
            ch, ci = c // TPC, c % TPC
            if c == 0:
                b.wait("tensor", "g_wa", gw_final)
            b.wait("tensor", "vm", vm_xnt[c])
            b.wait_g("tensor", bankg.get("q0"))
            b.wait_g("tensor", bankg.get("gate"))
            for j in range(ND):
                fn = (lambda e, l=xnTb[ch % 2][:, j, ci * 128:(ci + 1) * 128],
                      r_=gw_sb[:, j, :], st=(j == 0), sp=(j == ND - 1):
                      e.matmul(gslice, l, r_, start=st, stop=sp))
                if j == ND - 1:
                    gmm_d[c] = b.emit("tensor", fn, inc="tm")
                else:
                    b.emit("tensor", fn)

        def stage_G(c):
            b.wait("scalar", "tm", gmm_d[c])
            en_ev[c] = b.emit("scalar",
                              lambda e, d=en_all[:, c, :]:
                              e.activation(d, gslice, AF.Exp, scale=-1.0),
                              w=[en_all], inc="sm")
            bankg["gate"] = ("sm", en_ev[c])

        def stage_H(c):
            b.wait("vector", "sm", en_ev[c])
            b.emit("vector", lambda e, s=en_all[:, c, :]:
                   e.tensor_scalar_add(sgt[:], s, 1.0), r=[en_all], w=[sgt])
            b.emit("vector", lambda e: e.reciprocal(sg[:], sgt[:]),
                   r=[sgt], w=[sg])
            b.emit("vector", lambda e: e.tensor_add(bsd[:], sg[:], ebc[:]),
                   r=[sg, ebc], w=[bsd])
            b.emit("vector", lambda e: e.max(top8[:], bsd[:]),
                   r=[bsd], w=[top8])
            b.emit("vector", lambda e, d=mask_all[:, c, :]: e.tensor_scalar(
                d, bsd[:], top8[:, 1:2], None, ALU.is_ge),
                   r=[bsd, top8], w=[mask_all])
            b.emit("vector", lambda e, mk=mask_all[:, c, :]:
                   e.tensor_mul(smv[:], sg[:], mk),
                   r=[sg, mask_all], w=[smv])
            b.emit("vector", lambda e, d=s2tok[:, c, :]:
                   e.tensor_mul(d, smv[:], smv[:]),
                   r=[smv], w=[s2tok], inc="vm")

        def chunk_work(ch):
            if ch == 0:
                b.wait("tensor", "g_wa", gw_final)
            b.wait("tensor", "vm", vm_xnt[ch * TPC + TPC - 1])
            k_ev_last = None
            for j2 in range(ND):
                key = f"k{j2 % 4}"
                sl = kslice[j2 % 4]
                b.wait_g("tensor", bankg.get(key))
                for j in range(ND):
                    fn = (lambda e, o=sl,
                          l=wk_sb[:, j, j2 * 128:(j2 + 1) * 128],
                          r_=xnTb[ch % 2][:, j, :], st=(j == 0), sp=(j == ND - 1):
                          e.matmul(o, l, r_, start=st, stop=sp))
                    if j == ND - 1:
                        m_k = b.emit("tensor", fn, inc="tm")
                    else:
                        b.emit("tensor", fn)
                b.wait("scalar", "tm", m_k)
                if ch >= 2 and j2 == 0:
                    b.wait("scalar", f"s_sp{ch % 2}", 96 * ((ch - 2) // 2 + 1))
                m = b.emit("scalar",
                           lambda e, o=kbuf[ch % 2][:, j2, :], i_=sl,
                           bb=bkc[:, j2:j2 + 1]:
                           e.activation(o, i_, AF.Identity, bias=bb),
                           w=[kbuf[ch % 2]], inc="sm")
                bankg[key] = ("sm", m)
                if j2 == ND - 1:
                    k_ev_last = m
            b.wait("sync", "vm", vm_xnt[ch * TPC + TPC - 1])
            b.emit("sync", lambda e, d=xnT_h[ch * 128:(ch + 1) * 128, :],
                   s=xnTb[ch % 2].rearrange("p j t -> p (j t)"):
                   e.dma_start(out=d, in_=s), inc=f"s_sp{ch % 2}", n=16)
            b.emit("sync", lambda e,
                   d=xn_h[ch * CH:(ch + 1) * CH, :].rearrange(
                       "(c p) d -> p c d", p=128),
                   s=xn[:, ch * TPC:(ch + 1) * TPC, :]:
                   e.dma_start(out=d, in_=s), inc="s_xh", n=16)
            b.wait("sync", "sm", k_ev_last)
            b.emit("sync", lambda e, d=kT_h[ch * 128:(ch + 1) * 128, :],
                   s=kbuf[ch % 2].rearrange("p j t -> p (j t)"):
                   e.dma_start(out=d, in_=s), inc=f"s_sp{ch % 2}", n=16)
            q_ev_last = None
            for j2 in range(ND):
                key = f"q{j2 % 2}"
                sl = qslice[j2 % 2]
                b.wait_g("tensor", bankg.get(key))
                if j2 % 2 == 0:
                    b.wait_g("tensor", bankg.get("gate"))
                for j in range(ND):
                    fn = (lambda e, o=sl,
                          l=wq_sb[:, j, j2 * 128:(j2 + 1) * 128],
                          r_=xnTb[ch % 2][:, j, :], st=(j == 0), sp=(j == ND - 1):
                          e.matmul(o, l, r_, start=st, stop=sp))
                    if j == ND - 1:
                        m_q = b.emit("tensor", fn, inc="tm")
                    else:
                        b.emit("tensor", fn)
                if j2 == ND - 1:
                    qmm_last[ch] = m_q
                b.wait("scalar", "tm", m_q)
                if ch >= 2 and j2 == 0:
                    b.wait("scalar", f"s_sp{ch % 2}", 96 * ((ch - 2) // 2 + 1))
                m = b.emit("scalar",
                           lambda e,
                           o=qbuf[ch % 2][:, :, j2 * 128:(j2 + 1) * 128],
                           i_=sl.rearrange("p (u t) -> p u t", u=TPC),
                           bb=bqc32[:, j2:j2 + 1]:
                           e.activation(o, i_, AF.Identity, scale=SM_SCALE,
                                        bias=bb),
                           w=[qbuf[ch % 2]], inc="sm")
                bankg[key] = ("sm", m)
                q_ev_last = m
            b.wait("sync", "sm", q_ev_last)
            for u2 in range(TPC):
                t_ = ch * TPC + u2
                b.emit("sync", lambda e, d=qT_h[t_ * 128:(t_ + 1) * 128, :],
                       s=qbuf[ch % 2][:, u2, :]:
                       e.dma_start(out=d, in_=s), inc=f"s_sp{ch % 2}", n=16)

        wcg = {0: None, 1: None}
        ph0_state = {}

        def ph0_wcomb():
            b.wait("tensor", "g_wb", wv_final)
            for m in range(ND):
                wpr = [psB, psC][m % 2]
                b.wait_g("tensor", wcg[m % 2])
                for j in range(ND):
                    for hf in range(2):
                        fn = (lambda e, o=wpr[:, hf * 512:(hf + 1) * 512],
                              l=wve_sb[:, j, m * 128:(m + 1) * 128],
                              r_=wpt_sb[:, j, hf * 512:(hf + 1) * 512],
                              st=(j == 0), sp=(j == ND - 1):
                              e.matmul(o, l, r_, start=st, stop=sp))
                        if j == ND - 1 and hf == 1:
                            m_wc = b.emit("tensor", fn, inc="tm")
                        else:
                            b.emit("tensor", fn)
                b.wait("scalar", "tm", m_wc)
                m_wce = b.emit("scalar", lambda e, o=wc_sb[:, m, :], s=wpr:
                               e.activation(o, s, AF.Identity),
                               w=[wc_sb], inc="sm")
                wcg[m % 2] = ("sm", m_wce)
            b.wait_g("tensor", wcg[1])
            for j in range(ND):
                for hf in range(2):
                    fn = (lambda e, o=psC[0:1, hf * 512:(hf + 1) * 512],
                          l=bvc[:, j:j + 1],
                          r_=wpt_sb[:, j, hf * 512:(hf + 1) * 512],
                          st=(j == 0), sp=(j == ND - 1):
                          e.matmul(o, l, r_, start=st, stop=sp))
                    if j == ND - 1 and hf == 1:
                        ph0_state["m_bp"] = b.emit("tensor", fn, inc="tm")
                    else:
                        b.emit("tensor", fn)

        def ph0_finish():
            b.wait("vector", "tm", ph0_state["m_bp"])
            b.wait("vector", "s_c1", c1_final)
            m_bpr = b.emit("vector", lambda e: e.tensor_add(
                bprime[:], psC[0:1, :], bpr[:]),
                r=[bpr], w=[bprime], inc="vm")
            b.wait("sync", "vm", m_bpr)
            b.emit("sync", lambda e: e.dma_start(out=bprime_h, in_=bprime[:]),
                   inc="s_bp", n=16)
            b.wait("sync", "s_bp", 16)
            b.emit("sync", lambda e: e.dma_start(
                out=bpbc[:], in_=bprime_h.broadcast_to([128, D])),
                inc="s_bp", n=16)
            bankg["k2"] = bankg["k3"] = wcg[0]
            bankg["q0"] = bankg["q1"] = bankg["gate"] = ("vm", m_bpr)
        bp_final = 32

        for step in range(NT + 9):
            if step < NT:
                stage_A(step)
                stage_B(step)
            if step == 1:
                ph0_wcomb()
            if step == 4:
                ph0_finish()
            c = step - 1
            if 0 <= c < NT:
                stage_C(c)
            c = step - 2
            if 0 <= c < NT:
                stage_D(c)
                stage_E(c)
            c = step - 3
            if 0 <= c < NT:
                if c % TPC == TPC - 1:
                    chunk_work(c // TPC)
            c = step - 6
            if 0 <= c < NT:
                stage_F(c)
                stage_G(c)
            c = step - 8
            if 0 <= c < NT:
                stage_H(c)

        b.barrier()
        pha.close()

        phb = contextlib.ExitStack()
        _b2 = phb.enter_context
        kT = _b2(nc.sbuf_tensor("kT", [128, ND, S], BF16))[:]

        def precast_weights():
            for g9 in range(9):
                wcast(wupb_h[g9 * 128:(g9 + 1) * 128, :].rearrange(
                          "p (jd h) -> p jd h", jd=ND),
                      wup_d[g9 * D:(g9 + 1) * D, :].rearrange(
                          "(jd p) h -> p jd h", p=128), "g_wc")
                wcast(wdnb_h[g9 * 128:(g9 + 1) * 128, :].rearrange(
                          "p (h d) -> p h d", h=NH),
                      wdn_d[g9 * HID:(g9 + 1) * HID, :].rearrange(
                          "(h p) d -> p h d", p=128), "g_wc")

        for jq in range(ND):
            b.emit("sync" if jq < 4 else "scalar", lambda e, j_=jq: e.dma_start(
                out=kT.rearrange("p j (c t) -> p j c t", t=CH)[:, j_:j_ + 1],
                in_=kT_h.rearrange("(c p) (j t) -> p j c t",
                                   p=128, t=CH)[:, j_:j_ + 1]),
                inc="s_kt", n=16)
        kt_final = b.cnt["s_kt"]

        qT_ld = [_b2(nc.sbuf_tensor(f"qT_ld{i}", [128, ND * 128], BF16))[:]
                 for i in range(2)]
        P_sb = [_b2(nc.sbuf_tensor(f"P_sb{i}", [128, KC], BF16))[:]
                for i in range(2)]
        PT_sb = [_b2(nc.sbuf_tensor(f"PT_sb{i}", [128, 8, 128], BF16))[:]
                 for i in range(2)]
        oacc = _b2(nc.sbuf_tensor("oacc", [128, D], F32))[:]
        outp = _b2(nc.sbuf_tensor("outp", [128, D], BF16))[:]
        outpT = _b2(nc.sbuf_tensor("outpT", [128, ND, 128], BF16))[:]
        atile = [_b2(nc.sbuf_tensor(f"atile{i}", [128, D], BF16))[:]
                 for i in range(2)]
        zac2 = [_b2(nc.sbuf_tensor(f"zacc{i}", [128, 1], F32))[:]
                for i in range(2)]
        zc = _b2(nc.sbuf_tensor("zc", [128, 1], F32))[:]
        nmx = [_b2(nc.sbuf_tensor(f"nmx{i}", [128, 1], F32))[:] for i in range(4)]
        nmc = _b2(nc.sbuf_tensor("nmc", [128, 1], F32))[:]
        dal = [_b2(nc.sbuf_tensor(f"dal{i}", [128, 1], F32))[:] for i in range(2)]
        alp2 = [_b2(nc.sbuf_tensor(f"alp{i}", [128, 1], F32))[:]
                for i in range(4)]
        rz = _b2(nc.sbuf_tensor("rz", [128, 1], F32))[:]


        BIG = 65536.0
        ixs = contextlib.ExitStack()
        _ix = ixs.enter_context
        inclt2 = [_ix(nc.sbuf_tensor(f"inclt{i}", [128, NT], F32))[:]
                  for i in range(2)]
        inclt = inclt2[0]
        cnt_all = _ix(nc.sbuf_tensor("cnt_all", [128, NEXP], F32))[:]
        cntT = _ix(nc.sbuf_tensor("cntT", [NEXP, 128], F32))[:]
        c_incl = _ix(nc.sbuf_tensor("c_incl", [NEXP, 128], F32))[:]
        c_excl = _ix(nc.sbuf_tensor("c_excl", [NEXP, 128], F32))[:]
        tot8 = _ix(nc.sbuf_tensor("tot8", [NEXP, 1], F32))[:]
        ccnt = _ix(nc.sbuf_tensor("ccnt", [NEXP, 1], F32))[:]
        tfr = _ix(nc.sbuf_tensor("tfr", [NEXP, 1], F32))[:]
        cc_row = _ix(nc.sbuf_tensor("cc_row", [1, NEXP], F32))[:]
        ti_row = _ix(nc.sbuf_tensor("ti_row", [1, NEXP], F32))[:]
        S_row = _ix(nc.sbuf_tensor("S_row", [1, NEXP], F32))[:]
        S_col = _ix(nc.sbuf_tensor("S_col", [NEXP, 1], F32))[:]
        rbase = _ix(nc.sbuf_tensor("rbase", [NEXP, 1], F32))[:]
        off_pg = _ix(nc.sbuf_tensor("off_pg", [NEXP, 128], F32))[:]
        off_tok = _ix(nc.sbuf_tensor("off_tok", [128, NEXP], F32))[:]
        posb2 = [_ix(nc.sbuf_tensor(f"posb{i}", [128, NT], F32))[:]
                 for i in range(2)]
        pos22 = [_ix(nc.sbuf_tensor(f"pos2{i}", [128, NT], F32))[:]
                 for i in range(2)]
        posm2 = [_ix(nc.sbuf_tensor(f"posm{i}", [128, NT], I32))[:]
                 for i in range(2)]
        stg2 = [_ix(nc.sbuf_tensor(f"stg{i}", [128, NT, 2], I32))[:]
                for i in range(2)]
        iota_id = _ix(nc.sbuf_tensor("iota_id", [128, NT], I32))[:]
        jj8 = _ix(nc.sbuf_tensor("jj8", [NEXP, NCNK], I32))[:]
        jjf = _ix(nc.sbuf_tensor("jjf", [NEXP, NCNK], F32))[:]
        geb = _ix(nc.sbuf_tensor("geb", [NEXP, NCNK], F32))[:]
        ones8 = _ix(nc.sbuf_tensor("ones8", [NEXP, 1], F32))[:]
        expf = _ix(nc.sbuf_tensor("expf", [1, NCNK], F32))[:]
        expbc = _ix(nc.sbuf_tensor("expbc", [128, NCNK], F32))[:]
        iotap_i = _ix(nc.sbuf_tensor("iotap_i", [128, NCNK], I32))[:]
        iotap_f = _ix(nc.sbuf_tensor("iotap_f", [128, NCNK], F32))[:]
        idxgw_f = _ix(nc.sbuf_tensor("idxgw_f", [128, NCNK], F32))[:]
        init_sb = _ix(nc.sbuf_tensor("init_sb", [128, CAPR * 2 // 128], I32))[:]

        m_ones = b.emit("vector", lambda e: e.memset(ones8, 1.0),
                        w=[ones8], inc="vm")
        m_init = b.emit("vector", lambda e: e.memset(init_sb, S),
                        w=[init_sb], inc="vm")
        b.wait("gpsimd", "vm", m_init)
        b.emit("gpsimd", lambda e: e.iota(iota_id, pattern=[[128, NT]], base=0,
                                          channel_multiplier=1))
        b.emit("gpsimd", lambda e: e.iota(jj8, pattern=[[1, NCNK]], base=0,
                                          channel_multiplier=0))
        m_iot = b.emit("gpsimd", lambda e: e.iota(
            iotap_i, pattern=[[0, NCNK]], base=0, channel_multiplier=1),
            inc="gm")
        b.emit("gpsimd", lambda e: e.dma_start(out=idx_h, in_=init_sb),
               inc="g_ix", n=16)
        ix_init = b.cnt["g_ix"]
        b.wait("vector", "gm", m_iot)
        b.emit("vector", lambda e: e.tensor_copy(jjf, jj8), r=[jj8], w=[jjf])
        b.emit("vector", lambda e: e.tensor_copy(iotap_f, iotap_i),
               r=[iotap_i], w=[iotap_f])
        for g in range(NEXP):
            b.emit("vector", lambda e, mk=mask_all[:, :, g]:
                   e.tensor_tensor_scan(inclt, mk, mk, 0.0, ALU.add, ALU.bypass),
                   r=[mask_all], w=[inclt])
            b.emit("vector", lambda e, d=cnt_all[:, g:g + 1]:
                   e.tensor_copy(d, inclt[:, NT - 1:NT]),
                   r=[inclt], w=[cnt_all])
        m1 = b.emit("vector", lambda e: e.engine_nop(), inc="vm")
        b.wait("gpsimd", "vm", m1)
        b.emit("gpsimd", lambda e: e.dma_start(
            out=scr_h[0:NEXP, 0:128].rearrange("g p -> p g"), in_=cnt_all,
            allow_slow_non_contiguous=True),
            inc="g_ix", n=16)
        b.wait("gpsimd", "g_ix", b.cnt["g_ix"])
        b.emit("gpsimd", lambda e: e.dma_start(out=cntT, in_=scr_h[0:NEXP, 0:128]),
               inc="g_ix", n=16)
        b.wait("vector", "g_ix", b.cnt["g_ix"])
        b.emit("vector", lambda e: e.tensor_tensor_scan(
            c_incl, cntT, cntT, 0.0, ALU.add, ALU.bypass), r=[cntT], w=[c_incl])
        b.emit("vector", lambda e: e.tensor_sub(c_excl, c_incl, cntT),
               r=[c_incl, cntT], w=[c_excl])
        b.emit("vector", lambda e: e.tensor_copy(tot8, c_incl[:, 127:128]),
               r=[c_incl], w=[tot8])
        b.emit("vector", lambda e: e.tensor_scalar_add(ccnt, tot8, 511.0),
               r=[tot8], w=[ccnt])
        b.emit("vector", lambda e: e.tensor_scalar_mul(ccnt, ccnt, 1.0 / 512.0),
               r=[ccnt], w=[ccnt])
        b.emit("vector", lambda e: e.tensor_scalar_add(tfr, ccnt, 8388608.0),
               r=[ccnt], w=[tfr])
        b.emit("vector", lambda e: e.tensor_scalar_add(tfr, tfr, -8388608.0),
               r=[tfr], w=[tfr])
        b.emit("vector", lambda e: e.tensor_tensor(tot8, tfr, ccnt, ALU.is_gt),
               r=[tfr, ccnt], w=[tot8])
        m2 = b.emit("vector", lambda e: e.tensor_sub(ccnt, tfr, tot8),
                    r=[tfr, tot8], w=[ccnt], inc="vm")
        b.wait("gpsimd", "vm", m2)
        b.emit("gpsimd", lambda e: e.dma_start(
            out=scr_h[8:9, 0:NEXP].rearrange("o g -> g o"), in_=ccnt,
            allow_slow_non_contiguous=True),
            inc="g_ix", n=16)
        b.wait("gpsimd", "g_ix", b.cnt["g_ix"])
        b.emit("gpsimd", lambda e: e.dma_start(out=cc_row, in_=scr_h[8:9, 0:NEXP]),
               inc="g_ix", n=16)
        b.wait("vector", "g_ix", b.cnt["g_ix"])
        b.emit("vector", lambda e: e.tensor_tensor_scan(
            ti_row, cc_row, cc_row, 0.0, ALU.add, ALU.bypass),
            r=[cc_row], w=[ti_row])
        m3 = b.emit("vector", lambda e: e.tensor_sub(S_row, ti_row, cc_row),
                    r=[ti_row, cc_row], w=[S_row], inc="vm")
        b.wait("gpsimd", "vm", m3)
        b.emit("gpsimd", lambda e: e.dma_start(
            out=scr_h[9:10, 0:NEXP], in_=S_row), inc="g_ix", n=16)
        b.wait("gpsimd", "g_ix", b.cnt["g_ix"])
        b.emit("gpsimd", lambda e: e.dma_start(
            out=S_col, in_=scr_h[9:10, 0:NEXP].rearrange("o g -> g o"),
            allow_slow_non_contiguous=True),
            inc="g_ix", n=16)
        b.wait("vector", "g_ix", b.cnt["g_ix"])
        b.emit("vector", lambda e: e.tensor_scalar_mul(rbase, S_col, 512.0),
               r=[S_col], w=[rbase])
        m4 = b.emit("vector", lambda e: e.tensor_scalar(
            off_pg, c_excl, rbase, None, ALU.add),
            r=[c_excl, rbase], w=[off_pg], inc="vm")
        b.wait("gpsimd", "vm", m4)
        b.emit("gpsimd", lambda e: e.dma_start(
            out=scr_h[16:16 + NEXP, 0:128], in_=off_pg), inc="g_ix", n=16)
        b.wait("gpsimd", "g_ix", b.cnt["g_ix"])
        b.emit("gpsimd", lambda e: e.dma_start(
            out=off_tok, in_=scr_h[16:16 + NEXP, 0:128].rearrange("g p -> p g"),
            allow_slow_non_contiguous=True),
            inc="g_ix", n=16)
        ix_pre = b.cnt["g_ix"]

        def ix_expert(g):
            gp = g % 2
            inc_, pb, p2_, pm, sg2 = (inclt2[gp], posb2[gp], pos22[gp],
                                      posm2[gp], stg2[gp])
            if g == 0:
                b.wait("vector", "g_ix", ix_pre)
            b.emit("vector", lambda e, mk=mask_all[:, :, g]:
                   e.tensor_tensor_scan(inc_, mk, mk, 0.0, ALU.add, ALU.bypass),
                   r=[mask_all], w=[inc_])
            b.emit("vector", lambda e, mk=mask_all[:, :, g]:
                   e.scalar_tensor_tensor(pb, mk, -1.0, inc_, ALU.mult, ALU.add),
                   r=[mask_all, inc_], w=[pb])
            b.emit("vector", lambda e, oc=off_tok[:, g:g + 1]:
                   e.tensor_scalar(pb, pb, oc, None, ALU.add),
                   r=[pb, off_tok], w=[pb])
            b.emit("vector", lambda e: e.tensor_scalar_add(p2_, pb, BIG),
                   r=[pb], w=[p2_])
            b.emit("vector", lambda e, mk=mask_all[:, :, g]:
                   e.scalar_tensor_tensor(pb, mk, -BIG, p2_, ALU.mult, ALU.add),
                   r=[mask_all, p2_], w=[pb])
            b.emit("vector", lambda e: e.tensor_copy(pm, pb),
                   r=[pb], w=[pm])
            b.emit("vector", lambda e: e.tensor_copy(
                sg2[:, :, 0:1].rearrange("p t o -> p (t o)"), iota_id),
                r=[iota_id], w=[sg2])
            mg = b.emit("vector", lambda e, s2=s2tok[:, :, g:g + 1].bitcast(I32):
                        e.tensor_copy(sg2[:, :, 1:2].rearrange("p t o -> p (t o)"),
                                      s2.rearrange("p t o -> p (t o)")),
                        r=[s2tok], w=[sg2], inc="vm")
            b.wait("gpsimd", "vm", mg)
            b.wait("gpsimd", "g_ix", b.cnt["g_ix"])
            for c in range(NT):
                b.emit("gpsimd", lambda e, po=pm[:, c:c + 1],
                       sg_=sg2[:, c, :]:
                       e.indirect_dma_start(
                           out=idx_h, out_offset=IndirectOffsetOnAxis(po, 0),
                           in_=sg_, in_offset=None,
                           bounds_check=regs["capr"], oob_is_err=False),
                       inc="g_ix", n=16)
            return b.cnt["g_ix"]

        ix_scat = {}

        pairg = {0: None, 1: None}
        avg_g = {"c": None}
        ptg = {0: None, 1: None}
        sm_exp = {}
        sm_pt = {}
        m_s_d = {}
        m_av_d = {}

        def p2_scores_mm(t, u):
            seq = t * NKC + u
            pr = pairs[seq % 2]
            b.wait("tensor", f"s_qt{t % 2}", 16 * (t // 2 + 1))
            if t == 0 and u == 0:
                b.wait("tensor", "s_kt", kt_final)
            b.wait_g("tensor", pairg[seq % 2])
            for j in range(ND):
                for hf in range(2):
                    fn = (lambda e, o=pr[:, hf * 512:(hf + 1) * 512],
                          l=qT_ld[t % 2][:, j * 128:(j + 1) * 128],
                          r_=kT[:, j, u * KC + hf * 512:u * KC + (hf + 1) * 512],
                          st=(j == 0), sp=(j == ND - 1):
                          e.matmul(o, l, r_, start=st, stop=sp))
                    if j == ND - 1 and hf == 1:
                        m_s = b.emit("tensor", fn, inc="tm")
                    else:
                        b.emit("tensor", fn)
            if u == NKC - 1:
                tm_smm[t] = m_s
            m_s_d[(t, u)] = m_s

        def p2_scores_max(t, u):
            seq = t * NKC + u
            pr = pairs[seq % 2]
            b.wait("vector", "tm", m_s_d[(t, u)])
            b.emit("vector", lambda e, s=pr: e.reduce_max(
                nmc[:], s, axis=AX.X, negate=True), w=[nmc])
            if u == 0:
                m_mx = b.emit("vector", lambda e, d=nmx[seq % 4]:
                              e.tensor_copy(d, nmc[:]), r=[nmc],
                              w=[nmx[seq % 4]], inc="vm")
            else:
                b.emit("vector", lambda e, d=nmx[seq % 4], a=nmx[(seq - 1) % 4]:
                       e.tensor_tensor(d, a, nmc[:], ALU.min),
                       r=[nmx[(seq - 1) % 4], nmc], w=[nmx[seq % 4]])
                m_mx = b.emit("vector", lambda e, a=nmx[(seq - 1) % 4],
                              n_=nmx[seq % 4], dd=dal[seq % 2]:
                              e.tensor_sub(dd, n_, a),
                              r=[nmx[seq % 4], nmx[(seq - 1) % 4]],
                              w=[dal[seq % 2]], inc="vm")
            return m_mx

        def p2_scores_exp(t, u, m_mx):
            seq = t * NKC + u
            pr = pairs[seq % 2]
            al = alp2[seq % 4]
            b.wait("scalar", "vm", m_mx)
            if u > 0:
                b.emit("scalar", lambda e, a=al, dd=dal[seq % 2]:
                       e.activation(a, dd, AF.Exp), r=[dal[seq % 2]], w=[al])
            sm_exp[(t, u)] = b.emit(
                "scalar", lambda e, o=P_sb[seq % 2], s=pr, n_=nmx[seq % 4]:
                e.activation(o, s, AF.Exp, bias=n_, accum_out=zc[:]),
                r=[nmx[seq % 4]], w=[P_sb[seq % 2], zc], inc="sm")
            pairg[seq % 2] = ("sm", sm_exp[(t, u)])

        def p2_scores_z(t, u):
            seq = t * NKC + u
            al = alp2[seq % 4]
            za = zac2[t % 2]
            b.wait("vector", "sm", sm_exp[(t, u)])
            if u == 0:
                b.emit("vector", lambda e, z=za: e.tensor_copy(z, zc[:]),
                       r=[zc], w=[za])
            else:
                b.emit("vector", lambda e, z=za, a=al: e.scalar_tensor_tensor(
                    z, z, a, zc[:], ALU.mult, ALU.add),
                    r=[za, al, zc], w=[za])

        def p2_pt_tr(t, u):
            seq = t * NKC + u
            b.wait("tensor", "sm", sm_exp[(t, u)])
            b.wait_g("tensor", ptg[seq % 2])
            for i in range(8):
                fn = (lambda e, o=pT[seq % 2][:, i * 128:(i + 1) * 128],
                      i_=P_sb[seq % 2][:, i * 128:(i + 1) * 128]:
                      e.transpose(o, i_, idb))
                if i == 7:
                    m_pt = b.emit("tensor", fn, inc="tm")
                else:
                    b.emit("tensor", fn)
            return m_pt

        def p2_pt_evac(t, u, m_pt):
            seq = t * NKC + u
            b.wait("scalar", "tm", m_pt)
            sm_pt[(t, u)] = b.emit(
                "scalar", lambda e, d=PT_sb[seq % 2],
                s=pT[seq % 2].rearrange("p (i t) -> p i t", i=8):
                e.activation(d, s, AF.Identity), w=[PT_sb[seq % 2]], inc="sm")
            ptg[seq % 2] = ("sm", sm_pt[(t, u)])

        def p2_av(t, u):
            seq = t * NKC + u
            b.wait("tensor", "sm", sm_pt[(t, u)])
            b.wait_g("tensor", avg_g["c"])
            for i in range(8):
                for hf in range(2):
                    fn = (lambda e, o=psC[:, hf * 512:(hf + 1) * 512],
                          l=PT_sb[seq % 2][:, i, :],
                          r_=xn[:, u * 8 + i, hf * 512:(hf + 1) * 512],
                          st=(i == 0), sp=(i == 7):
                          e.matmul(o, l, r_, start=st, stop=sp))
                    if i == 7 and hf == 1:
                        m_av = b.emit("tensor", fn, inc="tm")
                    else:
                        b.emit("tensor", fn)
            m_av_d[(t, u)] = m_av

        def p2_oacc(t, u):
            seq = t * NKC + u
            al = alp2[seq % 4]
            b.wait("vector", "tm", m_av_d[(t, u)])
            if u == 0:
                m_oe = b.emit("vector", lambda e: e.tensor_copy(oacc[:], psC[:]),
                              w=[oacc], inc="vm")
            else:
                m_oe = b.emit("vector", lambda e, a=al: e.scalar_tensor_tensor(
                    oacc[:], oacc[:], a, psC[:], ALU.mult, ALU.add),
                    r=[oacc, al], w=[oacc], inc="vm")
            avg_g["c"] = ("vm", m_oe)

        def qt_load(t):
            if t >= NT:
                return
            b.wait("sync", "tm", tm_smm.get(t - 2))
            b.emit("sync", lambda e, d=qT_ld[t % 2],
                   s=qT_h[t * 128:(t + 1) * 128, :]:
                   e.dma_start(out=d, in_=s), inc=f"s_qt{t % 2}", n=16)

        b.wait("vector", "vm", m_ones)
        mge = b.emit("vector", lambda e: e.tensor_scalar(
            geb, jjf, S_col, None, ALU.is_ge), r=[jjf, S_col], w=[geb], inc="vm")
        b.wait("tensor", "vm", mge)
        b.wait_g("tensor", avg_g["c"])
        m_ec = b.emit("tensor", lambda e: e.matmul(
            psC[0:1, 0:NCNK], ones8, geb, start=True, stop=True), inc="tm")
        b.wait("vector", "tm", m_ec)
        m_ec2 = b.emit("vector", lambda e: e.tensor_scalar_add(
            expf, psC[0:1, 0:NCNK], -1.0), w=[expf], inc="vm")
        avg_g["c"] = ("vm", m_ec2)
        b.wait("gpsimd", "vm", m_ec2)
        b.emit("gpsimd", lambda e: e.dma_start(
            out=scr_h[10:11, 0:NCNK], in_=expf), inc="g_ib", n=16)
        b.wait("gpsimd", "g_ib", b.cnt["g_ib"])
        b.emit("gpsimd", lambda e: e.dma_start(
            out=expbc, in_=scr_h[10:11, 0:NCNK].broadcast_to([128, NCNK])),
            inc="g_ib", n=16)
        b.wait("vector", "g_ib", b.cnt["g_ib"])
        b.emit("vector", lambda e: e.scalar_tensor_tensor(
            idxgw_f, expbc, 128.0, iotap_f, ALU.mult, ALU.add),
            r=[expbc, iotap_f], w=[idxgw_f])
        m_wix = b.emit("vector", lambda e: e.tensor_copy(idxgw_i, idxgw_f),
                       r=[idxgw_f], w=[idxgw_i], inc="vm")


        qt_load(0)
        qt_load(1)
        for bseq in range(2):
            bt, bu = divmod(bseq, NKC)
            p2_scores_mm(bt, bu)
            m_mxb = p2_scores_max(bt, bu)
            p2_scores_exp(bt, bu, m_mxb)
            p2_scores_z(bt, bu)
        for t in range(NT):
            if t < NEXP:
                if t >= 2:
                    b.wait("vector", "g_ix", ix_scat[t - 2])
                ix_scat[t] = ix_expert(t)
                if t == NEXP - 1:
                    precast_weights()
            for u in range(NKC):
                m_pt = p2_pt_tr(t, u)
                p2_pt_evac(t, u, m_pt)
                nseq = t * NKC + u + 2
                nxt = (nseq // NKC, nseq % NKC) if nseq < NT * NKC else None
                if nxt:
                    p2_scores_mm(*nxt)
                    m_mx = p2_scores_max(*nxt)
                p2_av(t, u)
                if nxt:
                    p2_scores_exp(*nxt, m_mx)
                    p2_scores_z(*nxt)
                p2_oacc(t, u)
            qt_load(t + 2)
            b.emit("vector", lambda e, z=zac2[t % 2]: e.reciprocal(rz[:], z),
                   r=[zac2[t % 2]], w=[rz])
            m_op = b.emit("vector", lambda e: e.tensor_scalar(
                outp[:], oacc[:], rz[:], None, ALU.mult),
                r=[oacc, rz], w=[outp], inc="vm")
            seqlast = t * NKC + NKC - 1
            b.wait("tensor", "vm", m_op)
            b.wait_g("tensor", ptg[seqlast % 2])
            for j in range(ND):
                fn = (lambda e, o=pT[seqlast % 2][:, j * 128:(j + 1) * 128],
                      i_=outp[:, j * 128:(j + 1) * 128]: e.transpose(o, i_, idb))
                if j == ND - 1:
                    m_t1 = b.emit("tensor", fn, inc="tm")
                else:
                    b.emit("tensor", fn)
            b.wait("scalar", "tm", m_t1)
            m_ev1 = b.emit("scalar", lambda e, s=pT[seqlast % 2].rearrange(
                "p (j t) -> p j t", j=ND):
                e.activation(outpT[:], s, AF.Identity), w=[outpT], inc="sm")
            ptg[seqlast % 2] = ("sm", m_ev1)
            b.wait("tensor", "sm", m_ev1)
            b.wait_g("tensor", avg_g["c"])
            for j in range(ND):
                for hf in range(2):
                    fn = (lambda e, o=psC[:, hf * 512:(hf + 1) * 512],
                          l=outpT[:, j, :],
                          r_=wc_sb[:, j, hf * 512:(hf + 1) * 512],
                          st=(j == 0), sp=(j == ND - 1):
                          e.matmul(o, l, r_, start=st, stop=sp))
                    if j == ND - 1 and hf == 1:
                        m_pj = b.emit("tensor", fn, inc="tm")
                    else:
                        b.emit("tensor", fn)
            b.wait("vector", "tm", m_pj)
            b.wait("vector", "s_bp", bp_final)
            if t >= 2:
                b.wait("vector", f"s_ast{t % 2}", 16 * (t // 2))
            m_at = b.emit("vector", lambda e, d=atile[t % 2]:
                          e.tensor_add(d, psC[:], bpbc[:]),
                          r=[bpbc], w=[atile[t % 2]], inc="vm")
            avg_g["c"] = ("vm", m_at)
            b.wait("sync", "vm", m_at)
            b.emit("sync", lambda e, d=attn_h[t * 128:(t + 1) * 128, :],
                   s=atile[t % 2]: e.dma_start(out=d, in_=s),
                   inc=f"s_ast{t % 2}", n=16)

        b.barrier()
        ixs.close()
        phb.close()
        xn_stack.close()

        phc = contextlib.ExitStack()
        _c = phc.enter_context
        wupg = [_c(nc.sbuf_tensor(f"wupg{i}", [128, ND, HID], BF16))[:]
                for i in range(2)]
        wdng = [_c(nc.sbuf_tensor(f"wdng{i}", [128, NH, D], BF16))[:]
                for i in range(2)]
        xnTm_u = _c(nc.sbuf_tensor("xnTm_u", [128, 2 * ND * CH], BF16))[:]
        xnTm = [xnTm_u[:, i * ND * CH:(i + 1) * ND * CH].rearrange(
            "p (j t) -> p j t", j=ND) for i in range(2)]
        h_sb = _c(nc.sbuf_tensor("h_sb", [128, NH, CH], BF16))[:]
        rbf = [_c(nc.sbuf_tensor(f"rbf{i}", [128, CH], BF16))[:] for i in range(2)]
        osh = _c(nc.sbuf_tensor("osh", [128, D], F32))[:]
        gbuf = [_c(nc.sbuf_tensor(f"gbuf{i}", [128, TPC, D], BF16))[:]
                for i in range(2)]
        gxT = _c(nc.sbuf_tensor("gxT", [128, ND, CH], BF16))[:]
        obuf = xnTm_u.bitcast(F32).rearrange("p (u d) -> p u d", u=TPC)
        idxg_sb = _c(nc.sbuf_tensor("idxg_sb", [128, NGT, 2], I32))[:]

        upslice = [psA[:, 0:512], psA[:, 512:1024]]
        dnpair = [psB, psC]

        b.emit("scalar", lambda e: e.dma_start(
            out=wupg[0], in_=wupb_h[8 * 128:9 * 128, :].rearrange(
                "p (jd h) -> p jd h", jd=ND)), inc="s_ig", n=16)
        b.emit("gpsimd", lambda e: e.dma_start(
            out=wdng[0], in_=wdnb_h[8 * 128:9 * 128, :].rearrange(
                "p (h d) -> p h d", h=NH)), inc="g_wb", n=16)
        b.emit("scalar", lambda e: e.dma_start(
            out=idxg_sb, in_=idx_h.rearrange("(i p) v -> p i v", p=128)),
            inc="s_mf1", n=16)
        ig_final = b.cnt["s_ig"]
        ig2_final = b.cnt["g_wb"]
        mf1_final = b.cnt["s_mf1"]

        tm_up_done = {}
        tm_dn_done = {}
        tm_tr_done = {}
        vm_hsq = {}
        vm_ev = {}
        up_bank_free = {}
        dn_bank_free = {}
        sh_store = 0

        def moe_up(k, wsrc, rhs_buf, width, gwait=None):
            if gwait:
                b.wait("tensor", gwait[0], gwait[1])
            for h in range(NH):
                bank = h % 2
                b.wait_g("tensor", up_bank_free.get(bank))
                for j in range(ND):
                    fn = (lambda e, o=upslice[bank][:, 0:width],
                          l=wsrc[:, j, h * 128:(h + 1) * 128],
                          r_=rhs_buf[:, j, 0:width], st=(j == 0),
                          sp=(j == ND - 1):
                          e.matmul(o, l, r_, start=st, stop=sp))
                    if j == ND - 1:
                        m_up = b.emit("tensor", fn, inc="tm")
                    else:
                        b.emit("tensor", fn)
                if h == NH - 1:
                    tm_up_done[k] = m_up
                b.wait("scalar", "tm", m_up)
                prev_rbf = (k, h - 2) if h >= 2 else (k - 1, NH - 2 + h)
                b.wait("scalar", "vm", vm_hsq.get(prev_rbf))
                m_rl = b.emit(
                    "scalar", lambda e, o=rbf[h % 2][:, 0:width],
                    s=upslice[bank][:, 0:width]:
                    e.activation(o, s, AF.Relu), w=[rbf[h % 2]], inc="sm")
                up_bank_free[bank] = ("sm", m_rl)
                b.wait("vector", "sm", m_rl)
                b.wait("vector", "tm", tm_dn_done.get((k - 1, TPC - 1)))
                vm_hsq[(k, h)] = b.emit(
                    "vector", lambda e, d=h_sb[:, h, 0:width],
                    s=rbf[h % 2][:, 0:width]:
                    e.tensor_mul(d, s, s), r=[rbf[h % 2]], w=[h_sb], inc="vm")

        def moe_down_mm(k, u2, wsrc):
            pr = dnpair[u2 % 2]
            b.wait("tensor", "vm", vm_ev.get(dn_bank_free.get(u2 % 2)))
            if u2 == 0:
                b.wait("tensor", "vm", vm_hsq[(k, NH - 1)])
            for h in range(NH):
                for hf in range(2):
                    fn = (lambda e, o=pr[:, hf * 512:(hf + 1) * 512],
                          l=h_sb[:, h, u2 * 128:(u2 + 1) * 128],
                          r_=wsrc[:, h, hf * 512:(hf + 1) * 512],
                          st=(h == 0), sp=(h == NH - 1):
                          e.matmul(o, l, r_, start=st, stop=sp))
                    if h == NH - 1 and hf == 1:
                        m_dn = b.emit("tensor", fn, inc="tm")
                    else:
                        b.emit("tensor", fn)
            tm_dn_done[(k, u2)] = m_dn
            dn_bank_free[u2 % 2] = (k, u2)
            return m_dn, pr

        KOFF = NCH

        def routed_gathers(j):
            if j >= NCNK:
                return
            k = KOFF + j
            jp = (j + 1) % 2
            if j >= 2:
                b.wait("gpsimd", "tm", tm_up_done.get(k - 2))
            elif j == 1:
                b.wait("gpsimd", "tm", tm_up_done.get(KOFF - 1))
            else:
                b.wait("gpsimd", "tm", tm_up_done.get(2))
            b.emit("gpsimd", lambda e,
                   o=wupg[jp].rearrange("p jd h -> p (jd h)"),
                   ix=idxgw_i[:, j:j + 1]:
                   e.indirect_dma_start(
                       out=o, out_offset=None, in_=wupb_h,
                       in_offset=IndirectOffsetOnAxis(ix, 0)),
                   inc=f"g_wg{jp}", n=16)
            if j >= 2:
                b.wait("gpsimd", "tm", tm_dn_done.get((k - 2, TPC - 1)))
            elif j == 1:
                b.wait("gpsimd", "tm", tm_dn_done.get((KOFF - 1, TPC - 1)))
            b.emit("gpsimd", lambda e,
                   o=wdng[jp].rearrange("p h d -> p (h d)"),
                   ix=idxgw_i[:, j:j + 1]:
                   e.indirect_dma_start(
                       out=o, out_offset=None, in_=wdnb_h,
                       in_offset=IndirectOffsetOnAxis(ix, 0)),
                   inc=f"g_wg{jp}", n=16)
            if j <= 1:
                b.wait("gpsimd", "s_mf1", mf1_final)
            b.wait("gpsimd", "tm", tm_tr_done.get(k - 2))
            for i in range(TPC):
                b.emit("gpsimd", lambda e, o=gbuf[jp][:, i, :],
                       ix=idxg_sb[:, j * TPC + i, 0:1]:
                       e.indirect_dma_start(
                           out=o, out_offset=None, in_=xn_h,
                           in_offset=IndirectOffsetOnAxis(ix, 0),
                           bounds_check=regs["sm1"], oob_is_err=False),
                       inc=f"g_tg{jp}", n=16)

        def xnm_load(ch):
            if ch >= NCH:
                return
            b.wait("sync", "tm", tm_up_done.get(ch - 2))
            b.emit("sync", lambda e, d=xnTm[ch % 2],
                   s=xnT_h[ch * 128:(ch + 1) * 128, :].rearrange(
                       "p (j t) -> p j t", j=ND):
                   e.dma_start(out=d, in_=s), inc=f"s_xnt{ch % 2}", n=16)

        xnm_load(0)
        for ch in range(NCH):
            k = ch
            xnm_load(ch + 1)
            if ch == NCH - 2:
                routed_gathers(0)
            b.wait("tensor", f"s_xnt{ch % 2}", 16 * (ch // 2 + 1))
            if ch == 0:
                b.wait("tensor", "s_ig", ig_final)
                b.wait("tensor", "g_wb", ig2_final)
            moe_up(k, wupg[0], xnTm[ch % 2], CH)
            for u2 in range(TPC):
                tt = ch * TPC + u2
                m_dn, pr = moe_down_mm(k, u2, wdng[0])
                b.wait("vector", "tm", m_dn)
                if sh_store >= 1:
                    b.wait("vector", "s_sh", 16 * sh_store)
                vm_ev[(k, u2)] = b.emit(
                    "vector", lambda e, s=pr: e.tensor_copy(osh[:], s),
                    w=[osh], inc="vm")
                b.wait("sync", "vm", vm_ev[(k, u2)])
                b.emit("sync", lambda e, d=moe_h[tt * 128:(tt + 1) * 128, :]:
                       e.dma_start(out=d, in_=osh[:]), inc="s_sh", n=16)
                sh_store += 1
        sh_final = 16 * sh_store

        routed_gathers(1)
        n_sca = 0
        for j in range(NCNK):
            k = KOFF + j
            jp = (j + 1) % 2
            b.wait("tensor", f"g_wg{jp}", 16 * 2 * (j // 2 + 1))
            b.wait("tensor", f"g_tg{jp}", 16 * TPC * (j // 2 + 1))
            for i in range(TPC):
                b.wait_g("tensor", ptg[i % 2])
                for jd in range(ND):
                    fn = (lambda e, o=pT[i % 2][:, jd * 128:(jd + 1) * 128],
                          i_=gbuf[jp][:, i, jd * 128:(jd + 1) * 128]:
                          e.transpose(o, i_, idb))
                    if jd == ND - 1:
                        m_tr = b.emit("tensor", fn, inc="tm")
                    else:
                        b.emit("tensor", fn)
                b.wait("scalar", "tm", m_tr)
                m_te = b.emit("scalar", lambda e,
                              d=gxT[:, :, i * 128:(i + 1) * 128],
                              s=pT[i % 2].rearrange("p (jd t) -> p jd t", jd=ND):
                              e.activation(d, s, AF.Identity), w=[gxT], inc="sm")
                ptg[i % 2] = ("sm", m_te)
                if i == TPC - 1:
                    tm_tr_done[k] = m_tr
            moe_up(k, wupg[jp], gxT, CH, gwait=("sm", m_te))
            for u2 in range(TPC):
                gt = j * TPC + u2
                m_dn, pr = moe_down_mm(k, u2, wdng[jp])
                b.wait("vector", "tm", m_dn)
                if j >= 1:
                    pk = (j - 1) * TPC + u2
                    b.wait("vector", f"g_sa{pk % 2}", 16 * (pk // 2 + 1))
                if j == 0 and u2 == 0:
                    b.wait("vector", "s_mf1", mf1_final)
                vm_ev[(k, u2)] = b.emit(
                    "vector", lambda e, d=obuf[:, u2, :], s=pr,
                    s2=idxg_sb[:, gt, 1:2].bitcast(F32):
                    e.tensor_scalar(d, s, s2, None, ALU.mult),
                    r=[idxg_sb], w=[obuf], inc="vm")
                b.wait("gpsimd", "vm", vm_ev[(k, u2)])
                if n_sca == 0:
                    b.wait("gpsimd", "s_sh", sh_final)
                else:
                    p_ = (n_sca - 1) % 2
                    b.wait("gpsimd", f"g_sa{p_}", 16 * ((n_sca - 1) // 2 + 1))
                b.emit("gpsimd", lambda e, ix=idxg_sb[:, gt, 0:1],
                       s=obuf[:, u2, :]:
                       e.indirect_dma_start(
                           out=moe_h, out_offset=IndirectOffsetOnAxis(ix, 0),
                           in_=s, in_offset=None,
                           bounds_check=regs["sm1"], oob_is_err=False,
                           compute_op=ALU.add),
                       inc=f"g_sa{n_sca % 2}", n=16)
                n_sca += 1
            routed_gathers(j + 2)
        sa0_final = b.cnt["g_sa0"]
        sa1_final = b.cnt["g_sa1"]

        asmst = contextlib.ExitStack()
        _as = asmst.enter_context
        xfin = [_as(nc.sbuf_tensor(f"xfin{i}", [128, D], F32))[:] for i in range(2)]
        afin = [_as(nc.sbuf_tensor(f"afin{i}", [128, D], BF16))[:] for i in range(2)]
        mfin = [_as(nc.sbuf_tensor(f"mfin{i}", [128, D], F32))[:] for i in range(2)]
        t1f = _as(nc.sbuf_tensor("t1f", [128, D], F32))[:]
        otile = [_as(nc.sbuf_tensor("otile0", [128, D], F32))[:], osh]
        b.wait("vector", "s_sh", sh_final)
        b.wait("sync", "g_sa0", sa0_final)
        b.wait("sync", "g_sa1", sa1_final)
        for tt in range(NT):
            b.wait("sync", "vm", vm_fin.get(tt - 2))
            b.emit("sync", lambda e, d=xfin[tt % 2],
                   s=x_d[tt * 128:(tt + 1) * 128, :]:
                   e.dma_start(out=d, in_=s), inc=f"s_fin{tt % 2}", n=16)
            b.emit("sync", lambda e, d=afin[tt % 2],
                   s=attn_h[tt * 128:(tt + 1) * 128, :]:
                   e.dma_start(out=d, in_=s), inc=f"s_fin{tt % 2}", n=16)
            b.emit("sync", lambda e, d=mfin[tt % 2],
                   s=moe_h[tt * 128:(tt + 1) * 128, :]:
                   e.dma_start(out=d, in_=s), inc=f"s_fin{tt % 2}", n=16)
            b.wait("vector", f"s_fin{tt % 2}", 48 * (tt // 2 + 1))
            if tt >= 2:
                b.wait("vector", f"s_out{tt % 2}", 16 * (tt // 2))
            b.emit("vector", lambda e, xf=xfin[tt % 2], af=afin[tt % 2]:
                   e.tensor_add(t1f[:], xf, af),
                   r=[xfin[tt % 2], afin[tt % 2]], w=[t1f])
            vm_fin[tt] = b.emit("vector", lambda e, d=otile[tt % 2],
                                m=mfin[tt % 2]: e.tensor_add(d, t1f[:], m),
                                r=[t1f, mfin[tt % 2]], w=[otile[tt % 2]],
                                inc="vm")
            b.wait("sync", "vm", vm_fin[tt])
            b.emit("sync", lambda e, d=out_d[tt * 128:(tt + 1) * 128, :],
                   s=otile[tt % 2]: e.dma_start(out=d, in_=s),
                   inc=f"s_out{tt % 2}", n=16)
        asmst.close()

        finals = dict(b.cnt)
        for s, cfin in finals.items():
            b.wait("sync", s, cfin)
        phc.close()

        with nc.Block() as blk:
            @blk.sync
            def _(e):
                for f in b.th["sync"]:
                    f(e)

            @blk.gpsimd
            def _(e):
                for f in b.th["gpsimd"]:
                    f(e)

            @blk.vector
            def _(e):
                for f in b.th["vector"]:
                    f(e)

            @blk.scalar
            def _(e):
                for f in b.th["scalar"]:
                    f(e)

            @blk.tensor
            def _(e):
                for f in b.th["tensor"]:
                    f(e)

    return nc


def stage_inputs(S, L0_attn_x, L0_attn_W_qkv, L0_attn_b_qkv, L0_attn_W_proj,
                 L0_attn_b_proj, L0_ffn_gate_w, L0_ffn_expert_bias, L0_ffn_w_up,
                 L0_ffn_w_down, L0_ffn_shared_w_up, L0_ffn_shared_w_down):
    asc = np.ascontiguousarray
    f32 = np.float32
    Wqkv = np.asarray(L0_attn_W_qkv, f32)
    bqkv = np.asarray(L0_attn_b_qkv, f32)
    shared = {
        "wqT": asc(Wqkv[0:D].T),
        "wkT": asc(Wqkv[D:2 * D].T),
        "wv_e": asc(Wqkv[2 * D:3 * D]),
        "wpT": asc(np.asarray(L0_attn_W_proj, f32).T),
        "bq_pe": asc(bqkv[0:D].reshape(ND, 128).T),
        "bk_pe": asc(bqkv[D:2 * D].reshape(ND, 128).T),
        "bv_pe": asc(bqkv[2 * D:3 * D].reshape(ND, 128).T),
        "bp_row": asc(np.asarray(L0_attn_b_proj, f32).reshape(1, D)),
        "gwT": asc(np.asarray(L0_ffn_gate_w, f32).T),
        "eb_row": asc(np.asarray(L0_ffn_expert_bias, f32).reshape(1, NEXP)),
        "wupT": asc(np.concatenate(
            [np.asarray(L0_ffn_w_up, f32).transpose(0, 2, 1),
             np.asarray(L0_ffn_shared_w_up, f32).T[None]], axis=0
        ).reshape(9 * D, HID)),
        "wdnT": asc(np.concatenate(
            [np.asarray(L0_ffn_w_down, f32).transpose(0, 2, 1),
             np.asarray(L0_ffn_shared_w_down, f32).T[None]], axis=0
        ).reshape(9 * HID, D)),
        "identb": np.eye(128).astype(ml_dtypes.bfloat16),
    }
    x = np.asarray(L0_attn_x, f32)
    in_maps = []
    for c in range(x.shape[0]):
        m = dict(shared)
        m["x"] = asc(x[c][:S])
        in_maps.append(m)
    return in_maps


def kernel(**inputs):
    S = 4096
    in_maps = stage_inputs(S, **inputs)
    nc = build_nc(S)
    res = run_bass_kernel_spmd(nc, in_maps, core_ids=list(range(8)))
    out = np.stack([np.asarray(res.results[c]["out"]) for c in range(8)], axis=0)
    return out.astype(np.float32)


# version 25
# speedup vs baseline: 1.2084x; 1.2084x over previous
import contextlib
import numpy as np
import ml_dtypes
import concourse.bass as bass
import concourse.mybir as mybir
from concourse.bass_utils import run_bass_kernel_spmd
from concourse.bass import IndirectOffsetOnAxis

F32 = mybir.dt.float32
BF16 = mybir.dt.bfloat16
I32 = mybir.dt.int32
AF = mybir.ActivationFunctionType
ALU = mybir.AluOpType
AX = mybir.AxisListType

D = 1024
HID = 1408
ND = 8
NH = 11
NEXP = 8
CH = 512
KC = 1024
EPS = 1e-5
SM_SCALE = 1.0 / 32.0

ENGS = ("sync", "gpsimd", "vector", "scalar", "tensor")


def _names(aps):
    out = set()
    for a in aps:
        t = getattr(a, "tensor", None)
        out.add(t.name if t is not None else str(a))
    return out


class B:

    def __init__(self, nc):
        self.nc = nc
        self.th = {e: [] for e in ENGS}
        self.cnt = {}
        self.handles = {}
        self.dw = {"vector": set(), "scalar": set()}
        self.dr = {"vector": set(), "scalar": set()}

    def sem(self, name, stack):
        self.handles[name] = stack.enter_context(self.nc.semaphore(name))
        self.cnt[name] = 0

    def wait(self, eng, sem, val):
        if val is None or val <= 0:
            return
        h = self.handles[sem]
        self.th[eng].append(lambda e, h=h, v=val: e.wait_ge(h, v))

    def wait_g(self, eng, guard):
        if guard:
            self.wait(eng, guard[0], guard[1])

    def emit(self, eng, fn, r=(), w=(), inc=None, n=1):
        if eng in self.dw:
            nr, nw = _names(r), _names(w)
            dw, dr = self.dw[eng], self.dr[eng]
            if (nr & dw) or (nw & (dw | dr)):
                self.th[eng].append(lambda e: e.drain())
                dw.clear()
                dr.clear()
            dw |= nw
            dr |= nr
        if inc is not None:
            self.cnt[inc] += n
            mark = self.cnt[inc]
            h = self.handles[inc]
            self.th[eng].append(
                lambda e, fn=fn, h=h, k=n: fn(e).then_inc(h, k))
            return mark
        self.th[eng].append(fn)
        return None

    def drain_all(self):
        for e in self.dw:
            self.th[e].append(lambda eng: eng.drain())
            self.dw[e].clear()
            self.dr[e].clear()

    def barrier(self):
        self.drain_all()
        snap = dict(self.cnt)
        for eng in ENGS:
            for s, c in snap.items():
                self.wait(eng, s, c)


def build_nc(S=4096):
    NT = S // 128
    NCH = S // CH
    TPC = CH // 128
    NKC = S // KC
    NCNK = S // 256 + 8
    CAPR = NCNK * 512
    NGT = CAPR // 128

    nc = bass.Bass("TRN2", target_bir_lowering=False)

    dt_ = nc.dram_tensor
    x_d = dt_("x", [S, D], F32, kind="ExternalInput").ap()
    wqT_d = dt_("wqT", [D, D], F32, kind="ExternalInput").ap()
    wkT_d = dt_("wkT", [D, D], F32, kind="ExternalInput").ap()
    wve_d = dt_("wv_e", [D, D], F32, kind="ExternalInput").ap()
    wpT_d = dt_("wpT", [D, D], F32, kind="ExternalInput").ap()
    bq_d = dt_("bq_pe", [128, ND], F32, kind="ExternalInput").ap()
    bk_d = dt_("bk_pe", [128, ND], F32, kind="ExternalInput").ap()
    bv_d = dt_("bv_pe", [128, ND], F32, kind="ExternalInput").ap()
    bp_d = dt_("bp_row", [1, D], F32, kind="ExternalInput").ap()
    gw_d = dt_("gwT", [D, NEXP], F32, kind="ExternalInput").ap()
    eb_d = dt_("eb_row", [1, NEXP], F32, kind="ExternalInput").ap()
    wup_d = dt_("wupT", [9 * D, HID], F32, kind="ExternalInput").ap()
    wdn_d = dt_("wdnT", [9 * HID, D], F32, kind="ExternalInput").ap()
    idb_d = dt_("identb", [128, 128], BF16, kind="ExternalInput").ap()

    out_d = dt_("out", [S, D], F32, kind="ExternalOutput").ap()

    xnT_h = dt_("xnT_h", [NCH * 128, ND * CH], BF16, kind="Internal").ap()
    kT_h = dt_("kT_h", [NCH * 128, ND * CH], BF16, kind="Internal").ap()
    qT_h = dt_("qT_h", [NT * 128, ND * 128], BF16, kind="Internal").ap()
    attn_h = dt_("attn_h", [S, D], BF16, kind="Internal").ap()
    bprime_h = dt_("bprime_h", [1, D], F32, kind="Internal").ap()
    xn_h = dt_("xn_h", [S, D], BF16, kind="Internal").ap()
    wupb_h = dt_("wupb_h", [9 * 128, ND * HID], BF16, kind="Internal").ap()
    wdnb_h = dt_("wdnb_h", [9 * 128, NH * D], BF16, kind="Internal").ap()
    moe_h = dt_("moe_h", [S, D], F32, kind="Internal").ap()
    idx_h = dt_("idx_h", [NCNK * 512, 2], I32, kind="Internal").ap()
    scr_h = dt_("scr_h", [64, 256], F32, kind="Internal").ap()

    with contextlib.ExitStack() as top:
        b = B(nc)
        for s in ("s_x0", "s_x1", "s_x2", "s_x3", "s_sp0", "s_sp1", "s_kt", "s_qt0", "s_qt1",
                  "s_ast0", "s_ast1", "s_xnt0", "s_xnt1", "s_fin0", "s_fin1",
                  "s_out0", "s_out1", "s_c1", "s_bp", "g_wa", "g_wb", "g_wu",
                  "g_wd", "g_wc", "g_ix", "g_tg0", "g_tg1", "g_wg0", "g_wg1", "g_sa0", "g_sa1", "s_xh",
                  "s_sh", "s_ig", "s_mf0", "s_mf1", "g_ib", "gm", "tm", "vm", "sm"):
            b.sem(s, top)

        pstack = top.enter_context(contextlib.ExitStack())
        _p = pstack.enter_context
        idb = _p(nc.sbuf_tensor("idb", [128, 128], BF16))[:]
        bqc = _p(nc.sbuf_tensor("bqc", [128, ND], F32))[:]
        bqc32 = _p(nc.sbuf_tensor("bqc32", [128, ND], F32))[:]
        bkc = _p(nc.sbuf_tensor("bkc", [128, ND], F32))[:]
        bvc = _p(nc.sbuf_tensor("bvc", [128, ND], BF16))[:]
        bpr = _p(nc.sbuf_tensor("bpr", [1, D], F32))[:]
        ebc = _p(nc.sbuf_tensor("ebc", [128, NEXP], F32))[:]
        bprime = _p(nc.sbuf_tensor("bprime", [1, D], F32))[:]
        bpbc = _p(nc.sbuf_tensor("bpbc", [128, D], F32))[:]
        wc_sb = _p(nc.sbuf_tensor("wc_sb", [128, ND, D], BF16))[:]
        s2tok = _p(nc.sbuf_tensor("s2tok", [128, NT, NEXP], F32))[:]
        en_all = _p(nc.sbuf_tensor("en_all", [128, NT, NEXP], F32))[:]
        mask_all = _p(nc.sbuf_tensor("mask_all", [128, NT, NEXP], F32))[:]
        idxgw_i = _p(nc.sbuf_tensor("idxgw_i", [128, NCNK], I32))[:]

        psA = _p(nc.psum_tensor("psA", [128, 1024], F32))[:]
        psB = _p(nc.psum_tensor("psB", [128, 1024], F32))[:]
        psC = _p(nc.psum_tensor("psC", [128, 1024], F32))[:]
        pT = [_p(nc.psum_tensor(f"pT{i}", [128, 1024], BF16))[:]
              for i in range(2)]
        pairs = [psA, psB]

        def cst(dst, src):
            b.emit("sync", lambda e, d=dst, s=src: e.dma_start(out=d, in_=s),
                   inc="s_c1", n=16)
        cst(idb, idb_d)
        cst(bqc, bq_d)
        cst(bkc, bk_d)
        cst(bpr, bp_d)
        cst(ebc, eb_d.broadcast_to([128, NEXP]))
        c1_final = b.cnt["s_c1"]

        regs = {}
        b.emit("gpsimd", lambda e: regs.update(
            capr=e.to_reg(CAPR - 1), sm1=e.to_reg(S - 1)))

        def wcast(dst, src, sem="g_wa"):
            return b.emit("gpsimd",
                          lambda e, d=dst, s=src: e.dma_start(out=d, in_=s),
                          inc=sem, n=16)

        xn_stack = contextlib.ExitStack()
        xn = xn_stack.enter_context(nc.sbuf_tensor("xn", [128, NT, D], BF16))[:]

        ph0 = contextlib.ExitStack()
        wve_sb = ph0.enter_context(
            nc.sbuf_tensor("wve_sb", [128, ND, D], BF16))[:]
        wpt_sb = ph0.enter_context(
            nc.sbuf_tensor("wpt_sb", [128, ND, D], BF16))[:]
        wcast(wve_sb, wve_d.rearrange("(j p) e -> p j e", p=128), "g_wb")
        wcast(wpt_sb, wpT_d.rearrange("(j p) e -> p j e", p=128), "g_wb")
        wcast(bvc, bv_d, "g_wb")
        wv_final = b.cnt["g_wb"]
        ph0.close()

        pha = contextlib.ExitStack()
        _a = pha.enter_context
        qbuf = [_a(nc.sbuf_tensor(f"qbuf{i}", [128, TPC, ND * 128], BF16))[:]
                for i in range(2)]
        kbuf = [_a(nc.sbuf_tensor(f"kbuf{i}", [128, ND, CH], BF16))[:]
                for i in range(2)]
        xbuf = [_a(nc.sbuf_tensor(f"xbuf{i}", [128, D], F32))[:] for i in range(4)]
        stats = _a(nc.sbuf_tensor("stats", [128, 12], F32))[:]
        aggr = _a(nc.sbuf_tensor("aggr", [128, 2], F32))[:]
        lnv = [_a(nc.sbuf_tensor(f"lnv{i}", [128, 1], F32))[:] for i in range(2)]
        negm = [_a(nc.sbuf_tensor(f"negm{i}", [128, 1], F32))[:] for i in range(2)]
        rstd_s = _a(nc.sbuf_tensor("rstd_s", [128, 1], F32))[:]
        nmr_s = _a(nc.sbuf_tensor("nmr_s", [128, 1], F32))[:]
        xnTb = [_a(nc.sbuf_tensor(f"xnTb{i}", [128, ND, CH], BF16))[:]
                for i in range(2)]
        wk_sb = _a(nc.sbuf_tensor("wk_sb", [128, ND, D], BF16))[:]
        wq_sb = _a(nc.sbuf_tensor("wq_sb", [128, ND, D], BF16))[:]
        gw_sb = _a(nc.sbuf_tensor("gw_sb", [128, ND, NEXP], BF16))[:]
        sgt = _a(nc.sbuf_tensor("sgt", [128, NEXP], F32))[:]
        sg = _a(nc.sbuf_tensor("sg", [128, NEXP], F32))[:]
        bsd = _a(nc.sbuf_tensor("bsd", [128, NEXP], F32))[:]
        top8 = _a(nc.sbuf_tensor("top8", [128, NEXP], F32))[:]
        mskb = _a(nc.sbuf_tensor("mskb", [128, NEXP], F32))[:]
        smv = _a(nc.sbuf_tensor("smv", [128, NEXP], F32))[:]

        wcast(wk_sb, wkT_d.rearrange("(j p) e -> p j e", p=128))
        wcast(wq_sb, wqT_d.rearrange("(j p) e -> p j e", p=128))
        wcast(gw_sb, gw_d.rearrange("(j p) e -> p j e", p=128))
        gw_final = b.cnt["g_wa"]

        b.wait("vector", "s_c1", c1_final)
        b.emit("vector", lambda e: e.tensor_scalar_mul(bqc32[:], bqc[:], SM_SCALE),
               r=[bqc], w=[bqc32], inc="vm")

        kslice = [psA[:, 0:512], psA[:, 512:1024], psB[:, 0:512], psB[:, 512:1024]]
        qslice = [psC[:, 0:512], psC[:, 512:1024]]
        gslice = psC[:, 0:NEXP]

        bankg = {}
        trg = {}

        sm_xn = {}
        vm_ln = {}
        vm_xnt = {}
        tr_mm = {}
        gmm_d = {}
        en_ev = {}
        tm_smm = {}
        vm_fin = {}
        qmm_last = {}

        def stage_A(c):
            if c < 4:
                b.wait("sync", "s_c1", c1_final + 16 * c)
                b.emit("sync",
                       lambda e, d=xbuf[c % 4], s=x_d[c * 128:(c + 1) * 128, :]:
                       e.dma_start(out=d, in_=s), inc="s_c1", n=16)
            else:
                b.wait("gpsimd", "sm", sm_xn.get(c - 4))
                b.emit("gpsimd",
                       lambda e, d=xbuf[c % 4], s=x_d[c * 128:(c + 1) * 128, :]:
                       e.dma_start(out=d, in_=s), inc=f"s_x{c % 4}", n=16)

        def stage_B(c):
            if c < 4:
                b.wait("vector", "s_c1", c1_final + 16 * (c + 1))
            else:
                b.wait("vector", f"s_x{c % 4}", 16 * (c // 4))
            b.wait("vector", "sm", sm_xn.get(c - 2))
            b.emit("vector", lambda e, xb=xbuf[c % 4]:
                   e.bn_stats(stats[:, 0:6], xb[:, 0:512]),
                   r=[xbuf[c % 4]], w=[stats])
            b.emit("vector", lambda e, xb=xbuf[c % 4]:
                   e.bn_stats(stats[:, 6:12], xb[:, 512:1024]),
                   r=[xbuf[c % 4]], w=[stats])
            b.emit("vector", lambda e: e.bn_aggr(aggr[:], stats[:]),
                   r=[stats], w=[aggr])
            b.emit("vector", lambda e, d=lnv[c % 2]:
                   e.tensor_scalar_add(d, aggr[:, 1:2], EPS),
                   r=[aggr], w=[lnv[c % 2]])
            vm_ln[c] = b.emit("vector", lambda e, d=negm[c % 2]:
                              e.tensor_scalar_mul(d, aggr[:, 0:1], -1.0),
                              r=[aggr], w=[negm[c % 2]], inc="vm")

        def stage_C(c):
            b.wait("scalar", "vm", vm_ln[c])
            b.emit("scalar", lambda e, s=lnv[c % 2]:
                   e.activation(rstd_s[:], s, AF.Ln),
                   r=[lnv[c % 2]], w=[rstd_s])
            b.emit("scalar", lambda e: e.activation(rstd_s[:], rstd_s[:],
                                                    AF.Exp, scale=-0.5),
                   r=[rstd_s], w=[rstd_s])
            b.emit("scalar", lambda e, s=negm[c % 2]:
                   e.mul(nmr_s[:], s, rstd_s[:]),
                   r=[negm[c % 2], rstd_s], w=[nmr_s])
            sm_xn[c] = b.emit("scalar",
                              lambda e, xb=xbuf[c % 4], d=xn[:, c, :]:
                              e.activation(d, xb, AF.Identity, scale=rstd_s[:],
                                           bias=nmr_s[:]),
                              r=[xbuf[c % 4], rstd_s, nmr_s], w=[xn], inc="sm")

        def stage_D(c):
            b.wait("tensor", "s_c1", c1_final)
            b.wait("tensor", "sm", sm_xn[c])
            b.wait_g("tensor", trg.get(c % 2))
            for j in range(ND):
                fn = (lambda e, o=pT[c % 2][:, j * 128:(j + 1) * 128],
                      i_=xn[:, c, j * 128:(j + 1) * 128]:
                      e.transpose(o, i_, idb))
                if j == ND - 1:
                    tr_mm[c] = b.emit("tensor", fn, inc="tm")
                else:
                    b.emit("tensor", fn)

        def stage_E(c):
            ch, ci = c // TPC, c % TPC
            b.wait("vector", "tm", tr_mm[c])
            if ci == 0 and ch >= 2:
                b.wait("vector", f"s_sp{ch % 2}", 96 * ((ch - 2) // 2 + 1))
                b.wait("vector", "tm", qmm_last.get(ch - 2))
                b.wait("vector", "tm", gmm_d.get((ch - 2) * TPC + TPC - 1))
            m = b.emit("vector",
                       lambda e, d=xnTb[ch % 2][:, :, ci * 128:(ci + 1) * 128],
                       s=pT[c % 2].rearrange("p (j t) -> p j t", j=ND):
                       e.tensor_copy(d, s), w=[xnTb[ch % 2]], inc="vm")
            vm_xnt[c] = m
            trg[c % 2] = ("vm", m)

        def stage_F(c):
            ch, ci = c // TPC, c % TPC
            if c == 0:
                b.wait("tensor", "g_wa", gw_final)
            b.wait("tensor", "vm", vm_xnt[c])
            b.wait_g("tensor", bankg.get("q0"))
            b.wait_g("tensor", bankg.get("gate"))
            for j in range(ND):
                fn = (lambda e, l=xnTb[ch % 2][:, j, ci * 128:(ci + 1) * 128],
                      r_=gw_sb[:, j, :], st=(j == 0), sp=(j == ND - 1):
                      e.matmul(gslice, l, r_, start=st, stop=sp))
                if j == ND - 1:
                    gmm_d[c] = b.emit("tensor", fn, inc="tm")
                else:
                    b.emit("tensor", fn)

        def stage_G(c):
            b.wait("scalar", "tm", gmm_d[c])
            en_ev[c] = b.emit("scalar",
                              lambda e, d=en_all[:, c, :]:
                              e.activation(d, gslice, AF.Exp, scale=-1.0),
                              w=[en_all], inc="sm")
            bankg["gate"] = ("sm", en_ev[c])

        def stage_H(c):
            b.wait("vector", "sm", en_ev[c])
            b.emit("vector", lambda e, s=en_all[:, c, :]:
                   e.tensor_scalar_add(sgt[:], s, 1.0), r=[en_all], w=[sgt])
            b.emit("vector", lambda e: e.reciprocal(sg[:], sgt[:]),
                   r=[sgt], w=[sg])
            b.emit("vector", lambda e: e.tensor_add(bsd[:], sg[:], ebc[:]),
                   r=[sg, ebc], w=[bsd])
            b.emit("vector", lambda e: e.max(top8[:], bsd[:]),
                   r=[bsd], w=[top8])
            b.emit("vector", lambda e, d=mask_all[:, c, :]: e.tensor_scalar(
                d, bsd[:], top8[:, 1:2], None, ALU.is_ge),
                   r=[bsd, top8], w=[mask_all])
            b.emit("vector", lambda e, mk=mask_all[:, c, :]:
                   e.tensor_mul(smv[:], sg[:], mk),
                   r=[sg, mask_all], w=[smv])
            b.emit("vector", lambda e, d=s2tok[:, c, :]:
                   e.tensor_mul(d, smv[:], smv[:]),
                   r=[smv], w=[s2tok], inc="vm")

        def chunk_work(ch):
            if ch == 0:
                b.wait("tensor", "g_wa", gw_final)
            b.wait("tensor", "vm", vm_xnt[ch * TPC + TPC - 1])
            k_ev_last = None
            for j2 in range(ND):
                key = f"k{j2 % 4}"
                sl = kslice[j2 % 4]
                b.wait_g("tensor", bankg.get(key))
                for j in range(ND):
                    fn = (lambda e, o=sl,
                          l=wk_sb[:, j, j2 * 128:(j2 + 1) * 128],
                          r_=xnTb[ch % 2][:, j, :], st=(j == 0), sp=(j == ND - 1):
                          e.matmul(o, l, r_, start=st, stop=sp))
                    if j == ND - 1:
                        m_k = b.emit("tensor", fn, inc="tm")
                    else:
                        b.emit("tensor", fn)
                b.wait("scalar", "tm", m_k)
                if ch >= 2 and j2 == 0:
                    b.wait("scalar", f"s_sp{ch % 2}", 96 * ((ch - 2) // 2 + 1))
                m = b.emit("scalar",
                           lambda e, o=kbuf[ch % 2][:, j2, :], i_=sl,
                           bb=bkc[:, j2:j2 + 1]:
                           e.activation(o, i_, AF.Identity, bias=bb),
                           w=[kbuf[ch % 2]], inc="sm")
                bankg[key] = ("sm", m)
                if j2 == ND - 1:
                    k_ev_last = m
            b.wait("sync", "vm", vm_xnt[ch * TPC + TPC - 1])
            b.emit("sync", lambda e, d=xnT_h[ch * 128:(ch + 1) * 128, :],
                   s=xnTb[ch % 2].rearrange("p j t -> p (j t)"):
                   e.dma_start(out=d, in_=s), inc=f"s_sp{ch % 2}", n=16)
            b.emit("sync", lambda e,
                   d=xn_h[ch * CH:(ch + 1) * CH, :].rearrange(
                       "(c p) d -> p c d", p=128),
                   s=xn[:, ch * TPC:(ch + 1) * TPC, :]:
                   e.dma_start(out=d, in_=s), inc="s_xh", n=16)
            b.wait("sync", "sm", k_ev_last)
            b.emit("sync", lambda e, d=kT_h[ch * 128:(ch + 1) * 128, :],
                   s=kbuf[ch % 2].rearrange("p j t -> p (j t)"):
                   e.dma_start(out=d, in_=s), inc=f"s_sp{ch % 2}", n=16)
            q_ev_last = None
            for j2 in range(ND):
                key = f"q{j2 % 2}"
                sl = qslice[j2 % 2]
                b.wait_g("tensor", bankg.get(key))
                if j2 % 2 == 0:
                    b.wait_g("tensor", bankg.get("gate"))
                for j in range(ND):
                    fn = (lambda e, o=sl,
                          l=wq_sb[:, j, j2 * 128:(j2 + 1) * 128],
                          r_=xnTb[ch % 2][:, j, :], st=(j == 0), sp=(j == ND - 1):
                          e.matmul(o, l, r_, start=st, stop=sp))
                    if j == ND - 1:
                        m_q = b.emit("tensor", fn, inc="tm")
                    else:
                        b.emit("tensor", fn)
                if j2 == ND - 1:
                    qmm_last[ch] = m_q
                b.wait("scalar", "tm", m_q)
                if ch >= 2 and j2 == 0:
                    b.wait("scalar", f"s_sp{ch % 2}", 96 * ((ch - 2) // 2 + 1))
                m = b.emit("scalar",
                           lambda e,
                           o=qbuf[ch % 2][:, :, j2 * 128:(j2 + 1) * 128],
                           i_=sl.rearrange("p (u t) -> p u t", u=TPC),
                           bb=bqc32[:, j2:j2 + 1]:
                           e.activation(o, i_, AF.Identity, scale=SM_SCALE,
                                        bias=bb),
                           w=[qbuf[ch % 2]], inc="sm")
                bankg[key] = ("sm", m)
                q_ev_last = m
            b.wait("sync", "sm", q_ev_last)
            for u2 in range(TPC):
                t_ = ch * TPC + u2
                b.emit("sync", lambda e, d=qT_h[t_ * 128:(t_ + 1) * 128, :],
                       s=qbuf[ch % 2][:, u2, :]:
                       e.dma_start(out=d, in_=s), inc=f"s_sp{ch % 2}", n=16)

        wcg = {0: None, 1: None}
        ph0_state = {}

        def ph0_wcomb():
            b.wait("tensor", "g_wb", wv_final)
            for m in range(ND):
                wpr = [psB, psC][m % 2]
                b.wait_g("tensor", wcg[m % 2])
                for j in range(ND):
                    for hf in range(2):
                        fn = (lambda e, o=wpr[:, hf * 512:(hf + 1) * 512],
                              l=wve_sb[:, j, m * 128:(m + 1) * 128],
                              r_=wpt_sb[:, j, hf * 512:(hf + 1) * 512],
                              st=(j == 0), sp=(j == ND - 1):
                              e.matmul(o, l, r_, start=st, stop=sp))
                        if j == ND - 1 and hf == 1:
                            m_wc = b.emit("tensor", fn, inc="tm")
                        else:
                            b.emit("tensor", fn)
                b.wait("scalar", "tm", m_wc)
                m_wce = b.emit("scalar", lambda e, o=wc_sb[:, m, :], s=wpr:
                               e.activation(o, s, AF.Identity),
                               w=[wc_sb], inc="sm")
                wcg[m % 2] = ("sm", m_wce)
            b.wait_g("tensor", wcg[1])
            for j in range(ND):
                for hf in range(2):
                    fn = (lambda e, o=psC[0:1, hf * 512:(hf + 1) * 512],
                          l=bvc[:, j:j + 1],
                          r_=wpt_sb[:, j, hf * 512:(hf + 1) * 512],
                          st=(j == 0), sp=(j == ND - 1):
                          e.matmul(o, l, r_, start=st, stop=sp))
                    if j == ND - 1 and hf == 1:
                        ph0_state["m_bp"] = b.emit("tensor", fn, inc="tm")
                    else:
                        b.emit("tensor", fn)

        def ph0_finish():
            b.wait("vector", "tm", ph0_state["m_bp"])
            b.wait("vector", "s_c1", c1_final)
            m_bpr = b.emit("vector", lambda e: e.tensor_add(
                bprime[:], psC[0:1, :], bpr[:]),
                r=[bpr], w=[bprime], inc="vm")
            b.wait("sync", "vm", m_bpr)
            b.emit("sync", lambda e: e.dma_start(out=bprime_h, in_=bprime[:]),
                   inc="s_bp", n=16)
            b.wait("sync", "s_bp", 16)
            b.emit("sync", lambda e: e.dma_start(
                out=bpbc[:], in_=bprime_h.broadcast_to([128, D])),
                inc="s_bp", n=16)
            bankg["k2"] = bankg["k3"] = wcg[0]
            bankg["q0"] = bankg["q1"] = bankg["gate"] = ("vm", m_bpr)
        bp_final = 32

        for step in range(NT + 9):
            if step < NT:
                stage_A(step)
                stage_B(step)
            if step == 1:
                ph0_wcomb()
            if step == 4:
                ph0_finish()
            c = step - 1
            if 0 <= c < NT:
                stage_C(c)
            c = step - 2
            if 0 <= c < NT:
                stage_D(c)
                stage_E(c)
            c = step - 3
            if 0 <= c < NT:
                if c % TPC == TPC - 1:
                    chunk_work(c // TPC)
            c = step - 6
            if 0 <= c < NT:
                stage_F(c)
                stage_G(c)
            c = step - 8
            if 0 <= c < NT:
                stage_H(c)

        b.barrier()
        pha.close()

        phb = contextlib.ExitStack()
        _b2 = phb.enter_context
        kT = _b2(nc.sbuf_tensor("kT", [128, ND, S], BF16))[:]

        def precast_weights():
            for g9 in range(9):
                wcast(wupb_h[g9 * 128:(g9 + 1) * 128, :].rearrange(
                          "p (jd h) -> p jd h", jd=ND),
                      wup_d[g9 * D:(g9 + 1) * D, :].rearrange(
                          "(jd p) h -> p jd h", p=128), "g_wc")
                wcast(wdnb_h[g9 * 128:(g9 + 1) * 128, :].rearrange(
                          "p (h d) -> p h d", h=NH),
                      wdn_d[g9 * HID:(g9 + 1) * HID, :].rearrange(
                          "(h p) d -> p h d", p=128), "g_wc")

        for jq in range(ND):
            b.emit("sync" if jq < 4 else "scalar", lambda e, j_=jq: e.dma_start(
                out=kT.rearrange("p j (c t) -> p j c t", t=CH)[:, j_:j_ + 1],
                in_=kT_h.rearrange("(c p) (j t) -> p j c t",
                                   p=128, t=CH)[:, j_:j_ + 1]),
                inc="s_kt", n=16)
        kt_final = b.cnt["s_kt"]

        qT_ld = [_b2(nc.sbuf_tensor(f"qT_ld{i}", [128, ND * 128], BF16))[:]
                 for i in range(2)]
        P_sb = [_b2(nc.sbuf_tensor(f"P_sb{i}", [128, KC], BF16))[:]
                for i in range(2)]
        PT_sb = [_b2(nc.sbuf_tensor(f"PT_sb{i}", [128, 8, 128], BF16))[:]
                 for i in range(2)]
        oacc = _b2(nc.sbuf_tensor("oacc", [128, D], F32))[:]
        outp = _b2(nc.sbuf_tensor("outp", [128, D], BF16))[:]
        outpT = _b2(nc.sbuf_tensor("outpT", [128, ND, 128], BF16))[:]
        atile = [_b2(nc.sbuf_tensor(f"atile{i}", [128, D], BF16))[:]
                 for i in range(2)]
        zac2 = [_b2(nc.sbuf_tensor(f"zacc{i}", [128, 1], F32))[:]
                for i in range(2)]
        zc = _b2(nc.sbuf_tensor("zc", [128, 1], F32))[:]
        nmx = [_b2(nc.sbuf_tensor(f"nmx{i}", [128, 1], F32))[:] for i in range(4)]
        nmc = _b2(nc.sbuf_tensor("nmc", [128, 1], F32))[:]
        dal = [_b2(nc.sbuf_tensor(f"dal{i}", [128, 1], F32))[:] for i in range(2)]
        alp2 = [_b2(nc.sbuf_tensor(f"alp{i}", [128, 1], F32))[:]
                for i in range(4)]
        rz = _b2(nc.sbuf_tensor("rz", [128, 1], F32))[:]


        BIG = 65536.0
        ixs = contextlib.ExitStack()
        _ix = ixs.enter_context
        inclt2 = [_ix(nc.sbuf_tensor(f"inclt{i}", [128, NT], F32))[:]
                  for i in range(2)]
        inclt = inclt2[0]
        cnt_all = _ix(nc.sbuf_tensor("cnt_all", [128, NEXP], F32))[:]
        cntT = _ix(nc.sbuf_tensor("cntT", [NEXP, 128], F32))[:]
        c_incl = _ix(nc.sbuf_tensor("c_incl", [NEXP, 128], F32))[:]
        c_excl = _ix(nc.sbuf_tensor("c_excl", [NEXP, 128], F32))[:]
        tot8 = _ix(nc.sbuf_tensor("tot8", [NEXP, 1], F32))[:]
        ccnt = _ix(nc.sbuf_tensor("ccnt", [NEXP, 1], F32))[:]
        tfr = _ix(nc.sbuf_tensor("tfr", [NEXP, 1], F32))[:]
        cc_row = _ix(nc.sbuf_tensor("cc_row", [1, NEXP], F32))[:]
        ti_row = _ix(nc.sbuf_tensor("ti_row", [1, NEXP], F32))[:]
        S_row = _ix(nc.sbuf_tensor("S_row", [1, NEXP], F32))[:]
        S_col = _ix(nc.sbuf_tensor("S_col", [NEXP, 1], F32))[:]
        rbase = _ix(nc.sbuf_tensor("rbase", [NEXP, 1], F32))[:]
        off_pg = _ix(nc.sbuf_tensor("off_pg", [NEXP, 128], F32))[:]
        off_tok = _ix(nc.sbuf_tensor("off_tok", [128, NEXP], F32))[:]
        posb2 = [_ix(nc.sbuf_tensor(f"posb{i}", [128, NT], F32))[:]
                 for i in range(2)]
        pos22 = [_ix(nc.sbuf_tensor(f"pos2{i}", [128, NT], F32))[:]
                 for i in range(2)]
        posm2 = [_ix(nc.sbuf_tensor(f"posm{i}", [128, NT], I32))[:]
                 for i in range(2)]
        stg2 = [_ix(nc.sbuf_tensor(f"stg{i}", [128, NT, 2], I32))[:]
                for i in range(2)]
        iota_id = _ix(nc.sbuf_tensor("iota_id", [128, NT], I32))[:]
        jj8 = _ix(nc.sbuf_tensor("jj8", [NEXP, NCNK], I32))[:]
        jjf = _ix(nc.sbuf_tensor("jjf", [NEXP, NCNK], F32))[:]
        geb = _ix(nc.sbuf_tensor("geb", [NEXP, NCNK], F32))[:]
        ones8 = _ix(nc.sbuf_tensor("ones8", [NEXP, 1], F32))[:]
        expf = _ix(nc.sbuf_tensor("expf", [1, NCNK], F32))[:]
        expbc = _ix(nc.sbuf_tensor("expbc", [128, NCNK], F32))[:]
        iotap_i = _ix(nc.sbuf_tensor("iotap_i", [128, NCNK], I32))[:]
        iotap_f = _ix(nc.sbuf_tensor("iotap_f", [128, NCNK], F32))[:]
        idxgw_f = _ix(nc.sbuf_tensor("idxgw_f", [128, NCNK], F32))[:]
        init_sb = _ix(nc.sbuf_tensor("init_sb", [128, CAPR * 2 // 128], I32))[:]

        m_ones = b.emit("vector", lambda e: e.memset(ones8, 1.0),
                        w=[ones8], inc="vm")
        m_init = b.emit("vector", lambda e: e.memset(init_sb, S),
                        w=[init_sb], inc="vm")
        b.wait("gpsimd", "vm", m_init)
        b.emit("gpsimd", lambda e: e.iota(iota_id, pattern=[[128, NT]], base=0,
                                          channel_multiplier=1))
        b.emit("gpsimd", lambda e: e.iota(jj8, pattern=[[1, NCNK]], base=0,
                                          channel_multiplier=0))
        m_iot = b.emit("gpsimd", lambda e: e.iota(
            iotap_i, pattern=[[0, NCNK]], base=0, channel_multiplier=1),
            inc="gm")
        b.emit("gpsimd", lambda e: e.dma_start(out=idx_h, in_=init_sb),
               inc="g_ix", n=16)
        ix_init = b.cnt["g_ix"]
        b.wait("vector", "gm", m_iot)
        b.emit("vector", lambda e: e.tensor_copy(jjf, jj8), r=[jj8], w=[jjf])
        b.emit("vector", lambda e: e.tensor_copy(iotap_f, iotap_i),
               r=[iotap_i], w=[iotap_f])
        for g in range(NEXP):
            b.emit("vector", lambda e, mk=mask_all[:, :, g]:
                   e.tensor_tensor_scan(inclt, mk, mk, 0.0, ALU.add, ALU.bypass),
                   r=[mask_all], w=[inclt])
            b.emit("vector", lambda e, d=cnt_all[:, g:g + 1]:
                   e.tensor_copy(d, inclt[:, NT - 1:NT]),
                   r=[inclt], w=[cnt_all])
        m1 = b.emit("vector", lambda e: e.engine_nop(), inc="vm")
        b.wait("gpsimd", "vm", m1)
        b.emit("gpsimd", lambda e: e.dma_start(
            out=scr_h[0:NEXP, 0:128].rearrange("g p -> p g"), in_=cnt_all,
            allow_slow_non_contiguous=True),
            inc="g_ix", n=16)
        b.wait("gpsimd", "g_ix", b.cnt["g_ix"])
        b.emit("gpsimd", lambda e: e.dma_start(out=cntT, in_=scr_h[0:NEXP, 0:128]),
               inc="g_ix", n=16)
        b.wait("vector", "g_ix", b.cnt["g_ix"])
        b.emit("vector", lambda e: e.tensor_tensor_scan(
            c_incl, cntT, cntT, 0.0, ALU.add, ALU.bypass), r=[cntT], w=[c_incl])
        b.emit("vector", lambda e: e.tensor_sub(c_excl, c_incl, cntT),
               r=[c_incl, cntT], w=[c_excl])
        b.emit("vector", lambda e: e.tensor_copy(tot8, c_incl[:, 127:128]),
               r=[c_incl], w=[tot8])
        b.emit("vector", lambda e: e.tensor_scalar_add(ccnt, tot8, 511.0),
               r=[tot8], w=[ccnt])
        b.emit("vector", lambda e: e.tensor_scalar_mul(ccnt, ccnt, 1.0 / 512.0),
               r=[ccnt], w=[ccnt])
        b.emit("vector", lambda e: e.tensor_scalar_add(tfr, ccnt, 8388608.0),
               r=[ccnt], w=[tfr])
        b.emit("vector", lambda e: e.tensor_scalar_add(tfr, tfr, -8388608.0),
               r=[tfr], w=[tfr])
        b.emit("vector", lambda e: e.tensor_tensor(tot8, tfr, ccnt, ALU.is_gt),
               r=[tfr, ccnt], w=[tot8])
        m2 = b.emit("vector", lambda e: e.tensor_sub(ccnt, tfr, tot8),
                    r=[tfr, tot8], w=[ccnt], inc="vm")
        b.wait("gpsimd", "vm", m2)
        b.emit("gpsimd", lambda e: e.dma_start(
            out=scr_h[8:9, 0:NEXP].rearrange("o g -> g o"), in_=ccnt,
            allow_slow_non_contiguous=True),
            inc="g_ix", n=16)
        b.wait("gpsimd", "g_ix", b.cnt["g_ix"])
        b.emit("gpsimd", lambda e: e.dma_start(out=cc_row, in_=scr_h[8:9, 0:NEXP]),
               inc="g_ix", n=16)
        b.wait("vector", "g_ix", b.cnt["g_ix"])
        b.emit("vector", lambda e: e.tensor_tensor_scan(
            ti_row, cc_row, cc_row, 0.0, ALU.add, ALU.bypass),
            r=[cc_row], w=[ti_row])
        m3 = b.emit("vector", lambda e: e.tensor_sub(S_row, ti_row, cc_row),
                    r=[ti_row, cc_row], w=[S_row], inc="vm")
        b.wait("gpsimd", "vm", m3)
        b.emit("gpsimd", lambda e: e.dma_start(
            out=scr_h[9:10, 0:NEXP], in_=S_row), inc="g_ix", n=16)
        b.wait("gpsimd", "g_ix", b.cnt["g_ix"])
        b.emit("gpsimd", lambda e: e.dma_start(
            out=S_col, in_=scr_h[9:10, 0:NEXP].rearrange("o g -> g o"),
            allow_slow_non_contiguous=True),
            inc="g_ix", n=16)
        b.wait("vector", "g_ix", b.cnt["g_ix"])
        b.emit("vector", lambda e: e.tensor_scalar_mul(rbase, S_col, 512.0),
               r=[S_col], w=[rbase])
        m4 = b.emit("vector", lambda e: e.tensor_scalar(
            off_pg, c_excl, rbase, None, ALU.add),
            r=[c_excl, rbase], w=[off_pg], inc="vm")
        b.wait("gpsimd", "vm", m4)
        b.emit("gpsimd", lambda e: e.dma_start(
            out=scr_h[16:16 + NEXP, 0:128], in_=off_pg), inc="g_ix", n=16)
        b.wait("gpsimd", "g_ix", b.cnt["g_ix"])
        b.emit("gpsimd", lambda e: e.dma_start(
            out=off_tok, in_=scr_h[16:16 + NEXP, 0:128].rearrange("g p -> p g"),
            allow_slow_non_contiguous=True),
            inc="g_ix", n=16)
        ix_pre = b.cnt["g_ix"]

        def ix_expert(g):
            gp = g % 2
            inc_, pb, p2_, pm, sg2 = (inclt2[gp], posb2[gp], pos22[gp],
                                      posm2[gp], stg2[gp])
            if g == 0:
                b.wait("vector", "g_ix", ix_pre)
            b.emit("vector", lambda e, mk=mask_all[:, :, g]:
                   e.tensor_tensor_scan(inc_, mk, mk, 0.0, ALU.add, ALU.bypass),
                   r=[mask_all], w=[inc_])
            b.emit("vector", lambda e, mk=mask_all[:, :, g]:
                   e.scalar_tensor_tensor(pb, mk, -1.0, inc_, ALU.mult, ALU.add),
                   r=[mask_all, inc_], w=[pb])
            b.emit("vector", lambda e, oc=off_tok[:, g:g + 1]:
                   e.tensor_scalar(pb, pb, oc, None, ALU.add),
                   r=[pb, off_tok], w=[pb])
            b.emit("vector", lambda e: e.tensor_scalar_add(p2_, pb, BIG),
                   r=[pb], w=[p2_])
            b.emit("vector", lambda e, mk=mask_all[:, :, g]:
                   e.scalar_tensor_tensor(pb, mk, -BIG, p2_, ALU.mult, ALU.add),
                   r=[mask_all, p2_], w=[pb])
            b.emit("vector", lambda e: e.tensor_copy(pm, pb),
                   r=[pb], w=[pm])
            b.emit("vector", lambda e: e.tensor_copy(
                sg2[:, :, 0:1].rearrange("p t o -> p (t o)"), iota_id),
                r=[iota_id], w=[sg2])
            mg = b.emit("vector", lambda e, s2=s2tok[:, :, g:g + 1].bitcast(I32):
                        e.tensor_copy(sg2[:, :, 1:2].rearrange("p t o -> p (t o)"),
                                      s2.rearrange("p t o -> p (t o)")),
                        r=[s2tok], w=[sg2], inc="vm")
            b.wait("gpsimd", "vm", mg)
            b.wait("gpsimd", "g_ix", b.cnt["g_ix"])
            for c in range(NT):
                b.emit("gpsimd", lambda e, po=pm[:, c:c + 1],
                       sg_=sg2[:, c, :]:
                       e.indirect_dma_start(
                           out=idx_h, out_offset=IndirectOffsetOnAxis(po, 0),
                           in_=sg_, in_offset=None,
                           bounds_check=regs["capr"], oob_is_err=False),
                       inc="g_ix", n=16)
            return b.cnt["g_ix"]

        ix_scat = {}

        pairg = {0: None, 1: None}
        avg_g = {"c": None}
        ptg = {0: None, 1: None}
        sm_exp = {}
        sm_pt = {}
        m_s_d = {}
        m_av_d = {}

        def p2_scores_mm(t, u):
            seq = t * NKC + u
            pr = pairs[seq % 2]
            b.wait("tensor", f"s_qt{t % 2}", 16 * (t // 2 + 1))
            if t == 0 and u == 0:
                b.wait("tensor", "s_kt", kt_final)
            b.wait_g("tensor", pairg[seq % 2])
            for j in range(ND):
                for hf in range(2):
                    fn = (lambda e, o=pr[:, hf * 512:(hf + 1) * 512],
                          l=qT_ld[t % 2][:, j * 128:(j + 1) * 128],
                          r_=kT[:, j, u * KC + hf * 512:u * KC + (hf + 1) * 512],
                          st=(j == 0), sp=(j == ND - 1):
                          e.matmul(o, l, r_, start=st, stop=sp))
                    if j == ND - 1 and hf == 1:
                        m_s = b.emit("tensor", fn, inc="tm")
                    else:
                        b.emit("tensor", fn)
            if u == NKC - 1:
                tm_smm[t] = m_s
            m_s_d[(t, u)] = m_s

        def p2_scores_max(t, u):
            seq = t * NKC + u
            pr = pairs[seq % 2]
            b.wait("vector", "tm", m_s_d[(t, u)])
            b.emit("vector", lambda e, s=pr: e.reduce_max(
                nmc[:], s, axis=AX.X, negate=True), w=[nmc])
            if u == 0:
                m_mx = b.emit("vector", lambda e, d=nmx[seq % 4]:
                              e.tensor_copy(d, nmc[:]), r=[nmc],
                              w=[nmx[seq % 4]], inc="vm")
            else:
                b.emit("vector", lambda e, d=nmx[seq % 4], a=nmx[(seq - 1) % 4]:
                       e.tensor_tensor(d, a, nmc[:], ALU.min),
                       r=[nmx[(seq - 1) % 4], nmc], w=[nmx[seq % 4]])
                m_mx = b.emit("vector", lambda e, a=nmx[(seq - 1) % 4],
                              n_=nmx[seq % 4], dd=dal[seq % 2]:
                              e.tensor_sub(dd, n_, a),
                              r=[nmx[seq % 4], nmx[(seq - 1) % 4]],
                              w=[dal[seq % 2]], inc="vm")
            return m_mx

        def p2_scores_exp(t, u, m_mx):
            seq = t * NKC + u
            pr = pairs[seq % 2]
            al = alp2[seq % 4]
            b.wait("scalar", "vm", m_mx)
            if u > 0:
                b.emit("scalar", lambda e, a=al, dd=dal[seq % 2]:
                       e.activation(a, dd, AF.Exp), r=[dal[seq % 2]], w=[al])
            sm_exp[(t, u)] = b.emit(
                "scalar", lambda e, o=P_sb[seq % 2], s=pr, n_=nmx[seq % 4]:
                e.activation(o, s, AF.Exp, bias=n_, accum_out=zc[:]),
                r=[nmx[seq % 4]], w=[P_sb[seq % 2], zc], inc="sm")
            pairg[seq % 2] = ("sm", sm_exp[(t, u)])

        def p2_scores_z(t, u):
            seq = t * NKC + u
            al = alp2[seq % 4]
            za = zac2[t % 2]
            b.wait("vector", "sm", sm_exp[(t, u)])
            if u == 0:
                b.emit("vector", lambda e, z=za: e.tensor_copy(z, zc[:]),
                       r=[zc], w=[za])
            else:
                b.emit("vector", lambda e, z=za, a=al: e.scalar_tensor_tensor(
                    z, z, a, zc[:], ALU.mult, ALU.add),
                    r=[za, al, zc], w=[za])

        def p2_pt_tr(t, u):
            seq = t * NKC + u
            b.wait("tensor", "sm", sm_exp[(t, u)])
            b.wait_g("tensor", ptg[seq % 2])
            for i in range(8):
                fn = (lambda e, o=pT[seq % 2][:, i * 128:(i + 1) * 128],
                      i_=P_sb[seq % 2][:, i * 128:(i + 1) * 128]:
                      e.transpose(o, i_, idb))
                if i == 7:
                    m_pt = b.emit("tensor", fn, inc="tm")
                else:
                    b.emit("tensor", fn)
            return m_pt

        def p2_pt_evac(t, u, m_pt):
            seq = t * NKC + u
            b.wait("scalar", "tm", m_pt)
            sm_pt[(t, u)] = b.emit(
                "scalar", lambda e, d=PT_sb[seq % 2],
                s=pT[seq % 2].rearrange("p (i t) -> p i t", i=8):
                e.activation(d, s, AF.Identity), w=[PT_sb[seq % 2]], inc="sm")
            ptg[seq % 2] = ("sm", sm_pt[(t, u)])

        def p2_av(t, u):
            seq = t * NKC + u
            b.wait("tensor", "sm", sm_pt[(t, u)])
            b.wait_g("tensor", avg_g["c"])
            for i in range(8):
                for hf in range(2):
                    fn = (lambda e, o=psC[:, hf * 512:(hf + 1) * 512],
                          l=PT_sb[seq % 2][:, i, :],
                          r_=xn[:, u * 8 + i, hf * 512:(hf + 1) * 512],
                          st=(i == 0), sp=(i == 7):
                          e.matmul(o, l, r_, start=st, stop=sp))
                    if i == 7 and hf == 1:
                        m_av = b.emit("tensor", fn, inc="tm")
                    else:
                        b.emit("tensor", fn)
            m_av_d[(t, u)] = m_av

        def p2_oacc(t, u):
            seq = t * NKC + u
            al = alp2[seq % 4]
            b.wait("vector", "tm", m_av_d[(t, u)])
            if u == 0:
                m_oe = b.emit("vector", lambda e: e.tensor_copy(oacc[:], psC[:]),
                              w=[oacc], inc="vm")
            else:
                m_oe = b.emit("vector", lambda e, a=al: e.scalar_tensor_tensor(
                    oacc[:], oacc[:], a, psC[:], ALU.mult, ALU.add),
                    r=[oacc, al], w=[oacc], inc="vm")
            avg_g["c"] = ("vm", m_oe)

        def qt_load(t):
            if t >= NT:
                return
            b.wait("sync", "tm", tm_smm.get(t - 2))
            b.emit("sync", lambda e, d=qT_ld[t % 2],
                   s=qT_h[t * 128:(t + 1) * 128, :]:
                   e.dma_start(out=d, in_=s), inc=f"s_qt{t % 2}", n=16)

        b.wait("vector", "vm", m_ones)
        mge = b.emit("vector", lambda e: e.tensor_scalar(
            geb, jjf, S_col, None, ALU.is_ge), r=[jjf, S_col], w=[geb], inc="vm")
        b.wait("tensor", "vm", mge)
        b.wait_g("tensor", avg_g["c"])
        m_ec = b.emit("tensor", lambda e: e.matmul(
            psC[0:1, 0:NCNK], ones8, geb, start=True, stop=True), inc="tm")
        b.wait("vector", "tm", m_ec)
        m_ec2 = b.emit("vector", lambda e: e.tensor_scalar_add(
            expf, psC[0:1, 0:NCNK], -1.0), w=[expf], inc="vm")
        avg_g["c"] = ("vm", m_ec2)
        b.wait("gpsimd", "vm", m_ec2)
        b.emit("gpsimd", lambda e: e.dma_start(
            out=scr_h[10:11, 0:NCNK], in_=expf), inc="g_ib", n=16)
        b.wait("gpsimd", "g_ib", b.cnt["g_ib"])
        b.emit("gpsimd", lambda e: e.dma_start(
            out=expbc, in_=scr_h[10:11, 0:NCNK].broadcast_to([128, NCNK])),
            inc="g_ib", n=16)
        b.wait("vector", "g_ib", b.cnt["g_ib"])
        b.emit("vector", lambda e: e.scalar_tensor_tensor(
            idxgw_f, expbc, 128.0, iotap_f, ALU.mult, ALU.add),
            r=[expbc, iotap_f], w=[idxgw_f])
        m_wix = b.emit("vector", lambda e: e.tensor_copy(idxgw_i, idxgw_f),
                       r=[idxgw_f], w=[idxgw_i], inc="vm")


        qt_load(0)
        qt_load(1)
        for bseq in range(2):
            bt, bu = divmod(bseq, NKC)
            p2_scores_mm(bt, bu)
            m_mxb = p2_scores_max(bt, bu)
            p2_scores_exp(bt, bu, m_mxb)
            p2_scores_z(bt, bu)
        def tail_v(t):
            b.emit("vector", lambda e, z=zac2[t % 2]: e.reciprocal(rz[:], z),
                   r=[zac2[t % 2]], w=[rz])
            return b.emit("vector", lambda e: e.tensor_scalar(
                outp[:], oacc[:], rz[:], None, ALU.mult),
                r=[oacc, rz], w=[outp], inc="vm")

        def tail_pe(t, m_op):
            seqlast = t * NKC + NKC - 1
            b.wait("tensor", "vm", m_op)
            b.wait_g("tensor", ptg[seqlast % 2])
            for j in range(ND):
                fn = (lambda e, o=pT[seqlast % 2][:, j * 128:(j + 1) * 128],
                      i_=outp[:, j * 128:(j + 1) * 128]: e.transpose(o, i_, idb))
                if j == ND - 1:
                    m_t1 = b.emit("tensor", fn, inc="tm")
                else:
                    b.emit("tensor", fn)
            b.wait("scalar", "tm", m_t1)
            m_ev1 = b.emit("scalar", lambda e, s=pT[seqlast % 2].rearrange(
                "p (j t) -> p j t", j=ND):
                e.activation(outpT[:], s, AF.Identity), w=[outpT], inc="sm")
            ptg[seqlast % 2] = ("sm", m_ev1)
            b.wait("tensor", "sm", m_ev1)
            b.wait_g("tensor", avg_g["c"])
            for j in range(ND):
                for hf in range(2):
                    fn = (lambda e, o=psC[:, hf * 512:(hf + 1) * 512],
                          l=outpT[:, j, :],
                          r_=wc_sb[:, j, hf * 512:(hf + 1) * 512],
                          st=(j == 0), sp=(j == ND - 1):
                          e.matmul(o, l, r_, start=st, stop=sp))
                    if j == ND - 1 and hf == 1:
                        m_pj = b.emit("tensor", fn, inc="tm")
                    else:
                        b.emit("tensor", fn)
            b.wait("vector", "tm", m_pj)
            b.wait("vector", "s_bp", bp_final)
            if t >= 2:
                b.wait("vector", f"s_ast{t % 2}", 16 * (t // 2))
            m_at = b.emit("vector", lambda e, d=atile[t % 2]:
                          e.tensor_add(d, psC[:], bpbc[:]),
                          r=[bpbc], w=[atile[t % 2]], inc="vm")
            avg_g["c"] = ("vm", m_at)
            b.wait("sync", "vm", m_at)
            b.emit("sync", lambda e, d=attn_h[t * 128:(t + 1) * 128, :],
                   s=atile[t % 2]: e.dma_start(out=d, in_=s),
                   inc=f"s_ast{t % 2}", n=16)

        for t in range(NT):
            if t < NEXP:
                if t >= 2:
                    b.wait("vector", "g_ix", ix_scat[t - 2])
                ix_scat[t] = ix_expert(t)
                if t == NEXP - 1:
                    precast_weights()
            for u in range(NKC):
                m_pt = p2_pt_tr(t, u)
                p2_pt_evac(t, u, m_pt)
                nseq = t * NKC + u + 2
                nxt = (nseq // NKC, nseq % NKC) if nseq < NT * NKC else None
                if nxt:
                    p2_scores_mm(*nxt)
                    m_mx = p2_scores_max(*nxt)
                p2_av(t, u)
                if u == 0 and t >= 1:
                    m_opd = tail_v(t - 1)
                if nxt:
                    p2_scores_exp(*nxt, m_mx)
                    p2_scores_z(*nxt)
                p2_oacc(t, u)
                if u == 0 and t >= 1:
                    tail_pe(t - 1, m_opd)
            qt_load(t + 2)
        m_opd = tail_v(NT - 1)
        tail_pe(NT - 1, m_opd)

        b.barrier()
        ixs.close()
        phb.close()
        xn_stack.close()

        phc = contextlib.ExitStack()
        _c = phc.enter_context
        wupg = [_c(nc.sbuf_tensor(f"wupg{i}", [128, ND, HID], BF16))[:]
                for i in range(2)]
        wdng = [_c(nc.sbuf_tensor(f"wdng{i}", [128, NH, D], BF16))[:]
                for i in range(2)]
        xnTm_u = _c(nc.sbuf_tensor("xnTm_u", [128, 2 * ND * CH], BF16))[:]
        xnTm = [xnTm_u[:, i * ND * CH:(i + 1) * ND * CH].rearrange(
            "p (j t) -> p j t", j=ND) for i in range(2)]
        h_sb = _c(nc.sbuf_tensor("h_sb", [128, NH, CH], BF16))[:]
        rbf = [_c(nc.sbuf_tensor(f"rbf{i}", [128, CH], BF16))[:] for i in range(2)]
        osh = _c(nc.sbuf_tensor("osh", [128, D], F32))[:]
        gbuf = [_c(nc.sbuf_tensor(f"gbuf{i}", [128, TPC, D], BF16))[:]
                for i in range(2)]
        gxT = _c(nc.sbuf_tensor("gxT", [128, ND, CH], BF16))[:]
        obuf = xnTm_u.bitcast(F32).rearrange("p (u d) -> p u d", u=TPC)
        idxg_sb = _c(nc.sbuf_tensor("idxg_sb", [128, NGT, 2], I32))[:]

        upslice = [psA[:, 0:512], psA[:, 512:1024]]
        dnpair = [psB, psC]

        b.emit("scalar", lambda e: e.dma_start(
            out=wupg[0], in_=wupb_h[8 * 128:9 * 128, :].rearrange(
                "p (jd h) -> p jd h", jd=ND)), inc="s_ig", n=16)
        b.emit("gpsimd", lambda e: e.dma_start(
            out=wdng[0], in_=wdnb_h[8 * 128:9 * 128, :].rearrange(
                "p (h d) -> p h d", h=NH)), inc="g_wb", n=16)
        b.emit("scalar", lambda e: e.dma_start(
            out=idxg_sb, in_=idx_h.rearrange("(i p) v -> p i v", p=128)),
            inc="s_mf1", n=16)
        ig_final = b.cnt["s_ig"]
        ig2_final = b.cnt["g_wb"]
        mf1_final = b.cnt["s_mf1"]

        tm_up_done = {}
        tm_dn_done = {}
        tm_tr_done = {}
        vm_hsq = {}
        vm_ev = {}
        up_bank_free = {}
        dn_bank_free = {}
        sh_store = 0

        def moe_up(k, wsrc, rhs_buf, width, gwait=None):
            if gwait:
                b.wait("tensor", gwait[0], gwait[1])
            for h in range(NH):
                bank = h % 2
                b.wait_g("tensor", up_bank_free.get(bank))
                for j in range(ND):
                    fn = (lambda e, o=upslice[bank][:, 0:width],
                          l=wsrc[:, j, h * 128:(h + 1) * 128],
                          r_=rhs_buf[:, j, 0:width], st=(j == 0),
                          sp=(j == ND - 1):
                          e.matmul(o, l, r_, start=st, stop=sp))
                    if j == ND - 1:
                        m_up = b.emit("tensor", fn, inc="tm")
                    else:
                        b.emit("tensor", fn)
                if h == NH - 1:
                    tm_up_done[k] = m_up
                b.wait("scalar", "tm", m_up)
                prev_rbf = (k, h - 2) if h >= 2 else (k - 1, NH - 2 + h)
                b.wait("scalar", "vm", vm_hsq.get(prev_rbf))
                m_rl = b.emit(
                    "scalar", lambda e, o=rbf[h % 2][:, 0:width],
                    s=upslice[bank][:, 0:width]:
                    e.activation(o, s, AF.Relu), w=[rbf[h % 2]], inc="sm")
                up_bank_free[bank] = ("sm", m_rl)
                b.wait("vector", "sm", m_rl)
                b.wait("vector", "tm", tm_dn_done.get((k - 1, TPC - 1)))
                vm_hsq[(k, h)] = b.emit(
                    "vector", lambda e, d=h_sb[:, h, 0:width],
                    s=rbf[h % 2][:, 0:width]:
                    e.tensor_mul(d, s, s), r=[rbf[h % 2]], w=[h_sb], inc="vm")

        def moe_down_mm(k, u2, wsrc):
            pr = dnpair[u2 % 2]
            b.wait("tensor", "vm", vm_ev.get(dn_bank_free.get(u2 % 2)))
            if u2 == 0:
                b.wait("tensor", "vm", vm_hsq[(k, NH - 1)])
            for h in range(NH):
                for hf in range(2):
                    fn = (lambda e, o=pr[:, hf * 512:(hf + 1) * 512],
                          l=h_sb[:, h, u2 * 128:(u2 + 1) * 128],
                          r_=wsrc[:, h, hf * 512:(hf + 1) * 512],
                          st=(h == 0), sp=(h == NH - 1):
                          e.matmul(o, l, r_, start=st, stop=sp))
                    if h == NH - 1 and hf == 1:
                        m_dn = b.emit("tensor", fn, inc="tm")
                    else:
                        b.emit("tensor", fn)
            tm_dn_done[(k, u2)] = m_dn
            dn_bank_free[u2 % 2] = (k, u2)
            return m_dn, pr

        KOFF = NCH

        def routed_gathers(j):
            if j >= NCNK:
                return
            k = KOFF + j
            jp = (j + 1) % 2
            if j >= 2:
                b.wait("gpsimd", "tm", tm_up_done.get(k - 2))
            elif j == 1:
                b.wait("gpsimd", "tm", tm_up_done.get(KOFF - 1))
            else:
                b.wait("gpsimd", "tm", tm_up_done.get(2))
            b.emit("gpsimd", lambda e,
                   o=wupg[jp].rearrange("p jd h -> p (jd h)"),
                   ix=idxgw_i[:, j:j + 1]:
                   e.indirect_dma_start(
                       out=o, out_offset=None, in_=wupb_h,
                       in_offset=IndirectOffsetOnAxis(ix, 0)),
                   inc=f"g_wg{jp}", n=16)
            if j >= 2:
                b.wait("gpsimd", "tm", tm_dn_done.get((k - 2, TPC - 1)))
            elif j == 1:
                b.wait("gpsimd", "tm", tm_dn_done.get((KOFF - 1, TPC - 1)))
            b.emit("gpsimd", lambda e,
                   o=wdng[jp].rearrange("p h d -> p (h d)"),
                   ix=idxgw_i[:, j:j + 1]:
                   e.indirect_dma_start(
                       out=o, out_offset=None, in_=wdnb_h,
                       in_offset=IndirectOffsetOnAxis(ix, 0)),
                   inc=f"g_wg{jp}", n=16)
            if j <= 1:
                b.wait("gpsimd", "s_mf1", mf1_final)
            b.wait("gpsimd", "tm", tm_tr_done.get(k - 2))
            for i in range(TPC):
                b.emit("gpsimd", lambda e, o=gbuf[jp][:, i, :],
                       ix=idxg_sb[:, j * TPC + i, 0:1]:
                       e.indirect_dma_start(
                           out=o, out_offset=None, in_=xn_h,
                           in_offset=IndirectOffsetOnAxis(ix, 0),
                           bounds_check=regs["sm1"], oob_is_err=False),
                       inc=f"g_tg{jp}", n=16)

        def xnm_load(ch):
            if ch >= NCH:
                return
            b.wait("sync", "tm", tm_up_done.get(ch - 2))
            b.emit("sync", lambda e, d=xnTm[ch % 2],
                   s=xnT_h[ch * 128:(ch + 1) * 128, :].rearrange(
                       "p (j t) -> p j t", j=ND):
                   e.dma_start(out=d, in_=s), inc=f"s_xnt{ch % 2}", n=16)

        xnm_load(0)
        for ch in range(NCH):
            k = ch
            xnm_load(ch + 1)
            if ch == NCH - 2:
                routed_gathers(0)
            b.wait("tensor", f"s_xnt{ch % 2}", 16 * (ch // 2 + 1))
            if ch == 0:
                b.wait("tensor", "s_ig", ig_final)
                b.wait("tensor", "g_wb", ig2_final)
            moe_up(k, wupg[0], xnTm[ch % 2], CH)
            for u2 in range(TPC):
                tt = ch * TPC + u2
                m_dn, pr = moe_down_mm(k, u2, wdng[0])
                b.wait("vector", "tm", m_dn)
                if sh_store >= 1:
                    b.wait("vector", "s_sh", 16 * sh_store)
                vm_ev[(k, u2)] = b.emit(
                    "vector", lambda e, s=pr: e.tensor_copy(osh[:], s),
                    w=[osh], inc="vm")
                b.wait("sync", "vm", vm_ev[(k, u2)])
                b.emit("sync", lambda e, d=moe_h[tt * 128:(tt + 1) * 128, :]:
                       e.dma_start(out=d, in_=osh[:]), inc="s_sh", n=16)
                sh_store += 1
        sh_final = 16 * sh_store

        routed_gathers(1)
        n_sca = 0
        for j in range(NCNK):
            k = KOFF + j
            jp = (j + 1) % 2
            b.wait("tensor", f"g_wg{jp}", 16 * 2 * (j // 2 + 1))
            b.wait("tensor", f"g_tg{jp}", 16 * TPC * (j // 2 + 1))
            for i in range(TPC):
                b.wait_g("tensor", ptg[i % 2])
                for jd in range(ND):
                    fn = (lambda e, o=pT[i % 2][:, jd * 128:(jd + 1) * 128],
                          i_=gbuf[jp][:, i, jd * 128:(jd + 1) * 128]:
                          e.transpose(o, i_, idb))
                    if jd == ND - 1:
                        m_tr = b.emit("tensor", fn, inc="tm")
                    else:
                        b.emit("tensor", fn)
                b.wait("scalar", "tm", m_tr)
                m_te = b.emit("scalar", lambda e,
                              d=gxT[:, :, i * 128:(i + 1) * 128],
                              s=pT[i % 2].rearrange("p (jd t) -> p jd t", jd=ND):
                              e.activation(d, s, AF.Identity), w=[gxT], inc="sm")
                ptg[i % 2] = ("sm", m_te)
                if i == TPC - 1:
                    tm_tr_done[k] = m_tr
            moe_up(k, wupg[jp], gxT, CH, gwait=("sm", m_te))
            for u2 in range(TPC):
                gt = j * TPC + u2
                m_dn, pr = moe_down_mm(k, u2, wdng[jp])
                b.wait("vector", "tm", m_dn)
                if j >= 1:
                    pk = (j - 1) * TPC + u2
                    b.wait("vector", f"g_sa{pk % 2}", 16 * (pk // 2 + 1))
                if j == 0 and u2 == 0:
                    b.wait("vector", "s_mf1", mf1_final)
                vm_ev[(k, u2)] = b.emit(
                    "vector", lambda e, d=obuf[:, u2, :], s=pr,
                    s2=idxg_sb[:, gt, 1:2].bitcast(F32):
                    e.tensor_scalar(d, s, s2, None, ALU.mult),
                    r=[idxg_sb], w=[obuf], inc="vm")
                b.wait("gpsimd", "vm", vm_ev[(k, u2)])
                if n_sca == 0:
                    b.wait("gpsimd", "s_sh", sh_final)
                else:
                    p_ = (n_sca - 1) % 2
                    b.wait("gpsimd", f"g_sa{p_}", 16 * ((n_sca - 1) // 2 + 1))
                b.emit("gpsimd", lambda e, ix=idxg_sb[:, gt, 0:1],
                       s=obuf[:, u2, :]:
                       e.indirect_dma_start(
                           out=moe_h, out_offset=IndirectOffsetOnAxis(ix, 0),
                           in_=s, in_offset=None,
                           bounds_check=regs["sm1"], oob_is_err=False,
                           compute_op=ALU.add),
                       inc=f"g_sa{n_sca % 2}", n=16)
                n_sca += 1
            routed_gathers(j + 2)
        sa0_final = b.cnt["g_sa0"]
        sa1_final = b.cnt["g_sa1"]

        asmst = contextlib.ExitStack()
        _as = asmst.enter_context
        xfin = [_as(nc.sbuf_tensor(f"xfin{i}", [128, D], F32))[:] for i in range(2)]
        afin = [_as(nc.sbuf_tensor(f"afin{i}", [128, D], BF16))[:] for i in range(2)]
        mfin = [_as(nc.sbuf_tensor(f"mfin{i}", [128, D], F32))[:] for i in range(2)]
        t1f = _as(nc.sbuf_tensor("t1f", [128, D], F32))[:]
        otile = [_as(nc.sbuf_tensor("otile0", [128, D], F32))[:], osh]
        b.wait("vector", "s_sh", sh_final)
        b.wait("sync", "g_sa0", sa0_final)
        b.wait("sync", "g_sa1", sa1_final)
        for tt in range(NT):
            b.wait("sync", "vm", vm_fin.get(tt - 2))
            b.emit("sync", lambda e, d=xfin[tt % 2],
                   s=x_d[tt * 128:(tt + 1) * 128, :]:
                   e.dma_start(out=d, in_=s), inc=f"s_fin{tt % 2}", n=16)
            b.emit("sync", lambda e, d=afin[tt % 2],
                   s=attn_h[tt * 128:(tt + 1) * 128, :]:
                   e.dma_start(out=d, in_=s), inc=f"s_fin{tt % 2}", n=16)
            b.emit("sync", lambda e, d=mfin[tt % 2],
                   s=moe_h[tt * 128:(tt + 1) * 128, :]:
                   e.dma_start(out=d, in_=s), inc=f"s_fin{tt % 2}", n=16)
            b.wait("vector", f"s_fin{tt % 2}", 48 * (tt // 2 + 1))
            if tt >= 2:
                b.wait("vector", f"s_out{tt % 2}", 16 * (tt // 2))
            b.emit("vector", lambda e, xf=xfin[tt % 2], af=afin[tt % 2]:
                   e.tensor_add(t1f[:], xf, af),
                   r=[xfin[tt % 2], afin[tt % 2]], w=[t1f])
            vm_fin[tt] = b.emit("vector", lambda e, d=otile[tt % 2],
                                m=mfin[tt % 2]: e.tensor_add(d, t1f[:], m),
                                r=[t1f, mfin[tt % 2]], w=[otile[tt % 2]],
                                inc="vm")
            b.wait("sync", "vm", vm_fin[tt])
            b.emit("sync", lambda e, d=out_d[tt * 128:(tt + 1) * 128, :],
                   s=otile[tt % 2]: e.dma_start(out=d, in_=s),
                   inc=f"s_out{tt % 2}", n=16)
        asmst.close()

        finals = dict(b.cnt)
        for s, cfin in finals.items():
            b.wait("sync", s, cfin)
        phc.close()

        with nc.Block() as blk:
            @blk.sync
            def _(e):
                for f in b.th["sync"]:
                    f(e)

            @blk.gpsimd
            def _(e):
                for f in b.th["gpsimd"]:
                    f(e)

            @blk.vector
            def _(e):
                for f in b.th["vector"]:
                    f(e)

            @blk.scalar
            def _(e):
                for f in b.th["scalar"]:
                    f(e)

            @blk.tensor
            def _(e):
                for f in b.th["tensor"]:
                    f(e)

    return nc


def stage_inputs(S, L0_attn_x, L0_attn_W_qkv, L0_attn_b_qkv, L0_attn_W_proj,
                 L0_attn_b_proj, L0_ffn_gate_w, L0_ffn_expert_bias, L0_ffn_w_up,
                 L0_ffn_w_down, L0_ffn_shared_w_up, L0_ffn_shared_w_down):
    asc = np.ascontiguousarray
    f32 = np.float32
    Wqkv = np.asarray(L0_attn_W_qkv, f32)
    bqkv = np.asarray(L0_attn_b_qkv, f32)
    shared = {
        "wqT": asc(Wqkv[0:D].T),
        "wkT": asc(Wqkv[D:2 * D].T),
        "wv_e": asc(Wqkv[2 * D:3 * D]),
        "wpT": asc(np.asarray(L0_attn_W_proj, f32).T),
        "bq_pe": asc(bqkv[0:D].reshape(ND, 128).T),
        "bk_pe": asc(bqkv[D:2 * D].reshape(ND, 128).T),
        "bv_pe": asc(bqkv[2 * D:3 * D].reshape(ND, 128).T),
        "bp_row": asc(np.asarray(L0_attn_b_proj, f32).reshape(1, D)),
        "gwT": asc(np.asarray(L0_ffn_gate_w, f32).T),
        "eb_row": asc(np.asarray(L0_ffn_expert_bias, f32).reshape(1, NEXP)),
        "wupT": asc(np.concatenate(
            [np.asarray(L0_ffn_w_up, f32).transpose(0, 2, 1),
             np.asarray(L0_ffn_shared_w_up, f32).T[None]], axis=0
        ).reshape(9 * D, HID)),
        "wdnT": asc(np.concatenate(
            [np.asarray(L0_ffn_w_down, f32).transpose(0, 2, 1),
             np.asarray(L0_ffn_shared_w_down, f32).T[None]], axis=0
        ).reshape(9 * HID, D)),
        "identb": np.eye(128).astype(ml_dtypes.bfloat16),
    }
    x = np.asarray(L0_attn_x, f32)
    in_maps = []
    for c in range(x.shape[0]):
        m = dict(shared)
        m["x"] = asc(x[c][:S])
        in_maps.append(m)
    return in_maps


def kernel(**inputs):
    S = 4096
    in_maps = stage_inputs(S, **inputs)
    nc = build_nc(S)
    res = run_bass_kernel_spmd(nc, in_maps, core_ids=list(range(8)))
    out = np.stack([np.asarray(res.results[c]["out"]) for c in range(8)], axis=0)
    return out.astype(np.float32)


# version 26 (final)
# speedup vs baseline: 1.2402x; 1.0263x over previous
import contextlib
import numpy as np
import ml_dtypes
import concourse.bass as bass
import concourse.mybir as mybir
from concourse.bass_utils import run_bass_kernel_spmd
from concourse.bass import IndirectOffsetOnAxis

F32 = mybir.dt.float32
BF16 = mybir.dt.bfloat16
I32 = mybir.dt.int32
AF = mybir.ActivationFunctionType
ALU = mybir.AluOpType
AX = mybir.AxisListType

D = 1024
HID = 1408
ND = 8
NH = 11
NEXP = 8
CH = 512
KC = 1024
EPS = 1e-5
SM_SCALE = 1.0 / 32.0

ENGS = ("sync", "gpsimd", "vector", "scalar", "tensor")


def _names(aps):
    out = set()
    for a in aps:
        t = getattr(a, "tensor", None)
        out.add(t.name if t is not None else str(a))
    return out


class B:

    def __init__(self, nc):
        self.nc = nc
        self.th = {e: [] for e in ENGS}
        self.cnt = {}
        self.handles = {}
        self.dw = {"vector": set(), "scalar": set()}
        self.dr = {"vector": set(), "scalar": set()}

    def sem(self, name, stack):
        self.handles[name] = stack.enter_context(self.nc.semaphore(name))
        self.cnt[name] = 0

    def wait(self, eng, sem, val):
        if val is None or val <= 0:
            return
        h = self.handles[sem]
        self.th[eng].append(lambda e, h=h, v=val: e.wait_ge(h, v))

    def wait_g(self, eng, guard):
        if guard:
            self.wait(eng, guard[0], guard[1])

    def emit(self, eng, fn, r=(), w=(), inc=None, n=1):
        if eng in self.dw:
            nr, nw = _names(r), _names(w)
            dw, dr = self.dw[eng], self.dr[eng]
            if (nr & dw) or (nw & (dw | dr)):
                self.th[eng].append(lambda e: e.drain())
                dw.clear()
                dr.clear()
            dw |= nw
            dr |= nr
        if inc is not None:
            self.cnt[inc] += n
            mark = self.cnt[inc]
            h = self.handles[inc]
            self.th[eng].append(
                lambda e, fn=fn, h=h, k=n: fn(e).then_inc(h, k))
            return mark
        self.th[eng].append(fn)
        return None

    def drain_all(self):
        for e in self.dw:
            self.th[e].append(lambda eng: eng.drain())
            self.dw[e].clear()
            self.dr[e].clear()

    def barrier(self):
        self.drain_all()
        snap = dict(self.cnt)
        for eng in ENGS:
            for s, c in snap.items():
                self.wait(eng, s, c)


def build_nc(S=4096):
    NT = S // 128
    NCH = S // CH
    TPC = CH // 128
    NKC = S // KC
    NCNK = S // 256 + 8
    CAPR = NCNK * 512
    NGT = CAPR // 128

    nc = bass.Bass("TRN2", target_bir_lowering=False)

    dt_ = nc.dram_tensor
    x_d = dt_("x", [S, D], F32, kind="ExternalInput").ap()
    wqT_d = dt_("wqT", [D, D], F32, kind="ExternalInput").ap()
    wkT_d = dt_("wkT", [D, D], F32, kind="ExternalInput").ap()
    wve_d = dt_("wv_e", [D, D], F32, kind="ExternalInput").ap()
    wpT_d = dt_("wpT", [D, D], F32, kind="ExternalInput").ap()
    bq_d = dt_("bq_pe", [128, ND], F32, kind="ExternalInput").ap()
    bk_d = dt_("bk_pe", [128, ND], F32, kind="ExternalInput").ap()
    bv_d = dt_("bv_pe", [128, ND], F32, kind="ExternalInput").ap()
    bp_d = dt_("bp_row", [1, D], F32, kind="ExternalInput").ap()
    gw_d = dt_("gwT", [D, NEXP], F32, kind="ExternalInput").ap()
    eb_d = dt_("eb_row", [1, NEXP], F32, kind="ExternalInput").ap()
    wup_d = dt_("wupT", [9 * D, HID], F32, kind="ExternalInput").ap()
    wdn_d = dt_("wdnT", [9 * HID, D], F32, kind="ExternalInput").ap()
    idb_d = dt_("identb", [128, 128], BF16, kind="ExternalInput").ap()

    out_d = dt_("out", [S, D], F32, kind="ExternalOutput").ap()

    xnT_h = dt_("xnT_h", [NCH * 128, ND * CH], BF16, kind="Internal").ap()
    kT_h = dt_("kT_h", [NCH * 128, ND * CH], BF16, kind="Internal").ap()
    qT_h = dt_("qT_h", [NT * 128, ND * 128], BF16, kind="Internal").ap()
    attn_h = dt_("attn_h", [S, D], BF16, kind="Internal").ap()
    bprime_h = dt_("bprime_h", [1, D], F32, kind="Internal").ap()
    xn_h = dt_("xn_h", [S, D], BF16, kind="Internal").ap()
    wupb_h = dt_("wupb_h", [9 * 128, ND * HID], BF16, kind="Internal").ap()
    wdnb_h = dt_("wdnb_h", [9 * 128, NH * D], BF16, kind="Internal").ap()
    moe_h = dt_("moe_h", [S, D], F32, kind="Internal").ap()
    idx_h = dt_("idx_h", [NCNK * 512, 2], I32, kind="Internal").ap()
    scr_h = dt_("scr_h", [64, 256], F32, kind="Internal").ap()

    with contextlib.ExitStack() as top:
        b = B(nc)
        for s in ("s_x0", "s_x1", "s_x2", "s_x3", "s_sp0", "s_sp1", "s_kt", "s_qt0", "s_qt1",
                  "s_ast0", "s_ast1", "s_xnt0", "s_xnt1", "s_fin0", "s_fin1",
                  "s_out0", "s_out1", "s_c1", "s_bp", "g_wa", "g_wb", "g_wu",
                  "g_wd", "g_wc", "g_ix", "g_tg0", "g_tg1", "g_wg0", "g_wg1", "g_sa0", "g_sa1", "s_xh",
                  "s_sh", "s_ig", "s_mf0", "s_mf1", "g_ib", "gm", "tm", "vm", "sm"):
            b.sem(s, top)

        pstack = top.enter_context(contextlib.ExitStack())
        _p = pstack.enter_context
        idb = _p(nc.sbuf_tensor("idb", [128, 128], BF16))[:]
        bqc = _p(nc.sbuf_tensor("bqc", [128, ND], F32))[:]
        bqc32 = _p(nc.sbuf_tensor("bqc32", [128, ND], F32))[:]
        bkc = _p(nc.sbuf_tensor("bkc", [128, ND], F32))[:]
        bvc = _p(nc.sbuf_tensor("bvc", [128, ND], BF16))[:]
        bpr = _p(nc.sbuf_tensor("bpr", [1, D], F32))[:]
        ebc = _p(nc.sbuf_tensor("ebc", [128, NEXP], F32))[:]
        bprime = _p(nc.sbuf_tensor("bprime", [1, D], F32))[:]
        bpbc = _p(nc.sbuf_tensor("bpbc", [128, D], F32))[:]
        wc_sb = _p(nc.sbuf_tensor("wc_sb", [128, ND, D], BF16))[:]
        s2tok = _p(nc.sbuf_tensor("s2tok", [128, NT, NEXP], F32))[:]
        en_all = _p(nc.sbuf_tensor("en_all", [128, NT, NEXP], F32))[:]
        mask_all = _p(nc.sbuf_tensor("mask_all", [128, NT, NEXP], F32))[:]
        idxgw_i = _p(nc.sbuf_tensor("idxgw_i", [128, NCNK], I32))[:]

        psA = _p(nc.psum_tensor("psA", [128, 1024], F32))[:]
        psB = _p(nc.psum_tensor("psB", [128, 1024], F32))[:]
        psC = _p(nc.psum_tensor("psC", [128, 1024], F32))[:]
        pT = [_p(nc.psum_tensor(f"pT{i}", [128, 1024], BF16))[:]
              for i in range(2)]
        pairs = [psA, psB]

        def cst(dst, src):
            b.emit("sync", lambda e, d=dst, s=src: e.dma_start(out=d, in_=s),
                   inc="s_c1", n=16)
        cst(idb, idb_d)
        cst(bqc, bq_d)
        cst(bkc, bk_d)
        cst(bpr, bp_d)
        cst(ebc, eb_d.broadcast_to([128, NEXP]))
        c1_final = b.cnt["s_c1"]

        regs = {}
        b.emit("gpsimd", lambda e: regs.update(
            capr=e.to_reg(CAPR - 1), sm1=e.to_reg(S - 1)))

        def wcast(dst, src, sem="g_wa"):
            return b.emit("gpsimd",
                          lambda e, d=dst, s=src: e.dma_start(out=d, in_=s),
                          inc=sem, n=16)

        xn_stack = contextlib.ExitStack()
        xn = xn_stack.enter_context(nc.sbuf_tensor("xn", [128, NT, D], BF16))[:]

        ph0 = contextlib.ExitStack()
        wve_sb = ph0.enter_context(
            nc.sbuf_tensor("wve_sb", [128, ND, D], BF16))[:]
        wpt_sb = ph0.enter_context(
            nc.sbuf_tensor("wpt_sb", [128, ND, D], BF16))[:]
        wcast(wve_sb, wve_d.rearrange("(j p) e -> p j e", p=128), "g_wb")
        wcast(wpt_sb, wpT_d.rearrange("(j p) e -> p j e", p=128), "g_wb")
        wcast(bvc, bv_d, "g_wb")
        wv_final = b.cnt["g_wb"]
        ph0.close()

        pha = contextlib.ExitStack()
        _a = pha.enter_context
        qbuf = [_a(nc.sbuf_tensor(f"qbuf{i}", [128, TPC, ND * 128], BF16))[:]
                for i in range(2)]
        kbuf = [_a(nc.sbuf_tensor(f"kbuf{i}", [128, ND, CH], BF16))[:]
                for i in range(2)]
        xbuf = [_a(nc.sbuf_tensor(f"xbuf{i}", [128, D], F32))[:] for i in range(4)]
        stats = _a(nc.sbuf_tensor("stats", [128, 12], F32))[:]
        aggr = _a(nc.sbuf_tensor("aggr", [128, 2], F32))[:]
        lnv = [_a(nc.sbuf_tensor(f"lnv{i}", [128, 1], F32))[:] for i in range(2)]
        negm = [_a(nc.sbuf_tensor(f"negm{i}", [128, 1], F32))[:] for i in range(2)]
        rstd_s = _a(nc.sbuf_tensor("rstd_s", [128, 1], F32))[:]
        nmr_s = _a(nc.sbuf_tensor("nmr_s", [128, 1], F32))[:]
        xnTb = [_a(nc.sbuf_tensor(f"xnTb{i}", [128, ND, CH], BF16))[:]
                for i in range(2)]
        wk_sb = _a(nc.sbuf_tensor("wk_sb", [128, ND, D], BF16))[:]
        wq_sb = _a(nc.sbuf_tensor("wq_sb", [128, ND, D], BF16))[:]
        gw_sb = _a(nc.sbuf_tensor("gw_sb", [128, ND, NEXP], BF16))[:]
        sgt = _a(nc.sbuf_tensor("sgt", [128, NEXP], F32))[:]
        sg = _a(nc.sbuf_tensor("sg", [128, NEXP], F32))[:]
        bsd = _a(nc.sbuf_tensor("bsd", [128, NEXP], F32))[:]
        top8 = _a(nc.sbuf_tensor("top8", [128, NEXP], F32))[:]
        mskb = _a(nc.sbuf_tensor("mskb", [128, NEXP], F32))[:]
        smv = _a(nc.sbuf_tensor("smv", [128, NEXP], F32))[:]

        wcast(wk_sb, wkT_d.rearrange("(j p) e -> p j e", p=128))
        wcast(wq_sb, wqT_d.rearrange("(j p) e -> p j e", p=128))
        wcast(gw_sb, gw_d.rearrange("(j p) e -> p j e", p=128))
        gw_final = b.cnt["g_wa"]

        b.wait("vector", "s_c1", c1_final)
        b.emit("vector", lambda e: e.tensor_scalar_mul(bqc32[:], bqc[:], SM_SCALE),
               r=[bqc], w=[bqc32], inc="vm")

        kslice = [psA[:, 0:512], psA[:, 512:1024], psB[:, 0:512], psB[:, 512:1024]]
        qslice = [psC[:, 0:512], psC[:, 512:1024]]
        gslice = psC[:, 0:NEXP]

        bankg = {}
        trg = {}

        sm_xn = {}
        vm_ln = {}
        vm_xnt = {}
        tr_mm = {}
        gmm_d = {}
        en_ev = {}
        tm_smm = {}
        vm_fin = {}
        qmm_last = {}

        def stage_A(c):
            if c < 4:
                b.wait("sync", "s_c1", c1_final + 16 * c)
                b.emit("sync",
                       lambda e, d=xbuf[c % 4], s=x_d[c * 128:(c + 1) * 128, :]:
                       e.dma_start(out=d, in_=s), inc="s_c1", n=16)
            else:
                b.wait("gpsimd", "sm", sm_xn.get(c - 4))
                b.emit("gpsimd",
                       lambda e, d=xbuf[c % 4], s=x_d[c * 128:(c + 1) * 128, :]:
                       e.dma_start(out=d, in_=s), inc=f"s_x{c % 4}", n=16)

        def stage_B(c):
            if c < 4:
                b.wait("vector", "s_c1", c1_final + 16 * (c + 1))
            else:
                b.wait("vector", f"s_x{c % 4}", 16 * (c // 4))
            b.wait("vector", "sm", sm_xn.get(c - 2))
            b.emit("vector", lambda e, xb=xbuf[c % 4]:
                   e.bn_stats(stats[:, 0:6], xb[:, 0:512]),
                   r=[xbuf[c % 4]], w=[stats])
            b.emit("vector", lambda e, xb=xbuf[c % 4]:
                   e.bn_stats(stats[:, 6:12], xb[:, 512:1024]),
                   r=[xbuf[c % 4]], w=[stats])
            b.emit("vector", lambda e: e.bn_aggr(aggr[:], stats[:]),
                   r=[stats], w=[aggr])
            b.emit("vector", lambda e, d=lnv[c % 2]:
                   e.tensor_scalar_add(d, aggr[:, 1:2], EPS),
                   r=[aggr], w=[lnv[c % 2]])
            vm_ln[c] = b.emit("vector", lambda e, d=negm[c % 2]:
                              e.tensor_scalar_mul(d, aggr[:, 0:1], -1.0),
                              r=[aggr], w=[negm[c % 2]], inc="vm")

        def stage_C(c):
            b.wait("scalar", "vm", vm_ln[c])
            b.emit("scalar", lambda e, s=lnv[c % 2]:
                   e.activation(rstd_s[:], s, AF.Ln),
                   r=[lnv[c % 2]], w=[rstd_s])
            b.emit("scalar", lambda e: e.activation(rstd_s[:], rstd_s[:],
                                                    AF.Exp, scale=-0.5),
                   r=[rstd_s], w=[rstd_s])
            b.emit("scalar", lambda e, s=negm[c % 2]:
                   e.mul(nmr_s[:], s, rstd_s[:]),
                   r=[negm[c % 2], rstd_s], w=[nmr_s])
            sm_xn[c] = b.emit("scalar",
                              lambda e, xb=xbuf[c % 4], d=xn[:, c, :]:
                              e.activation(d, xb, AF.Identity, scale=rstd_s[:],
                                           bias=nmr_s[:]),
                              r=[xbuf[c % 4], rstd_s, nmr_s], w=[xn], inc="sm")

        def stage_D(c):
            b.wait("tensor", "s_c1", c1_final)
            b.wait("tensor", "sm", sm_xn[c])
            b.wait_g("tensor", trg.get(c % 2))
            for j in range(ND):
                fn = (lambda e, o=pT[c % 2][:, j * 128:(j + 1) * 128],
                      i_=xn[:, c, j * 128:(j + 1) * 128]:
                      e.transpose(o, i_, idb))
                if j == ND - 1:
                    tr_mm[c] = b.emit("tensor", fn, inc="tm")
                else:
                    b.emit("tensor", fn)

        def stage_E(c):
            ch, ci = c // TPC, c % TPC
            b.wait("vector", "tm", tr_mm[c])
            if ci == 0 and ch >= 2:
                b.wait("vector", f"s_sp{ch % 2}", 96 * ((ch - 2) // 2 + 1))
                b.wait("vector", "tm", qmm_last.get(ch - 2))
                b.wait("vector", "tm", gmm_d.get((ch - 2) * TPC + TPC - 1))
            m = b.emit("vector",
                       lambda e, d=xnTb[ch % 2][:, :, ci * 128:(ci + 1) * 128],
                       s=pT[c % 2].rearrange("p (j t) -> p j t", j=ND):
                       e.tensor_copy(d, s), w=[xnTb[ch % 2]], inc="vm")
            vm_xnt[c] = m
            trg[c % 2] = ("vm", m)

        def stage_F(c):
            ch, ci = c // TPC, c % TPC
            if c == 0:
                b.wait("tensor", "g_wa", gw_final)
            b.wait("tensor", "vm", vm_xnt[c])
            b.wait_g("tensor", bankg.get("q0"))
            b.wait_g("tensor", bankg.get("gate"))
            for j in range(ND):
                fn = (lambda e, l=xnTb[ch % 2][:, j, ci * 128:(ci + 1) * 128],
                      r_=gw_sb[:, j, :], st=(j == 0), sp=(j == ND - 1):
                      e.matmul(gslice, l, r_, start=st, stop=sp))
                if j == ND - 1:
                    gmm_d[c] = b.emit("tensor", fn, inc="tm")
                else:
                    b.emit("tensor", fn)

        def stage_G(c):
            b.wait("scalar", "tm", gmm_d[c])
            en_ev[c] = b.emit("scalar",
                              lambda e, d=en_all[:, c, :]:
                              e.activation(d, gslice, AF.Exp, scale=-1.0),
                              w=[en_all], inc="sm")
            bankg["gate"] = ("sm", en_ev[c])

        def stage_H(c):
            b.wait("vector", "sm", en_ev[c])
            b.emit("vector", lambda e, s=en_all[:, c, :]:
                   e.tensor_scalar_add(sgt[:], s, 1.0), r=[en_all], w=[sgt])
            b.emit("vector", lambda e: e.reciprocal(sg[:], sgt[:]),
                   r=[sgt], w=[sg])
            b.emit("vector", lambda e: e.tensor_add(bsd[:], sg[:], ebc[:]),
                   r=[sg, ebc], w=[bsd])
            b.emit("vector", lambda e: e.max(top8[:], bsd[:]),
                   r=[bsd], w=[top8])
            b.emit("vector", lambda e, d=mask_all[:, c, :]: e.tensor_scalar(
                d, bsd[:], top8[:, 1:2], None, ALU.is_ge),
                   r=[bsd, top8], w=[mask_all])
            b.emit("vector", lambda e, mk=mask_all[:, c, :]:
                   e.tensor_mul(smv[:], sg[:], mk),
                   r=[sg, mask_all], w=[smv])
            b.emit("vector", lambda e, d=s2tok[:, c, :]:
                   e.tensor_mul(d, smv[:], smv[:]),
                   r=[smv], w=[s2tok], inc="vm")

        def chunk_work(ch):
            if ch == 0:
                b.wait("tensor", "g_wa", gw_final)
            b.wait("tensor", "vm", vm_xnt[ch * TPC + TPC - 1])
            k_ev_last = None
            for j2 in range(ND):
                key = f"k{j2 % 4}"
                sl = kslice[j2 % 4]
                b.wait_g("tensor", bankg.get(key))
                for j in range(ND):
                    fn = (lambda e, o=sl,
                          l=wk_sb[:, j, j2 * 128:(j2 + 1) * 128],
                          r_=xnTb[ch % 2][:, j, :], st=(j == 0), sp=(j == ND - 1):
                          e.matmul(o, l, r_, start=st, stop=sp))
                    if j == ND - 1:
                        m_k = b.emit("tensor", fn, inc="tm")
                    else:
                        b.emit("tensor", fn)
                b.wait("scalar", "tm", m_k)
                if ch >= 2 and j2 == 0:
                    b.wait("scalar", f"s_sp{ch % 2}", 96 * ((ch - 2) // 2 + 1))
                m = b.emit("scalar",
                           lambda e, o=kbuf[ch % 2][:, j2, :], i_=sl,
                           bb=bkc[:, j2:j2 + 1]:
                           e.activation(o, i_, AF.Identity, bias=bb),
                           w=[kbuf[ch % 2]], inc="sm")
                bankg[key] = ("sm", m)
                if j2 == ND - 1:
                    k_ev_last = m
            b.wait("sync", "vm", vm_xnt[ch * TPC + TPC - 1])
            b.emit("sync", lambda e, d=xnT_h[ch * 128:(ch + 1) * 128, :],
                   s=xnTb[ch % 2].rearrange("p j t -> p (j t)"):
                   e.dma_start(out=d, in_=s), inc=f"s_sp{ch % 2}", n=16)
            b.emit("sync", lambda e,
                   d=xn_h[ch * CH:(ch + 1) * CH, :].rearrange(
                       "(c p) d -> p c d", p=128),
                   s=xn[:, ch * TPC:(ch + 1) * TPC, :]:
                   e.dma_start(out=d, in_=s), inc="s_xh", n=16)
            b.wait("sync", "sm", k_ev_last)
            b.emit("sync", lambda e, d=kT_h[ch * 128:(ch + 1) * 128, :],
                   s=kbuf[ch % 2].rearrange("p j t -> p (j t)"):
                   e.dma_start(out=d, in_=s), inc=f"s_sp{ch % 2}", n=16)
            q_ev_last = None
            for j2 in range(ND):
                key = f"q{j2 % 2}"
                sl = qslice[j2 % 2]
                b.wait_g("tensor", bankg.get(key))
                if j2 % 2 == 0:
                    b.wait_g("tensor", bankg.get("gate"))
                for j in range(ND):
                    fn = (lambda e, o=sl,
                          l=wq_sb[:, j, j2 * 128:(j2 + 1) * 128],
                          r_=xnTb[ch % 2][:, j, :], st=(j == 0), sp=(j == ND - 1):
                          e.matmul(o, l, r_, start=st, stop=sp))
                    if j == ND - 1:
                        m_q = b.emit("tensor", fn, inc="tm")
                    else:
                        b.emit("tensor", fn)
                if j2 == ND - 1:
                    qmm_last[ch] = m_q
                b.wait("scalar", "tm", m_q)
                if ch >= 2 and j2 == 0:
                    b.wait("scalar", f"s_sp{ch % 2}", 96 * ((ch - 2) // 2 + 1))
                m = b.emit("scalar",
                           lambda e,
                           o=qbuf[ch % 2][:, :, j2 * 128:(j2 + 1) * 128],
                           i_=sl.rearrange("p (u t) -> p u t", u=TPC),
                           bb=bqc32[:, j2:j2 + 1]:
                           e.activation(o, i_, AF.Identity, scale=SM_SCALE,
                                        bias=bb),
                           w=[qbuf[ch % 2]], inc="sm")
                bankg[key] = ("sm", m)
                q_ev_last = m
            b.wait("sync", "sm", q_ev_last)
            for u2 in range(TPC):
                t_ = ch * TPC + u2
                b.emit("sync", lambda e, d=qT_h[t_ * 128:(t_ + 1) * 128, :],
                       s=qbuf[ch % 2][:, u2, :]:
                       e.dma_start(out=d, in_=s), inc=f"s_sp{ch % 2}", n=16)

        wcg = {0: None, 1: None}
        ph0_state = {}

        def ph0_wcomb():
            b.wait("tensor", "g_wb", wv_final)
            for m in range(ND):
                wpr = [psB, psC][m % 2]
                b.wait_g("tensor", wcg[m % 2])
                for j in range(ND):
                    for hf in range(2):
                        fn = (lambda e, o=wpr[:, hf * 512:(hf + 1) * 512],
                              l=wve_sb[:, j, m * 128:(m + 1) * 128],
                              r_=wpt_sb[:, j, hf * 512:(hf + 1) * 512],
                              st=(j == 0), sp=(j == ND - 1):
                              e.matmul(o, l, r_, start=st, stop=sp))
                        if j == ND - 1 and hf == 1:
                            m_wc = b.emit("tensor", fn, inc="tm")
                        else:
                            b.emit("tensor", fn)
                b.wait("scalar", "tm", m_wc)
                m_wce = b.emit("scalar", lambda e, o=wc_sb[:, m, :], s=wpr:
                               e.activation(o, s, AF.Identity),
                               w=[wc_sb], inc="sm")
                wcg[m % 2] = ("sm", m_wce)
            b.wait_g("tensor", wcg[1])
            for j in range(ND):
                for hf in range(2):
                    fn = (lambda e, o=psC[0:1, hf * 512:(hf + 1) * 512],
                          l=bvc[:, j:j + 1],
                          r_=wpt_sb[:, j, hf * 512:(hf + 1) * 512],
                          st=(j == 0), sp=(j == ND - 1):
                          e.matmul(o, l, r_, start=st, stop=sp))
                    if j == ND - 1 and hf == 1:
                        ph0_state["m_bp"] = b.emit("tensor", fn, inc="tm")
                    else:
                        b.emit("tensor", fn)

        def ph0_finish():
            b.wait("vector", "tm", ph0_state["m_bp"])
            b.wait("vector", "s_c1", c1_final)
            m_bpr = b.emit("vector", lambda e: e.tensor_add(
                bprime[:], psC[0:1, :], bpr[:]),
                r=[bpr], w=[bprime], inc="vm")
            b.wait("sync", "vm", m_bpr)
            b.emit("sync", lambda e: e.dma_start(out=bprime_h, in_=bprime[:]),
                   inc="s_bp", n=16)
            b.wait("sync", "s_bp", 16)
            b.emit("sync", lambda e: e.dma_start(
                out=bpbc[:], in_=bprime_h.broadcast_to([128, D])),
                inc="s_bp", n=16)
            bankg["k2"] = bankg["k3"] = wcg[0]
            bankg["q0"] = bankg["q1"] = bankg["gate"] = ("vm", m_bpr)
        bp_final = 32

        for step in range(NT + 9):
            if step < NT:
                stage_A(step)
                stage_B(step)
            if step == 1:
                ph0_wcomb()
            if step == 4:
                ph0_finish()
            c = step - 1
            if 0 <= c < NT:
                stage_C(c)
            c = step - 2
            if 0 <= c < NT:
                stage_D(c)
                stage_E(c)
            c = step - 3
            if 0 <= c < NT:
                if c % TPC == TPC - 1:
                    chunk_work(c // TPC)
            c = step - 6
            if 0 <= c < NT:
                stage_F(c)
                stage_G(c)
            c = step - 8
            if 0 <= c < NT:
                stage_H(c)

        b.barrier()
        pha.close()

        phb = contextlib.ExitStack()
        _b2 = phb.enter_context
        kT = _b2(nc.sbuf_tensor("kT", [128, ND, S], BF16))[:]

        def precast_weights():
            for g9 in range(9):
                wcast(wupb_h[g9 * 128:(g9 + 1) * 128, :].rearrange(
                          "p (jd h) -> p jd h", jd=ND),
                      wup_d[g9 * D:(g9 + 1) * D, :].rearrange(
                          "(jd p) h -> p jd h", p=128), "g_wc")
                wcast(wdnb_h[g9 * 128:(g9 + 1) * 128, :].rearrange(
                          "p (h d) -> p h d", h=NH),
                      wdn_d[g9 * HID:(g9 + 1) * HID, :].rearrange(
                          "(h p) d -> p h d", p=128), "g_wc")

        for jq in range(ND):
            b.emit("sync" if jq < 4 else "scalar", lambda e, j_=jq: e.dma_start(
                out=kT.rearrange("p j (c t) -> p j c t", t=CH)[:, j_:j_ + 1],
                in_=kT_h.rearrange("(c p) (j t) -> p j c t",
                                   p=128, t=CH)[:, j_:j_ + 1]),
                inc="s_kt", n=16)
        kt_final = b.cnt["s_kt"]

        qT_ld = [_b2(nc.sbuf_tensor(f"qT_ld{i}", [128, ND * 128], BF16))[:]
                 for i in range(2)]
        P_sb = [_b2(nc.sbuf_tensor(f"P_sb{i}", [128, KC], BF16))[:]
                for i in range(2)]
        PT_sb = [_b2(nc.sbuf_tensor(f"PT_sb{i}", [128, 8, 128], BF16))[:]
                 for i in range(2)]
        oacc = _b2(nc.sbuf_tensor("oacc", [128, D], F32))[:]
        outp = _b2(nc.sbuf_tensor("outp", [128, D], BF16))[:]
        outpT = _b2(nc.sbuf_tensor("outpT", [128, ND, 128], BF16))[:]
        atile = [_b2(nc.sbuf_tensor(f"atile{i}", [128, D], BF16))[:]
                 for i in range(2)]
        zac2 = [_b2(nc.sbuf_tensor(f"zacc{i}", [128, 1], F32))[:]
                for i in range(2)]
        zc = _b2(nc.sbuf_tensor("zc", [128, 1], F32))[:]
        nmx = [_b2(nc.sbuf_tensor(f"nmx{i}", [128, 1], F32))[:] for i in range(4)]
        nmc = _b2(nc.sbuf_tensor("nmc", [128, 1], F32))[:]
        dal = [_b2(nc.sbuf_tensor(f"dal{i}", [128, 1], F32))[:] for i in range(2)]
        alp2 = [_b2(nc.sbuf_tensor(f"alp{i}", [128, 1], F32))[:]
                for i in range(4)]
        rz = _b2(nc.sbuf_tensor("rz", [128, 1], F32))[:]


        BIG = 65536.0
        ixs = contextlib.ExitStack()
        _ix = ixs.enter_context
        inclt2 = [_ix(nc.sbuf_tensor(f"inclt{i}", [128, NT], F32))[:]
                  for i in range(2)]
        inclt = inclt2[0]
        cnt_all = _ix(nc.sbuf_tensor("cnt_all", [128, NEXP], F32))[:]
        cntT = _ix(nc.sbuf_tensor("cntT", [NEXP, 128], F32))[:]
        c_incl = _ix(nc.sbuf_tensor("c_incl", [NEXP, 128], F32))[:]
        c_excl = _ix(nc.sbuf_tensor("c_excl", [NEXP, 128], F32))[:]
        tot8 = _ix(nc.sbuf_tensor("tot8", [NEXP, 1], F32))[:]
        ccnt = _ix(nc.sbuf_tensor("ccnt", [NEXP, 1], F32))[:]
        tfr = _ix(nc.sbuf_tensor("tfr", [NEXP, 1], F32))[:]
        cc_row = _ix(nc.sbuf_tensor("cc_row", [1, NEXP], F32))[:]
        ti_row = _ix(nc.sbuf_tensor("ti_row", [1, NEXP], F32))[:]
        S_row = _ix(nc.sbuf_tensor("S_row", [1, NEXP], F32))[:]
        S_col = _ix(nc.sbuf_tensor("S_col", [NEXP, 1], F32))[:]
        rbase = _ix(nc.sbuf_tensor("rbase", [NEXP, 1], F32))[:]
        off_pg = _ix(nc.sbuf_tensor("off_pg", [NEXP, 128], F32))[:]
        off_tok = _ix(nc.sbuf_tensor("off_tok", [128, NEXP], F32))[:]
        posb2 = [_ix(nc.sbuf_tensor(f"posb{i}", [128, NT], F32))[:]
                 for i in range(2)]
        pos22 = [_ix(nc.sbuf_tensor(f"pos2{i}", [128, NT], F32))[:]
                 for i in range(2)]
        posm2 = [_ix(nc.sbuf_tensor(f"posm{i}", [128, NT], I32))[:]
                 for i in range(2)]
        stg2 = [_ix(nc.sbuf_tensor(f"stg{i}", [128, NT, 2], I32))[:]
                for i in range(2)]
        iota_id = _ix(nc.sbuf_tensor("iota_id", [128, NT], I32))[:]
        jj8 = _ix(nc.sbuf_tensor("jj8", [NEXP, NCNK], I32))[:]
        jjf = _ix(nc.sbuf_tensor("jjf", [NEXP, NCNK], F32))[:]
        geb = _ix(nc.sbuf_tensor("geb", [NEXP, NCNK], F32))[:]
        ones8 = _ix(nc.sbuf_tensor("ones8", [NEXP, 1], F32))[:]
        expf = _ix(nc.sbuf_tensor("expf", [1, NCNK], F32))[:]
        expbc = _ix(nc.sbuf_tensor("expbc", [128, NCNK], F32))[:]
        iotap_i = _ix(nc.sbuf_tensor("iotap_i", [128, NCNK], I32))[:]
        iotap_f = _ix(nc.sbuf_tensor("iotap_f", [128, NCNK], F32))[:]
        idxgw_f = _ix(nc.sbuf_tensor("idxgw_f", [128, NCNK], F32))[:]
        init_sb = _ix(nc.sbuf_tensor("init_sb", [128, CAPR * 2 // 128], I32))[:]

        m_ones = b.emit("vector", lambda e: e.memset(ones8, 1.0),
                        w=[ones8], inc="vm")
        m_init = b.emit("vector", lambda e: e.memset(init_sb, S),
                        w=[init_sb], inc="vm")
        b.wait("gpsimd", "vm", m_init)
        b.emit("gpsimd", lambda e: e.iota(iota_id, pattern=[[128, NT]], base=0,
                                          channel_multiplier=1))
        b.emit("gpsimd", lambda e: e.iota(jj8, pattern=[[1, NCNK]], base=0,
                                          channel_multiplier=0))
        m_iot = b.emit("gpsimd", lambda e: e.iota(
            iotap_i, pattern=[[0, NCNK]], base=0, channel_multiplier=1),
            inc="gm")
        b.emit("gpsimd", lambda e: e.dma_start(out=idx_h, in_=init_sb),
               inc="g_ix", n=16)
        ix_init = b.cnt["g_ix"]
        b.wait("vector", "gm", m_iot)
        b.emit("vector", lambda e: e.tensor_copy(jjf, jj8), r=[jj8], w=[jjf])
        b.emit("vector", lambda e: e.tensor_copy(iotap_f, iotap_i),
               r=[iotap_i], w=[iotap_f])
        for g in range(NEXP):
            b.emit("vector", lambda e, mk=mask_all[:, :, g]:
                   e.tensor_tensor_scan(inclt, mk, mk, 0.0, ALU.add, ALU.bypass),
                   r=[mask_all], w=[inclt])
            b.emit("vector", lambda e, d=cnt_all[:, g:g + 1]:
                   e.tensor_copy(d, inclt[:, NT - 1:NT]),
                   r=[inclt], w=[cnt_all])
        m1 = b.emit("vector", lambda e: e.engine_nop(), inc="vm")
        b.wait("gpsimd", "vm", m1)
        b.emit("gpsimd", lambda e: e.dma_start(
            out=scr_h[0:NEXP, 0:128].rearrange("g p -> p g"), in_=cnt_all,
            allow_slow_non_contiguous=True),
            inc="g_ix", n=16)
        b.wait("gpsimd", "g_ix", b.cnt["g_ix"])
        b.emit("gpsimd", lambda e: e.dma_start(out=cntT, in_=scr_h[0:NEXP, 0:128]),
               inc="g_ix", n=16)
        b.wait("vector", "g_ix", b.cnt["g_ix"])
        b.emit("vector", lambda e: e.tensor_tensor_scan(
            c_incl, cntT, cntT, 0.0, ALU.add, ALU.bypass), r=[cntT], w=[c_incl])
        b.emit("vector", lambda e: e.tensor_sub(c_excl, c_incl, cntT),
               r=[c_incl, cntT], w=[c_excl])
        b.emit("vector", lambda e: e.tensor_copy(tot8, c_incl[:, 127:128]),
               r=[c_incl], w=[tot8])
        b.emit("vector", lambda e: e.tensor_scalar_add(ccnt, tot8, 511.0),
               r=[tot8], w=[ccnt])
        b.emit("vector", lambda e: e.tensor_scalar_mul(ccnt, ccnt, 1.0 / 512.0),
               r=[ccnt], w=[ccnt])
        b.emit("vector", lambda e: e.tensor_scalar_add(tfr, ccnt, 8388608.0),
               r=[ccnt], w=[tfr])
        b.emit("vector", lambda e: e.tensor_scalar_add(tfr, tfr, -8388608.0),
               r=[tfr], w=[tfr])
        b.emit("vector", lambda e: e.tensor_tensor(tot8, tfr, ccnt, ALU.is_gt),
               r=[tfr, ccnt], w=[tot8])
        m2 = b.emit("vector", lambda e: e.tensor_sub(ccnt, tfr, tot8),
                    r=[tfr, tot8], w=[ccnt], inc="vm")
        b.wait("gpsimd", "vm", m2)
        b.emit("gpsimd", lambda e: e.dma_start(
            out=scr_h[8:9, 0:NEXP].rearrange("o g -> g o"), in_=ccnt,
            allow_slow_non_contiguous=True),
            inc="g_ix", n=16)
        b.wait("gpsimd", "g_ix", b.cnt["g_ix"])
        b.emit("gpsimd", lambda e: e.dma_start(out=cc_row, in_=scr_h[8:9, 0:NEXP]),
               inc="g_ix", n=16)
        b.wait("vector", "g_ix", b.cnt["g_ix"])
        b.emit("vector", lambda e: e.tensor_tensor_scan(
            ti_row, cc_row, cc_row, 0.0, ALU.add, ALU.bypass),
            r=[cc_row], w=[ti_row])
        m3 = b.emit("vector", lambda e: e.tensor_sub(S_row, ti_row, cc_row),
                    r=[ti_row, cc_row], w=[S_row], inc="vm")
        b.wait("gpsimd", "vm", m3)
        b.emit("gpsimd", lambda e: e.dma_start(
            out=scr_h[9:10, 0:NEXP], in_=S_row), inc="g_ix", n=16)
        b.wait("gpsimd", "g_ix", b.cnt["g_ix"])
        b.emit("gpsimd", lambda e: e.dma_start(
            out=S_col, in_=scr_h[9:10, 0:NEXP].rearrange("o g -> g o"),
            allow_slow_non_contiguous=True),
            inc="g_ix", n=16)
        b.wait("vector", "g_ix", b.cnt["g_ix"])
        b.emit("vector", lambda e: e.tensor_scalar_mul(rbase, S_col, 512.0),
               r=[S_col], w=[rbase])
        m4 = b.emit("vector", lambda e: e.tensor_scalar(
            off_pg, c_excl, rbase, None, ALU.add),
            r=[c_excl, rbase], w=[off_pg], inc="vm")
        b.wait("gpsimd", "vm", m4)
        b.emit("gpsimd", lambda e: e.dma_start(
            out=scr_h[16:16 + NEXP, 0:128], in_=off_pg), inc="g_ix", n=16)
        b.wait("gpsimd", "g_ix", b.cnt["g_ix"])
        b.emit("gpsimd", lambda e: e.dma_start(
            out=off_tok, in_=scr_h[16:16 + NEXP, 0:128].rearrange("g p -> p g"),
            allow_slow_non_contiguous=True),
            inc="g_ix", n=16)
        ix_pre = b.cnt["g_ix"]

        def ix_expert(g):
            gp = g % 2
            inc_, pb, p2_, pm, sg2 = (inclt2[gp], posb2[gp], pos22[gp],
                                      posm2[gp], stg2[gp])
            if g == 0:
                b.wait("vector", "g_ix", ix_pre)
            b.emit("vector", lambda e, mk=mask_all[:, :, g]:
                   e.tensor_tensor_scan(inc_, mk, mk, 0.0, ALU.add, ALU.bypass),
                   r=[mask_all], w=[inc_])
            b.emit("vector", lambda e, mk=mask_all[:, :, g]:
                   e.scalar_tensor_tensor(pb, mk, -1.0, inc_, ALU.mult, ALU.add),
                   r=[mask_all, inc_], w=[pb])
            b.emit("vector", lambda e, oc=off_tok[:, g:g + 1]:
                   e.tensor_scalar(pb, pb, oc, None, ALU.add),
                   r=[pb, off_tok], w=[pb])
            b.emit("vector", lambda e: e.tensor_scalar_add(p2_, pb, BIG),
                   r=[pb], w=[p2_])
            b.emit("vector", lambda e, mk=mask_all[:, :, g]:
                   e.scalar_tensor_tensor(pb, mk, -BIG, p2_, ALU.mult, ALU.add),
                   r=[mask_all, p2_], w=[pb])
            b.emit("vector", lambda e: e.tensor_copy(pm, pb),
                   r=[pb], w=[pm])
            b.emit("vector", lambda e: e.tensor_copy(
                sg2[:, :, 0:1].rearrange("p t o -> p (t o)"), iota_id),
                r=[iota_id], w=[sg2])
            mg = b.emit("vector", lambda e, s2=s2tok[:, :, g:g + 1].bitcast(I32):
                        e.tensor_copy(sg2[:, :, 1:2].rearrange("p t o -> p (t o)"),
                                      s2.rearrange("p t o -> p (t o)")),
                        r=[s2tok], w=[sg2], inc="vm")
            ix_mg[g] = mg

        def ix_scatters(g, c0, c1):
            gp = g % 2
            pm, sg2 = posm2[gp], stg2[gp]
            b.wait("gpsimd", "vm", ix_mg[g])
            b.wait("gpsimd", "g_ix", b.cnt["g_ix"])
            for c in range(c0, c1):
                b.emit("gpsimd", lambda e, po=pm[:, c:c + 1],
                       sg_=sg2[:, c, :]:
                       e.indirect_dma_start(
                           out=idx_h, out_offset=IndirectOffsetOnAxis(po, 0),
                           in_=sg_, in_offset=None,
                           bounds_check=regs["capr"], oob_is_err=False),
                       inc="g_ix", n=16)
            return b.cnt["g_ix"]

        ix_scat = {}
        ix_mg = {}

        pairg = {0: None, 1: None}
        avg_g = {"c": None}
        ptg = {0: None, 1: None}
        sm_exp = {}
        sm_pt = {}
        m_s_d = {}
        m_av_d = {}

        def p2_scores_mm(t, u):
            seq = t * NKC + u
            pr = pairs[seq % 2]
            b.wait("tensor", f"s_qt{t % 2}", 16 * (t // 2 + 1))
            if t == 0 and u == 0:
                b.wait("tensor", "s_kt", kt_final)
            b.wait_g("tensor", pairg[seq % 2])
            for j in range(ND):
                for hf in range(2):
                    fn = (lambda e, o=pr[:, hf * 512:(hf + 1) * 512],
                          l=qT_ld[t % 2][:, j * 128:(j + 1) * 128],
                          r_=kT[:, j, u * KC + hf * 512:u * KC + (hf + 1) * 512],
                          st=(j == 0), sp=(j == ND - 1):
                          e.matmul(o, l, r_, start=st, stop=sp))
                    if j == ND - 1 and hf == 1:
                        m_s = b.emit("tensor", fn, inc="tm")
                    else:
                        b.emit("tensor", fn)
            if u == NKC - 1:
                tm_smm[t] = m_s
            m_s_d[(t, u)] = m_s

        def p2_scores_max(t, u):
            seq = t * NKC + u
            pr = pairs[seq % 2]
            b.wait("vector", "tm", m_s_d[(t, u)])
            b.emit("vector", lambda e, s=pr: e.reduce_max(
                nmc[:], s, axis=AX.X, negate=True), w=[nmc])
            if u == 0:
                m_mx = b.emit("vector", lambda e, d=nmx[seq % 4]:
                              e.tensor_copy(d, nmc[:]), r=[nmc],
                              w=[nmx[seq % 4]], inc="vm")
            else:
                b.emit("vector", lambda e, d=nmx[seq % 4], a=nmx[(seq - 1) % 4]:
                       e.tensor_tensor(d, a, nmc[:], ALU.min),
                       r=[nmx[(seq - 1) % 4], nmc], w=[nmx[seq % 4]])
                m_mx = b.emit("vector", lambda e, a=nmx[(seq - 1) % 4],
                              n_=nmx[seq % 4], dd=dal[seq % 2]:
                              e.tensor_sub(dd, n_, a),
                              r=[nmx[seq % 4], nmx[(seq - 1) % 4]],
                              w=[dal[seq % 2]], inc="vm")
            return m_mx

        def p2_scores_exp(t, u, m_mx):
            seq = t * NKC + u
            pr = pairs[seq % 2]
            al = alp2[seq % 4]
            b.wait("scalar", "vm", m_mx)
            if u > 0:
                b.emit("scalar", lambda e, a=al, dd=dal[seq % 2]:
                       e.activation(a, dd, AF.Exp), r=[dal[seq % 2]], w=[al])
            sm_exp[(t, u)] = b.emit(
                "scalar", lambda e, o=P_sb[seq % 2], s=pr, n_=nmx[seq % 4]:
                e.activation(o, s, AF.Exp, bias=n_, accum_out=zc[:]),
                r=[nmx[seq % 4]], w=[P_sb[seq % 2], zc], inc="sm")
            pairg[seq % 2] = ("sm", sm_exp[(t, u)])

        def p2_scores_z(t, u):
            seq = t * NKC + u
            al = alp2[seq % 4]
            za = zac2[t % 2]
            b.wait("vector", "sm", sm_exp[(t, u)])
            if u == 0:
                b.emit("vector", lambda e, z=za: e.tensor_copy(z, zc[:]),
                       r=[zc], w=[za])
            else:
                b.emit("vector", lambda e, z=za, a=al: e.scalar_tensor_tensor(
                    z, z, a, zc[:], ALU.mult, ALU.add),
                    r=[za, al, zc], w=[za])

        def p2_pt_tr(t, u):
            seq = t * NKC + u
            b.wait("tensor", "sm", sm_exp[(t, u)])
            b.wait_g("tensor", ptg[seq % 2])
            for i in range(8):
                fn = (lambda e, o=pT[seq % 2][:, i * 128:(i + 1) * 128],
                      i_=P_sb[seq % 2][:, i * 128:(i + 1) * 128]:
                      e.transpose(o, i_, idb))
                if i == 7:
                    m_pt = b.emit("tensor", fn, inc="tm")
                else:
                    b.emit("tensor", fn)
            return m_pt

        def p2_pt_evac(t, u, m_pt):
            seq = t * NKC + u
            b.wait("scalar", "tm", m_pt)
            sm_pt[(t, u)] = b.emit(
                "scalar", lambda e, d=PT_sb[seq % 2],
                s=pT[seq % 2].rearrange("p (i t) -> p i t", i=8):
                e.activation(d, s, AF.Identity), w=[PT_sb[seq % 2]], inc="sm")
            ptg[seq % 2] = ("sm", sm_pt[(t, u)])

        def p2_av(t, u):
            seq = t * NKC + u
            b.wait("tensor", "sm", sm_pt[(t, u)])
            b.wait_g("tensor", avg_g["c"])
            for i in range(8):
                for hf in range(2):
                    fn = (lambda e, o=psC[:, hf * 512:(hf + 1) * 512],
                          l=PT_sb[seq % 2][:, i, :],
                          r_=xn[:, u * 8 + i, hf * 512:(hf + 1) * 512],
                          st=(i == 0), sp=(i == 7):
                          e.matmul(o, l, r_, start=st, stop=sp))
                    if i == 7 and hf == 1:
                        m_av = b.emit("tensor", fn, inc="tm")
                    else:
                        b.emit("tensor", fn)
            m_av_d[(t, u)] = m_av

        def p2_oacc(t, u):
            seq = t * NKC + u
            al = alp2[seq % 4]
            b.wait("vector", "tm", m_av_d[(t, u)])
            if u == 0:
                m_oe = b.emit("vector", lambda e: e.tensor_copy(oacc[:], psC[:]),
                              w=[oacc], inc="vm")
            else:
                m_oe = b.emit("vector", lambda e, a=al: e.scalar_tensor_tensor(
                    oacc[:], oacc[:], a, psC[:], ALU.mult, ALU.add),
                    r=[oacc, al], w=[oacc], inc="vm")
            avg_g["c"] = ("vm", m_oe)

        def qt_load(t):
            if t >= NT:
                return
            b.wait("sync", "tm", tm_smm.get(t - 2))
            b.emit("sync", lambda e, d=qT_ld[t % 2],
                   s=qT_h[t * 128:(t + 1) * 128, :]:
                   e.dma_start(out=d, in_=s), inc=f"s_qt{t % 2}", n=16)

        b.wait("vector", "vm", m_ones)
        mge = b.emit("vector", lambda e: e.tensor_scalar(
            geb, jjf, S_col, None, ALU.is_ge), r=[jjf, S_col], w=[geb], inc="vm")
        b.wait("tensor", "vm", mge)
        b.wait_g("tensor", avg_g["c"])
        m_ec = b.emit("tensor", lambda e: e.matmul(
            psC[0:1, 0:NCNK], ones8, geb, start=True, stop=True), inc="tm")
        b.wait("vector", "tm", m_ec)
        m_ec2 = b.emit("vector", lambda e: e.tensor_scalar_add(
            expf, psC[0:1, 0:NCNK], -1.0), w=[expf], inc="vm")
        avg_g["c"] = ("vm", m_ec2)
        b.wait("gpsimd", "vm", m_ec2)
        b.emit("gpsimd", lambda e: e.dma_start(
            out=scr_h[10:11, 0:NCNK], in_=expf), inc="g_ib", n=16)
        b.wait("gpsimd", "g_ib", b.cnt["g_ib"])
        b.emit("gpsimd", lambda e: e.dma_start(
            out=expbc, in_=scr_h[10:11, 0:NCNK].broadcast_to([128, NCNK])),
            inc="g_ib", n=16)
        b.wait("vector", "g_ib", b.cnt["g_ib"])
        b.emit("vector", lambda e: e.scalar_tensor_tensor(
            idxgw_f, expbc, 128.0, iotap_f, ALU.mult, ALU.add),
            r=[expbc, iotap_f], w=[idxgw_f])
        m_wix = b.emit("vector", lambda e: e.tensor_copy(idxgw_i, idxgw_f),
                       r=[idxgw_f], w=[idxgw_i], inc="vm")


        qt_load(0)
        qt_load(1)
        for bseq in range(2):
            bt, bu = divmod(bseq, NKC)
            p2_scores_mm(bt, bu)
            m_mxb = p2_scores_max(bt, bu)
            p2_scores_exp(bt, bu, m_mxb)
            p2_scores_z(bt, bu)
        IXSPL = 2 if NT >= 2 * NEXP else 1

        def tail_v(t):
            b.emit("vector", lambda e, z=zac2[t % 2]: e.reciprocal(rz[:], z),
                   r=[zac2[t % 2]], w=[rz])
            return b.emit("vector", lambda e: e.tensor_scalar(
                outp[:], oacc[:], rz[:], None, ALU.mult),
                r=[oacc, rz], w=[outp], inc="vm")

        def tail_pe(t, m_op):
            seqlast = t * NKC + NKC - 1
            b.wait("tensor", "vm", m_op)
            b.wait_g("tensor", ptg[seqlast % 2])
            for j in range(ND):
                fn = (lambda e, o=pT[seqlast % 2][:, j * 128:(j + 1) * 128],
                      i_=outp[:, j * 128:(j + 1) * 128]: e.transpose(o, i_, idb))
                if j == ND - 1:
                    m_t1 = b.emit("tensor", fn, inc="tm")
                else:
                    b.emit("tensor", fn)
            b.wait("scalar", "tm", m_t1)
            m_ev1 = b.emit("scalar", lambda e, s=pT[seqlast % 2].rearrange(
                "p (j t) -> p j t", j=ND):
                e.activation(outpT[:], s, AF.Identity), w=[outpT], inc="sm")
            ptg[seqlast % 2] = ("sm", m_ev1)
            b.wait("tensor", "sm", m_ev1)
            b.wait_g("tensor", avg_g["c"])
            for j in range(ND):
                for hf in range(2):
                    fn = (lambda e, o=psC[:, hf * 512:(hf + 1) * 512],
                          l=outpT[:, j, :],
                          r_=wc_sb[:, j, hf * 512:(hf + 1) * 512],
                          st=(j == 0), sp=(j == ND - 1):
                          e.matmul(o, l, r_, start=st, stop=sp))
                    if j == ND - 1 and hf == 1:
                        m_pj = b.emit("tensor", fn, inc="tm")
                    else:
                        b.emit("tensor", fn)
            b.wait("vector", "tm", m_pj)
            b.wait("vector", "s_bp", bp_final)
            if t >= 2:
                b.wait("vector", f"s_ast{t % 2}", 16 * (t // 2))
            m_at = b.emit("vector", lambda e, d=atile[t % 2]:
                          e.tensor_add(d, psC[:], bpbc[:]),
                          r=[bpbc], w=[atile[t % 2]], inc="vm")
            avg_g["c"] = ("vm", m_at)
            b.wait("sync", "vm", m_at)
            b.emit("sync", lambda e, d=attn_h[t * 128:(t + 1) * 128, :],
                   s=atile[t % 2]: e.dma_start(out=d, in_=s),
                   inc=f"s_ast{t % 2}", n=16)

        for t in range(NT):
            if t < IXSPL * NEXP:
                g, h = divmod(t, IXSPL)
                if h == 0:
                    if g >= 2:
                        b.wait("vector", "g_ix", ix_scat[g - 2])
                    ix_expert(g)
                nsc = -(-NT // IXSPL)
                ix_scat[g] = ix_scatters(g, h * nsc, min((h + 1) * nsc, NT))
                if t == IXSPL * NEXP - 1:
                    precast_weights()
            for u in range(NKC):
                m_pt = p2_pt_tr(t, u)
                p2_pt_evac(t, u, m_pt)
                nseq = t * NKC + u + 2
                nxt = (nseq // NKC, nseq % NKC) if nseq < NT * NKC else None
                if nxt:
                    p2_scores_mm(*nxt)
                    m_mx = p2_scores_max(*nxt)
                p2_av(t, u)
                if u == 0 and t >= 1:
                    m_opd = tail_v(t - 1)
                if nxt:
                    p2_scores_exp(*nxt, m_mx)
                    p2_scores_z(*nxt)
                p2_oacc(t, u)
                if u == 0 and t >= 1:
                    tail_pe(t - 1, m_opd)
            qt_load(t + 2)
        m_opd = tail_v(NT - 1)
        tail_pe(NT - 1, m_opd)

        b.barrier()
        ixs.close()
        phb.close()
        xn_stack.close()

        phc = contextlib.ExitStack()
        _c = phc.enter_context
        wupg = [_c(nc.sbuf_tensor(f"wupg{i}", [128, ND, HID], BF16))[:]
                for i in range(2)]
        wdng = [_c(nc.sbuf_tensor(f"wdng{i}", [128, NH, D], BF16))[:]
                for i in range(2)]
        xnTm_u = _c(nc.sbuf_tensor("xnTm_u", [128, 2 * ND * CH], BF16))[:]
        xnTm = [xnTm_u[:, i * ND * CH:(i + 1) * ND * CH].rearrange(
            "p (j t) -> p j t", j=ND) for i in range(2)]
        h_sb = _c(nc.sbuf_tensor("h_sb", [128, NH, CH], BF16))[:]
        rbf = [_c(nc.sbuf_tensor(f"rbf{i}", [128, CH], BF16))[:] for i in range(2)]
        osh = _c(nc.sbuf_tensor("osh", [128, D], F32))[:]
        gbuf = [_c(nc.sbuf_tensor(f"gbuf{i}", [128, TPC, D], BF16))[:]
                for i in range(2)]
        gxT = _c(nc.sbuf_tensor("gxT", [128, ND, CH], BF16))[:]
        obuf = xnTm_u.bitcast(F32).rearrange("p (u d) -> p u d", u=TPC)
        idxg_sb = _c(nc.sbuf_tensor("idxg_sb", [128, NGT, 2], I32))[:]

        upslice = [psA[:, 0:512], psA[:, 512:1024]]
        dnpair = [psB, psC]

        b.emit("scalar", lambda e: e.dma_start(
            out=wupg[0], in_=wupb_h[8 * 128:9 * 128, :].rearrange(
                "p (jd h) -> p jd h", jd=ND)), inc="s_ig", n=16)
        b.emit("gpsimd", lambda e: e.dma_start(
            out=wdng[0], in_=wdnb_h[8 * 128:9 * 128, :].rearrange(
                "p (h d) -> p h d", h=NH)), inc="g_wb", n=16)
        b.emit("scalar", lambda e: e.dma_start(
            out=idxg_sb, in_=idx_h.rearrange("(i p) v -> p i v", p=128)),
            inc="s_mf1", n=16)
        ig_final = b.cnt["s_ig"]
        ig2_final = b.cnt["g_wb"]
        mf1_final = b.cnt["s_mf1"]

        tm_up_done = {}
        tm_dn_done = {}
        tm_tr_done = {}
        vm_hsq = {}
        vm_ev = {}
        up_bank_free = {}
        dn_bank_free = {}
        sh_store = 0

        def moe_up(k, wsrc, rhs_buf, width, gwait=None):
            if gwait:
                b.wait("tensor", gwait[0], gwait[1])
            for h in range(NH):
                bank = h % 2
                b.wait_g("tensor", up_bank_free.get(bank))
                for j in range(ND):
                    fn = (lambda e, o=upslice[bank][:, 0:width],
                          l=wsrc[:, j, h * 128:(h + 1) * 128],
                          r_=rhs_buf[:, j, 0:width], st=(j == 0),
                          sp=(j == ND - 1):
                          e.matmul(o, l, r_, start=st, stop=sp))
                    if j == ND - 1:
                        m_up = b.emit("tensor", fn, inc="tm")
                    else:
                        b.emit("tensor", fn)
                if h == NH - 1:
                    tm_up_done[k] = m_up
                b.wait("scalar", "tm", m_up)
                prev_rbf = (k, h - 2) if h >= 2 else (k - 1, NH - 2 + h)
                b.wait("scalar", "vm", vm_hsq.get(prev_rbf))
                m_rl = b.emit(
                    "scalar", lambda e, o=rbf[h % 2][:, 0:width],
                    s=upslice[bank][:, 0:width]:
                    e.activation(o, s, AF.Relu), w=[rbf[h % 2]], inc="sm")
                up_bank_free[bank] = ("sm", m_rl)
                b.wait("vector", "sm", m_rl)
                b.wait("vector", "tm", tm_dn_done.get((k - 1, TPC - 1)))
                vm_hsq[(k, h)] = b.emit(
                    "vector", lambda e, d=h_sb[:, h, 0:width],
                    s=rbf[h % 2][:, 0:width]:
                    e.tensor_mul(d, s, s), r=[rbf[h % 2]], w=[h_sb], inc="vm")

        def moe_down_mm(k, u2, wsrc):
            pr = dnpair[u2 % 2]
            b.wait("tensor", "vm", vm_ev.get(dn_bank_free.get(u2 % 2)))
            if u2 == 0:
                b.wait("tensor", "vm", vm_hsq[(k, NH - 1)])
            for h in range(NH):
                for hf in range(2):
                    fn = (lambda e, o=pr[:, hf * 512:(hf + 1) * 512],
                          l=h_sb[:, h, u2 * 128:(u2 + 1) * 128],
                          r_=wsrc[:, h, hf * 512:(hf + 1) * 512],
                          st=(h == 0), sp=(h == NH - 1):
                          e.matmul(o, l, r_, start=st, stop=sp))
                    if h == NH - 1 and hf == 1:
                        m_dn = b.emit("tensor", fn, inc="tm")
                    else:
                        b.emit("tensor", fn)
            tm_dn_done[(k, u2)] = m_dn
            dn_bank_free[u2 % 2] = (k, u2)
            return m_dn, pr

        KOFF = NCH

        def routed_gathers(j):
            if j >= NCNK:
                return
            k = KOFF + j
            jp = (j + 1) % 2
            if j >= 2:
                b.wait("gpsimd", "tm", tm_up_done.get(k - 2))
            elif j == 1:
                b.wait("gpsimd", "tm", tm_up_done.get(KOFF - 1))
            else:
                b.wait("gpsimd", "tm", tm_up_done.get(2))
            b.emit("gpsimd", lambda e,
                   o=wupg[jp].rearrange("p jd h -> p (jd h)"),
                   ix=idxgw_i[:, j:j + 1]:
                   e.indirect_dma_start(
                       out=o, out_offset=None, in_=wupb_h,
                       in_offset=IndirectOffsetOnAxis(ix, 0)),
                   inc=f"g_wg{jp}", n=16)
            if j >= 2:
                b.wait("gpsimd", "tm", tm_dn_done.get((k - 2, TPC - 1)))
            elif j == 1:
                b.wait("gpsimd", "tm", tm_dn_done.get((KOFF - 1, TPC - 1)))
            b.emit("gpsimd", lambda e,
                   o=wdng[jp].rearrange("p h d -> p (h d)"),
                   ix=idxgw_i[:, j:j + 1]:
                   e.indirect_dma_start(
                       out=o, out_offset=None, in_=wdnb_h,
                       in_offset=IndirectOffsetOnAxis(ix, 0)),
                   inc=f"g_wg{jp}", n=16)
            if j <= 1:
                b.wait("gpsimd", "s_mf1", mf1_final)
            b.wait("gpsimd", "tm", tm_tr_done.get(k - 2))
            for i in range(TPC):
                b.emit("gpsimd", lambda e, o=gbuf[jp][:, i, :],
                       ix=idxg_sb[:, j * TPC + i, 0:1]:
                       e.indirect_dma_start(
                           out=o, out_offset=None, in_=xn_h,
                           in_offset=IndirectOffsetOnAxis(ix, 0),
                           bounds_check=regs["sm1"], oob_is_err=False),
                       inc=f"g_tg{jp}", n=16)

        def xnm_load(ch):
            if ch >= NCH:
                return
            b.wait("sync", "tm", tm_up_done.get(ch - 2))
            b.emit("sync", lambda e, d=xnTm[ch % 2],
                   s=xnT_h[ch * 128:(ch + 1) * 128, :].rearrange(
                       "p (j t) -> p j t", j=ND):
                   e.dma_start(out=d, in_=s), inc=f"s_xnt{ch % 2}", n=16)

        xnm_load(0)
        for ch in range(NCH):
            k = ch
            xnm_load(ch + 1)
            if ch == NCH - 2:
                routed_gathers(0)
            b.wait("tensor", f"s_xnt{ch % 2}", 16 * (ch // 2 + 1))
            if ch == 0:
                b.wait("tensor", "s_ig", ig_final)
                b.wait("tensor", "g_wb", ig2_final)
            moe_up(k, wupg[0], xnTm[ch % 2], CH)
            for u2 in range(TPC):
                tt = ch * TPC + u2
                m_dn, pr = moe_down_mm(k, u2, wdng[0])
                b.wait("vector", "tm", m_dn)
                if sh_store >= 1:
                    b.wait("vector", "s_sh", 16 * sh_store)
                vm_ev[(k, u2)] = b.emit(
                    "vector", lambda e, s=pr: e.tensor_copy(osh[:], s),
                    w=[osh], inc="vm")
                b.wait("sync", "vm", vm_ev[(k, u2)])
                b.emit("sync", lambda e, d=moe_h[tt * 128:(tt + 1) * 128, :]:
                       e.dma_start(out=d, in_=osh[:]), inc="s_sh", n=16)
                sh_store += 1
        sh_final = 16 * sh_store

        routed_gathers(1)
        n_sca = 0
        for j in range(NCNK):
            k = KOFF + j
            jp = (j + 1) % 2
            b.wait("tensor", f"g_wg{jp}", 16 * 2 * (j // 2 + 1))
            b.wait("tensor", f"g_tg{jp}", 16 * TPC * (j // 2 + 1))
            for i in range(TPC):
                b.wait_g("tensor", ptg[i % 2])
                for jd in range(ND):
                    fn = (lambda e, o=pT[i % 2][:, jd * 128:(jd + 1) * 128],
                          i_=gbuf[jp][:, i, jd * 128:(jd + 1) * 128]:
                          e.transpose(o, i_, idb))
                    if jd == ND - 1:
                        m_tr = b.emit("tensor", fn, inc="tm")
                    else:
                        b.emit("tensor", fn)
                b.wait("scalar", "tm", m_tr)
                m_te = b.emit("scalar", lambda e,
                              d=gxT[:, :, i * 128:(i + 1) * 128],
                              s=pT[i % 2].rearrange("p (jd t) -> p jd t", jd=ND):
                              e.activation(d, s, AF.Identity), w=[gxT], inc="sm")
                ptg[i % 2] = ("sm", m_te)
                if i == TPC - 1:
                    tm_tr_done[k] = m_tr
            moe_up(k, wupg[jp], gxT, CH, gwait=("sm", m_te))
            for u2 in range(TPC):
                gt = j * TPC + u2
                m_dn, pr = moe_down_mm(k, u2, wdng[jp])
                b.wait("vector", "tm", m_dn)
                if j >= 1:
                    pk = (j - 1) * TPC + u2
                    b.wait("vector", f"g_sa{pk % 2}", 16 * (pk // 2 + 1))
                if j == 0 and u2 == 0:
                    b.wait("vector", "s_mf1", mf1_final)
                vm_ev[(k, u2)] = b.emit(
                    "vector", lambda e, d=obuf[:, u2, :], s=pr,
                    s2=idxg_sb[:, gt, 1:2].bitcast(F32):
                    e.tensor_scalar(d, s, s2, None, ALU.mult),
                    r=[idxg_sb], w=[obuf], inc="vm")
                b.wait("gpsimd", "vm", vm_ev[(k, u2)])
                if n_sca == 0:
                    b.wait("gpsimd", "s_sh", sh_final)
                else:
                    p_ = (n_sca - 1) % 2
                    b.wait("gpsimd", f"g_sa{p_}", 16 * ((n_sca - 1) // 2 + 1))
                b.emit("gpsimd", lambda e, ix=idxg_sb[:, gt, 0:1],
                       s=obuf[:, u2, :]:
                       e.indirect_dma_start(
                           out=moe_h, out_offset=IndirectOffsetOnAxis(ix, 0),
                           in_=s, in_offset=None,
                           bounds_check=regs["sm1"], oob_is_err=False,
                           compute_op=ALU.add),
                       inc=f"g_sa{n_sca % 2}", n=16)
                n_sca += 1
            routed_gathers(j + 2)
        sa0_final = b.cnt["g_sa0"]
        sa1_final = b.cnt["g_sa1"]

        asmst = contextlib.ExitStack()
        _as = asmst.enter_context
        xfin = [_as(nc.sbuf_tensor(f"xfin{i}", [128, D], F32))[:] for i in range(2)]
        afin = [_as(nc.sbuf_tensor(f"afin{i}", [128, D], BF16))[:] for i in range(2)]
        mfin = [_as(nc.sbuf_tensor(f"mfin{i}", [128, D], F32))[:] for i in range(2)]
        t1f = _as(nc.sbuf_tensor("t1f", [128, D], F32))[:]
        otile = [_as(nc.sbuf_tensor("otile0", [128, D], F32))[:], osh]
        b.wait("vector", "s_sh", sh_final)
        b.wait("sync", "g_sa0", sa0_final)
        b.wait("sync", "g_sa1", sa1_final)
        for tt in range(NT):
            b.wait("sync", "vm", vm_fin.get(tt - 2))
            b.emit("sync", lambda e, d=xfin[tt % 2],
                   s=x_d[tt * 128:(tt + 1) * 128, :]:
                   e.dma_start(out=d, in_=s), inc=f"s_fin{tt % 2}", n=16)
            b.emit("sync", lambda e, d=afin[tt % 2],
                   s=attn_h[tt * 128:(tt + 1) * 128, :]:
                   e.dma_start(out=d, in_=s), inc=f"s_fin{tt % 2}", n=16)
            b.emit("sync", lambda e, d=mfin[tt % 2],
                   s=moe_h[tt * 128:(tt + 1) * 128, :]:
                   e.dma_start(out=d, in_=s), inc=f"s_fin{tt % 2}", n=16)
            b.wait("vector", f"s_fin{tt % 2}", 48 * (tt // 2 + 1))
            if tt >= 2:
                b.wait("vector", f"s_out{tt % 2}", 16 * (tt // 2))
            b.emit("vector", lambda e, xf=xfin[tt % 2], af=afin[tt % 2]:
                   e.tensor_add(t1f[:], xf, af),
                   r=[xfin[tt % 2], afin[tt % 2]], w=[t1f])
            vm_fin[tt] = b.emit("vector", lambda e, d=otile[tt % 2],
                                m=mfin[tt % 2]: e.tensor_add(d, t1f[:], m),
                                r=[t1f, mfin[tt % 2]], w=[otile[tt % 2]],
                                inc="vm")
            b.wait("sync", "vm", vm_fin[tt])
            b.emit("sync", lambda e, d=out_d[tt * 128:(tt + 1) * 128, :],
                   s=otile[tt % 2]: e.dma_start(out=d, in_=s),
                   inc=f"s_out{tt % 2}", n=16)
        asmst.close()

        finals = dict(b.cnt)
        for s, cfin in finals.items():
            b.wait("sync", s, cfin)
        phc.close()

        with nc.Block() as blk:
            @blk.sync
            def _(e):
                for f in b.th["sync"]:
                    f(e)

            @blk.gpsimd
            def _(e):
                for f in b.th["gpsimd"]:
                    f(e)

            @blk.vector
            def _(e):
                for f in b.th["vector"]:
                    f(e)

            @blk.scalar
            def _(e):
                for f in b.th["scalar"]:
                    f(e)

            @blk.tensor
            def _(e):
                for f in b.th["tensor"]:
                    f(e)

    return nc


def stage_inputs(S, L0_attn_x, L0_attn_W_qkv, L0_attn_b_qkv, L0_attn_W_proj,
                 L0_attn_b_proj, L0_ffn_gate_w, L0_ffn_expert_bias, L0_ffn_w_up,
                 L0_ffn_w_down, L0_ffn_shared_w_up, L0_ffn_shared_w_down):
    asc = np.ascontiguousarray
    f32 = np.float32
    Wqkv = np.asarray(L0_attn_W_qkv, f32)
    bqkv = np.asarray(L0_attn_b_qkv, f32)
    shared = {
        "wqT": asc(Wqkv[0:D].T),
        "wkT": asc(Wqkv[D:2 * D].T),
        "wv_e": asc(Wqkv[2 * D:3 * D]),
        "wpT": asc(np.asarray(L0_attn_W_proj, f32).T),
        "bq_pe": asc(bqkv[0:D].reshape(ND, 128).T),
        "bk_pe": asc(bqkv[D:2 * D].reshape(ND, 128).T),
        "bv_pe": asc(bqkv[2 * D:3 * D].reshape(ND, 128).T),
        "bp_row": asc(np.asarray(L0_attn_b_proj, f32).reshape(1, D)),
        "gwT": asc(np.asarray(L0_ffn_gate_w, f32).T),
        "eb_row": asc(np.asarray(L0_ffn_expert_bias, f32).reshape(1, NEXP)),
        "wupT": asc(np.concatenate(
            [np.asarray(L0_ffn_w_up, f32).transpose(0, 2, 1),
             np.asarray(L0_ffn_shared_w_up, f32).T[None]], axis=0
        ).reshape(9 * D, HID)),
        "wdnT": asc(np.concatenate(
            [np.asarray(L0_ffn_w_down, f32).transpose(0, 2, 1),
             np.asarray(L0_ffn_shared_w_down, f32).T[None]], axis=0
        ).reshape(9 * HID, D)),
        "identb": np.eye(128).astype(ml_dtypes.bfloat16),
    }
    x = np.asarray(L0_attn_x, f32)
    in_maps = []
    for c in range(x.shape[0]):
        m = dict(shared)
        m["x"] = asc(x[c][:S])
        in_maps.append(m)
    return in_maps


def kernel(**inputs):
    S = 4096
    in_maps = stage_inputs(S, **inputs)
    nc = build_nc(S)
    res = run_bass_kernel_spmd(nc, in_maps, core_ids=list(range(8)))
    out = np.stack([np.asarray(res.results[c]["out"]) for c in range(8)], axis=0)
    return out.astype(np.float32)


# version 27
# speedup vs baseline: 1.2425x; 1.0019x over previous
import contextlib
import numpy as np
import ml_dtypes
import concourse.bass as bass
import concourse.mybir as mybir
from concourse.bass_utils import run_bass_kernel_spmd
from concourse.bass import IndirectOffsetOnAxis

F32 = mybir.dt.float32
BF16 = mybir.dt.bfloat16
I32 = mybir.dt.int32
AF = mybir.ActivationFunctionType
ALU = mybir.AluOpType
AX = mybir.AxisListType

D = 1024
HID = 1408
ND = 8
NH = 11
NEXP = 8
CH = 512
KC = 1024
EPS = 1e-5
SM_SCALE = 1.0 / 32.0

ENGS = ("sync", "gpsimd", "vector", "scalar", "tensor")


def _names(aps):
    out = set()
    for a in aps:
        t = getattr(a, "tensor", None)
        out.add(t.name if t is not None else str(a))
    return out


class B:

    def __init__(self, nc):
        self.nc = nc
        self.th = {e: [] for e in ENGS}
        self.cnt = {}
        self.handles = {}
        self.dw = {"vector": set(), "scalar": set()}
        self.dr = {"vector": set(), "scalar": set()}

    def sem(self, name, stack):
        self.handles[name] = stack.enter_context(self.nc.semaphore(name))
        self.cnt[name] = 0

    def wait(self, eng, sem, val):
        if val is None or val <= 0:
            return
        h = self.handles[sem]
        self.th[eng].append(lambda e, h=h, v=val: e.wait_ge(h, v))

    def wait_g(self, eng, guard):
        if guard:
            self.wait(eng, guard[0], guard[1])

    def emit(self, eng, fn, r=(), w=(), inc=None, n=1):
        if eng in self.dw:
            nr, nw = _names(r), _names(w)
            dw, dr = self.dw[eng], self.dr[eng]
            if (nr & dw) or (nw & (dw | dr)):
                self.th[eng].append(lambda e: e.drain())
                dw.clear()
                dr.clear()
            dw |= nw
            dr |= nr
        if inc is not None:
            self.cnt[inc] += n
            mark = self.cnt[inc]
            h = self.handles[inc]
            self.th[eng].append(
                lambda e, fn=fn, h=h, k=n: fn(e).then_inc(h, k))
            return mark
        self.th[eng].append(fn)
        return None

    def drain_all(self):
        for e in self.dw:
            self.th[e].append(lambda eng: eng.drain())
            self.dw[e].clear()
            self.dr[e].clear()

    def barrier(self):
        self.drain_all()
        snap = dict(self.cnt)
        for eng in ENGS:
            for s, c in snap.items():
                self.wait(eng, s, c)


def build_nc(S=4096):
    NT = S // 128
    NCH = S // CH
    TPC = CH // 128
    NKC = S // KC
    NCNK = S // 256 + 8
    CAPR = NCNK * 512
    NGT = CAPR // 128

    nc = bass.Bass("TRN2", target_bir_lowering=False)

    dt_ = nc.dram_tensor
    x_d = dt_("x", [S, D], F32, kind="ExternalInput").ap()
    wqT_d = dt_("wqT", [D, D], F32, kind="ExternalInput").ap()
    wkT_d = dt_("wkT", [D, D], F32, kind="ExternalInput").ap()
    wve_d = dt_("wv_e", [D, D], F32, kind="ExternalInput").ap()
    wpT_d = dt_("wpT", [D, D], F32, kind="ExternalInput").ap()
    bq_d = dt_("bq_pe", [128, ND], F32, kind="ExternalInput").ap()
    bk_d = dt_("bk_pe", [128, ND], F32, kind="ExternalInput").ap()
    bv_d = dt_("bv_pe", [128, ND], F32, kind="ExternalInput").ap()
    bp_d = dt_("bp_row", [1, D], F32, kind="ExternalInput").ap()
    gw_d = dt_("gwT", [D, NEXP], F32, kind="ExternalInput").ap()
    eb_d = dt_("eb_row", [1, NEXP], F32, kind="ExternalInput").ap()
    wup_d = dt_("wupT", [9 * D, HID], F32, kind="ExternalInput").ap()
    wdn_d = dt_("wdnT", [9 * HID, D], F32, kind="ExternalInput").ap()
    idb_d = dt_("identb", [128, 128], BF16, kind="ExternalInput").ap()

    out_d = dt_("out", [S, D], F32, kind="ExternalOutput").ap()

    xnT_h = dt_("xnT_h", [NCH * 128, ND * CH], BF16, kind="Internal").ap()
    kT_h = dt_("kT_h", [NCH * 128, ND * CH], BF16, kind="Internal").ap()
    qT_h = dt_("qT_h", [NT * 128, ND * 128], BF16, kind="Internal").ap()
    attn_h = dt_("attn_h", [S, D], BF16, kind="Internal").ap()
    bprime_h = dt_("bprime_h", [1, D], F32, kind="Internal").ap()
    xn_h = dt_("xn_h", [S, D], BF16, kind="Internal").ap()
    wupb_h = dt_("wupb_h", [9 * 128, ND * HID], BF16, kind="Internal").ap()
    wdnb_h = dt_("wdnb_h", [9 * 128, NH * D], BF16, kind="Internal").ap()
    moe_h = dt_("moe_h", [S, D], F32, kind="Internal").ap()
    idx_h = dt_("idx_h", [NCNK * 512, 2], I32, kind="Internal").ap()
    scr_h = dt_("scr_h", [64, 256], F32, kind="Internal").ap()

    with contextlib.ExitStack() as top:
        b = B(nc)
        for s in ("s_x0", "s_x1", "s_x2", "s_x3", "s_sp0", "s_sp1", "s_kt", "s_qt0", "s_qt1",
                  "s_ast0", "s_ast1", "s_xnt0", "s_xnt1", "s_fin0", "s_fin1",
                  "s_out0", "s_out1", "s_c1", "s_bp", "g_wa", "g_wb", "g_wu",
                  "g_wd", "g_wc", "g_ix", "g_tg0", "g_tg1", "g_wg0", "g_wg1", "g_sa0", "g_sa1", "s_xh",
                  "s_sh", "s_ig", "s_mf0", "s_mf1", "g_ib", "gm", "tm", "vm", "sm"):
            b.sem(s, top)

        pstack = top.enter_context(contextlib.ExitStack())
        _p = pstack.enter_context
        idb = _p(nc.sbuf_tensor("idb", [128, 128], BF16))[:]
        bqc = _p(nc.sbuf_tensor("bqc", [128, ND], F32))[:]
        bqc32 = _p(nc.sbuf_tensor("bqc32", [128, ND], F32))[:]
        bkc = _p(nc.sbuf_tensor("bkc", [128, ND], F32))[:]
        bvc = _p(nc.sbuf_tensor("bvc", [128, ND], BF16))[:]
        bpr = _p(nc.sbuf_tensor("bpr", [1, D], F32))[:]
        ebc = _p(nc.sbuf_tensor("ebc", [128, NEXP], F32))[:]
        bprime = _p(nc.sbuf_tensor("bprime", [1, D], F32))[:]
        bpbc = _p(nc.sbuf_tensor("bpbc", [128, D], F32))[:]
        wc_sb = _p(nc.sbuf_tensor("wc_sb", [128, ND, D], BF16))[:]
        s2tok = _p(nc.sbuf_tensor("s2tok", [128, NT, NEXP], F32))[:]
        en_all = _p(nc.sbuf_tensor("en_all", [128, NT, NEXP], F32))[:]
        mask_all = _p(nc.sbuf_tensor("mask_all", [128, NT, NEXP], F32))[:]
        idxgw_i = _p(nc.sbuf_tensor("idxgw_i", [128, NCNK], I32))[:]

        psA = _p(nc.psum_tensor("psA", [128, 1024], F32))[:]
        psB = _p(nc.psum_tensor("psB", [128, 1024], F32))[:]
        psC = _p(nc.psum_tensor("psC", [128, 1024], F32))[:]
        pT = [_p(nc.psum_tensor(f"pT{i}", [128, 1024], BF16))[:]
              for i in range(2)]
        pairs = [psA, psB]

        def cst(dst, src):
            b.emit("sync", lambda e, d=dst, s=src: e.dma_start(out=d, in_=s),
                   inc="s_c1", n=16)
        cst(idb, idb_d)
        cst(bqc, bq_d)
        cst(bkc, bk_d)
        cst(bpr, bp_d)
        cst(ebc, eb_d.broadcast_to([128, NEXP]))
        c1_final = b.cnt["s_c1"]

        regs = {}
        b.emit("gpsimd", lambda e: regs.update(
            capr=e.to_reg(CAPR - 1), sm1=e.to_reg(S - 1)))

        def wcast(dst, src, sem="g_wa"):
            return b.emit("gpsimd",
                          lambda e, d=dst, s=src: e.dma_start(out=d, in_=s),
                          inc=sem, n=16)

        xn_stack = contextlib.ExitStack()
        xn = xn_stack.enter_context(nc.sbuf_tensor("xn", [128, NT, D], BF16))[:]

        ph0 = contextlib.ExitStack()
        wve_sb = ph0.enter_context(
            nc.sbuf_tensor("wve_sb", [128, ND, D], BF16))[:]
        wpt_sb = ph0.enter_context(
            nc.sbuf_tensor("wpt_sb", [128, ND, D], BF16))[:]
        wcast(wve_sb, wve_d.rearrange("(j p) e -> p j e", p=128), "g_wb")
        wcast(wpt_sb, wpT_d.rearrange("(j p) e -> p j e", p=128), "g_wb")
        wcast(bvc, bv_d, "g_wb")
        wv_final = b.cnt["g_wb"]
        ph0.close()

        pha = contextlib.ExitStack()
        _a = pha.enter_context
        qbuf = [_a(nc.sbuf_tensor(f"qbuf{i}", [128, TPC, ND * 128], BF16))[:]
                for i in range(2)]
        kbuf = [_a(nc.sbuf_tensor(f"kbuf{i}", [128, ND, CH], BF16))[:]
                for i in range(2)]
        xbuf = [_a(nc.sbuf_tensor(f"xbuf{i}", [128, D], F32))[:] for i in range(4)]
        stats = _a(nc.sbuf_tensor("stats", [128, 12], F32))[:]
        aggr = _a(nc.sbuf_tensor("aggr", [128, 2], F32))[:]
        lnv = [_a(nc.sbuf_tensor(f"lnv{i}", [128, 1], F32))[:] for i in range(2)]
        negm = [_a(nc.sbuf_tensor(f"negm{i}", [128, 1], F32))[:] for i in range(2)]
        rstd_s = _a(nc.sbuf_tensor("rstd_s", [128, 1], F32))[:]
        nmr_s = _a(nc.sbuf_tensor("nmr_s", [128, 1], F32))[:]
        xnTb = [_a(nc.sbuf_tensor(f"xnTb{i}", [128, ND, CH], BF16))[:]
                for i in range(2)]
        wk_sb = _a(nc.sbuf_tensor("wk_sb", [128, ND, D], BF16))[:]
        wq_sb = _a(nc.sbuf_tensor("wq_sb", [128, ND, D], BF16))[:]
        gw_sb = _a(nc.sbuf_tensor("gw_sb", [128, ND, NEXP], BF16))[:]
        sgt = _a(nc.sbuf_tensor("sgt", [128, NEXP], F32))[:]
        sg = _a(nc.sbuf_tensor("sg", [128, NEXP], F32))[:]
        bsd = _a(nc.sbuf_tensor("bsd", [128, NEXP], F32))[:]
        top8 = _a(nc.sbuf_tensor("top8", [128, NEXP], F32))[:]
        mskb = _a(nc.sbuf_tensor("mskb", [128, NEXP], F32))[:]
        smv = _a(nc.sbuf_tensor("smv", [128, NEXP], F32))[:]

        wcast(wk_sb, wkT_d.rearrange("(j p) e -> p j e", p=128))
        wcast(wq_sb, wqT_d.rearrange("(j p) e -> p j e", p=128))
        wcast(gw_sb, gw_d.rearrange("(j p) e -> p j e", p=128))
        gw_final = b.cnt["g_wa"]

        b.wait("vector", "s_c1", c1_final)
        b.emit("vector", lambda e: e.tensor_scalar_mul(bqc32[:], bqc[:], SM_SCALE),
               r=[bqc], w=[bqc32], inc="vm")

        kslice = [psA[:, 0:512], psA[:, 512:1024], psB[:, 0:512], psB[:, 512:1024]]
        qslice = [psC[:, 0:512], psC[:, 512:1024]]
        gslice = psC[:, 0:NEXP]

        bankg = {}
        trg = {}

        sm_xn = {}
        vm_ln = {}
        vm_xnt = {}
        tr_mm = {}
        gmm_d = {}
        en_ev = {}
        tm_smm = {}
        vm_fin = {}
        qmm_last = {}

        def stage_A(c):
            if c < 4:
                b.wait("sync", "s_c1", c1_final + 16 * c)
                b.emit("sync",
                       lambda e, d=xbuf[c % 4], s=x_d[c * 128:(c + 1) * 128, :]:
                       e.dma_start(out=d, in_=s), inc="s_c1", n=16)
            else:
                b.wait("gpsimd", "sm", sm_xn.get(c - 4))
                b.emit("gpsimd",
                       lambda e, d=xbuf[c % 4], s=x_d[c * 128:(c + 1) * 128, :]:
                       e.dma_start(out=d, in_=s), inc=f"s_x{c % 4}", n=16)

        def stage_B(c):
            if c < 4:
                b.wait("vector", "s_c1", c1_final + 16 * (c + 1))
            else:
                b.wait("vector", f"s_x{c % 4}", 16 * (c // 4))
            b.wait("vector", "sm", sm_xn.get(c - 2))
            b.emit("vector", lambda e, xb=xbuf[c % 4]:
                   e.bn_stats(stats[:, 0:6], xb[:, 0:512]),
                   r=[xbuf[c % 4]], w=[stats])
            b.emit("vector", lambda e, xb=xbuf[c % 4]:
                   e.bn_stats(stats[:, 6:12], xb[:, 512:1024]),
                   r=[xbuf[c % 4]], w=[stats])
            b.emit("vector", lambda e: e.bn_aggr(aggr[:], stats[:]),
                   r=[stats], w=[aggr])
            b.emit("vector", lambda e, d=lnv[c % 2]:
                   e.tensor_scalar_add(d, aggr[:, 1:2], EPS),
                   r=[aggr], w=[lnv[c % 2]])
            vm_ln[c] = b.emit("vector", lambda e, d=negm[c % 2]:
                              e.tensor_scalar_mul(d, aggr[:, 0:1], -1.0),
                              r=[aggr], w=[negm[c % 2]], inc="vm")

        def stage_C(c):
            b.wait("scalar", "vm", vm_ln[c])
            b.emit("scalar", lambda e, s=lnv[c % 2]:
                   e.activation(rstd_s[:], s, AF.Ln),
                   r=[lnv[c % 2]], w=[rstd_s])
            b.emit("scalar", lambda e: e.activation(rstd_s[:], rstd_s[:],
                                                    AF.Exp, scale=-0.5),
                   r=[rstd_s], w=[rstd_s])
            b.emit("scalar", lambda e, s=negm[c % 2]:
                   e.mul(nmr_s[:], s, rstd_s[:]),
                   r=[negm[c % 2], rstd_s], w=[nmr_s])
            sm_xn[c] = b.emit("scalar",
                              lambda e, xb=xbuf[c % 4], d=xn[:, c, :]:
                              e.activation(d, xb, AF.Identity, scale=rstd_s[:],
                                           bias=nmr_s[:]),
                              r=[xbuf[c % 4], rstd_s, nmr_s], w=[xn], inc="sm")

        def stage_D(c):
            b.wait("tensor", "s_c1", c1_final)
            b.wait("tensor", "sm", sm_xn[c])
            b.wait_g("tensor", trg.get(c % 2))
            for j in range(ND):
                fn = (lambda e, o=pT[c % 2][:, j * 128:(j + 1) * 128],
                      i_=xn[:, c, j * 128:(j + 1) * 128]:
                      e.transpose(o, i_, idb))
                if j == ND - 1:
                    tr_mm[c] = b.emit("tensor", fn, inc="tm")
                else:
                    b.emit("tensor", fn)

        def stage_E(c):
            ch, ci = c // TPC, c % TPC
            b.wait("vector", "tm", tr_mm[c])
            if ci == 0 and ch >= 2:
                b.wait("vector", f"s_sp{ch % 2}", 96 * ((ch - 2) // 2 + 1))
                b.wait("vector", "tm", qmm_last.get(ch - 2))
                b.wait("vector", "tm", gmm_d.get((ch - 2) * TPC + TPC - 1))
            m = b.emit("vector",
                       lambda e, d=xnTb[ch % 2][:, :, ci * 128:(ci + 1) * 128],
                       s=pT[c % 2].rearrange("p (j t) -> p j t", j=ND):
                       e.tensor_copy(d, s), w=[xnTb[ch % 2]], inc="vm")
            vm_xnt[c] = m
            trg[c % 2] = ("vm", m)

        def stage_F(c):
            ch, ci = c // TPC, c % TPC
            if c == 0:
                b.wait("tensor", "g_wa", gw_final)
            b.wait("tensor", "vm", vm_xnt[c])
            b.wait_g("tensor", bankg.get("q0"))
            b.wait_g("tensor", bankg.get("gate"))
            for j in range(ND):
                fn = (lambda e, l=xnTb[ch % 2][:, j, ci * 128:(ci + 1) * 128],
                      r_=gw_sb[:, j, :], st=(j == 0), sp=(j == ND - 1):
                      e.matmul(gslice, l, r_, start=st, stop=sp))
                if j == ND - 1:
                    gmm_d[c] = b.emit("tensor", fn, inc="tm")
                else:
                    b.emit("tensor", fn)

        def stage_G(c):
            b.wait("scalar", "tm", gmm_d[c])
            en_ev[c] = b.emit("scalar",
                              lambda e, d=en_all[:, c, :]:
                              e.activation(d, gslice, AF.Exp, scale=-1.0),
                              w=[en_all], inc="sm")
            bankg["gate"] = ("sm", en_ev[c])

        def stage_H(c):
            b.wait("vector", "sm", en_ev[c])
            b.emit("vector", lambda e, s=en_all[:, c, :]:
                   e.tensor_scalar_add(sgt[:], s, 1.0), r=[en_all], w=[sgt])
            b.emit("vector", lambda e: e.reciprocal(sg[:], sgt[:]),
                   r=[sgt], w=[sg])
            b.emit("vector", lambda e: e.tensor_add(bsd[:], sg[:], ebc[:]),
                   r=[sg, ebc], w=[bsd])
            b.emit("vector", lambda e: e.max(top8[:], bsd[:]),
                   r=[bsd], w=[top8])
            b.emit("vector", lambda e, d=mask_all[:, c, :]: e.tensor_scalar(
                d, bsd[:], top8[:, 1:2], None, ALU.is_ge),
                   r=[bsd, top8], w=[mask_all])
            b.emit("vector", lambda e, mk=mask_all[:, c, :]:
                   e.tensor_mul(smv[:], sg[:], mk),
                   r=[sg, mask_all], w=[smv])
            b.emit("vector", lambda e, d=s2tok[:, c, :]:
                   e.tensor_mul(d, smv[:], smv[:]),
                   r=[smv], w=[s2tok], inc="vm")

        def chunk_work(ch):
            if ch == 0:
                b.wait("tensor", "g_wa", gw_final)
            b.wait("tensor", "vm", vm_xnt[ch * TPC + TPC - 1])
            k_ev_last = None
            for j2 in range(ND):
                key = f"k{j2 % 4}"
                sl = kslice[j2 % 4]
                b.wait_g("tensor", bankg.get(key))
                for j in range(ND):
                    fn = (lambda e, o=sl,
                          l=wk_sb[:, j, j2 * 128:(j2 + 1) * 128],
                          r_=xnTb[ch % 2][:, j, :], st=(j == 0), sp=(j == ND - 1):
                          e.matmul(o, l, r_, start=st, stop=sp))
                    if j == ND - 1:
                        m_k = b.emit("tensor", fn, inc="tm")
                    else:
                        b.emit("tensor", fn)
                b.wait("scalar", "tm", m_k)
                if ch >= 2 and j2 == 0:
                    b.wait("scalar", f"s_sp{ch % 2}", 96 * ((ch - 2) // 2 + 1))
                m = b.emit("scalar",
                           lambda e, o=kbuf[ch % 2][:, j2, :], i_=sl,
                           bb=bkc[:, j2:j2 + 1]:
                           e.activation(o, i_, AF.Identity, bias=bb),
                           w=[kbuf[ch % 2]], inc="sm")
                bankg[key] = ("sm", m)
                if j2 == ND - 1:
                    k_ev_last = m
            b.wait("sync", "vm", vm_xnt[ch * TPC + TPC - 1])
            b.emit("sync", lambda e, d=xnT_h[ch * 128:(ch + 1) * 128, :],
                   s=xnTb[ch % 2].rearrange("p j t -> p (j t)"):
                   e.dma_start(out=d, in_=s), inc=f"s_sp{ch % 2}", n=16)
            b.emit("sync", lambda e,
                   d=xn_h[ch * CH:(ch + 1) * CH, :].rearrange(
                       "(c p) d -> p c d", p=128),
                   s=xn[:, ch * TPC:(ch + 1) * TPC, :]:
                   e.dma_start(out=d, in_=s), inc="s_xh", n=16)
            b.wait("sync", "sm", k_ev_last)
            b.emit("sync", lambda e, d=kT_h[ch * 128:(ch + 1) * 128, :],
                   s=kbuf[ch % 2].rearrange("p j t -> p (j t)"):
                   e.dma_start(out=d, in_=s), inc=f"s_sp{ch % 2}", n=16)
            q_ev_last = None
            for j2 in range(ND):
                key = f"q{j2 % 2}"
                sl = qslice[j2 % 2]
                b.wait_g("tensor", bankg.get(key))
                if j2 % 2 == 0:
                    b.wait_g("tensor", bankg.get("gate"))
                for j in range(ND):
                    fn = (lambda e, o=sl,
                          l=wq_sb[:, j, j2 * 128:(j2 + 1) * 128],
                          r_=xnTb[ch % 2][:, j, :], st=(j == 0), sp=(j == ND - 1):
                          e.matmul(o, l, r_, start=st, stop=sp))
                    if j == ND - 1:
                        m_q = b.emit("tensor", fn, inc="tm")
                    else:
                        b.emit("tensor", fn)
                if j2 == ND - 1:
                    qmm_last[ch] = m_q
                b.wait("scalar", "tm", m_q)
                if ch >= 2 and j2 == 0:
                    b.wait("scalar", f"s_sp{ch % 2}", 96 * ((ch - 2) // 2 + 1))
                m = b.emit("scalar",
                           lambda e,
                           o=qbuf[ch % 2][:, :, j2 * 128:(j2 + 1) * 128],
                           i_=sl.rearrange("p (u t) -> p u t", u=TPC),
                           bb=bqc32[:, j2:j2 + 1]:
                           e.activation(o, i_, AF.Identity, scale=SM_SCALE,
                                        bias=bb),
                           w=[qbuf[ch % 2]], inc="sm")
                bankg[key] = ("sm", m)
                q_ev_last = m
            b.wait("sync", "sm", q_ev_last)
            for u2 in range(TPC):
                t_ = ch * TPC + u2
                b.emit("sync", lambda e, d=qT_h[t_ * 128:(t_ + 1) * 128, :],
                       s=qbuf[ch % 2][:, u2, :]:
                       e.dma_start(out=d, in_=s), inc=f"s_sp{ch % 2}", n=16)

        wcg = {0: None, 1: None}
        ph0_state = {}

        def ph0_wcomb():
            b.wait("tensor", "g_wb", wv_final)
            for m in range(ND):
                wpr = [psB, psC][m % 2]
                b.wait_g("tensor", wcg[m % 2])
                for j in range(ND):
                    for hf in range(2):
                        fn = (lambda e, o=wpr[:, hf * 512:(hf + 1) * 512],
                              l=wve_sb[:, j, m * 128:(m + 1) * 128],
                              r_=wpt_sb[:, j, hf * 512:(hf + 1) * 512],
                              st=(j == 0), sp=(j == ND - 1):
                              e.matmul(o, l, r_, start=st, stop=sp))
                        if j == ND - 1 and hf == 1:
                            m_wc = b.emit("tensor", fn, inc="tm")
                        else:
                            b.emit("tensor", fn)
                b.wait("scalar", "tm", m_wc)
                m_wce = b.emit("scalar", lambda e, o=wc_sb[:, m, :], s=wpr:
                               e.activation(o, s, AF.Identity),
                               w=[wc_sb], inc="sm")
                wcg[m % 2] = ("sm", m_wce)
            b.wait_g("tensor", wcg[1])
            for j in range(ND):
                for hf in range(2):
                    fn = (lambda e, o=psC[0:1, hf * 512:(hf + 1) * 512],
                          l=bvc[:, j:j + 1],
                          r_=wpt_sb[:, j, hf * 512:(hf + 1) * 512],
                          st=(j == 0), sp=(j == ND - 1):
                          e.matmul(o, l, r_, start=st, stop=sp))
                    if j == ND - 1 and hf == 1:
                        ph0_state["m_bp"] = b.emit("tensor", fn, inc="tm")
                    else:
                        b.emit("tensor", fn)

        def ph0_finish():
            b.wait("vector", "tm", ph0_state["m_bp"])
            b.wait("vector", "s_c1", c1_final)
            m_bpr = b.emit("vector", lambda e: e.tensor_add(
                bprime[:], psC[0:1, :], bpr[:]),
                r=[bpr], w=[bprime], inc="vm")
            b.wait("sync", "vm", m_bpr)
            b.emit("sync", lambda e: e.dma_start(out=bprime_h, in_=bprime[:]),
                   inc="s_bp", n=16)
            b.wait("sync", "s_bp", 16)
            b.emit("sync", lambda e: e.dma_start(
                out=bpbc[:], in_=bprime_h.broadcast_to([128, D])),
                inc="s_bp", n=16)
            bankg["k2"] = bankg["k3"] = wcg[0]
            bankg["q0"] = bankg["q1"] = bankg["gate"] = ("vm", m_bpr)
        bp_final = 32

        for step in range(NT + 6):
            if step < NT:
                stage_A(step)
                stage_B(step)
            if step == 1:
                ph0_wcomb()
            if step == 4:
                ph0_finish()
            c = step - 1
            if 0 <= c < NT:
                stage_C(c)
            c = step - 2
            if 0 <= c < NT:
                stage_D(c)
                stage_E(c)
            c = step - 3
            if 0 <= c < NT:
                if c % TPC == TPC - 1:
                    chunk_work(c // TPC)
            c = step - 4
            if 0 <= c < NT:
                stage_F(c)
                stage_G(c)
            c = step - 5
            if 0 <= c < NT:
                stage_H(c)

        b.barrier()
        pha.close()

        phb = contextlib.ExitStack()
        _b2 = phb.enter_context
        kT = _b2(nc.sbuf_tensor("kT", [128, ND, S], BF16))[:]

        def precast_weights():
            for g9 in range(9):
                wcast(wupb_h[g9 * 128:(g9 + 1) * 128, :].rearrange(
                          "p (jd h) -> p jd h", jd=ND),
                      wup_d[g9 * D:(g9 + 1) * D, :].rearrange(
                          "(jd p) h -> p jd h", p=128), "g_wc")
                wcast(wdnb_h[g9 * 128:(g9 + 1) * 128, :].rearrange(
                          "p (h d) -> p h d", h=NH),
                      wdn_d[g9 * HID:(g9 + 1) * HID, :].rearrange(
                          "(h p) d -> p h d", p=128), "g_wc")

        for jq in range(ND):
            b.emit("sync" if jq < 4 else "scalar", lambda e, j_=jq: e.dma_start(
                out=kT.rearrange("p j (c t) -> p j c t", t=CH)[:, j_:j_ + 1],
                in_=kT_h.rearrange("(c p) (j t) -> p j c t",
                                   p=128, t=CH)[:, j_:j_ + 1]),
                inc="s_kt", n=16)
        kt_final = b.cnt["s_kt"]

        qT_ld = [_b2(nc.sbuf_tensor(f"qT_ld{i}", [128, ND * 128], BF16))[:]
                 for i in range(2)]
        P_sb = [_b2(nc.sbuf_tensor(f"P_sb{i}", [128, KC], BF16))[:]
                for i in range(2)]
        PT_sb = [_b2(nc.sbuf_tensor(f"PT_sb{i}", [128, 8, 128], BF16))[:]
                 for i in range(2)]
        oacc = _b2(nc.sbuf_tensor("oacc", [128, D], F32))[:]
        outp = _b2(nc.sbuf_tensor("outp", [128, D], BF16))[:]
        outpT = _b2(nc.sbuf_tensor("outpT", [128, ND, 128], BF16))[:]
        atile = [_b2(nc.sbuf_tensor(f"atile{i}", [128, D], BF16))[:]
                 for i in range(2)]
        zac2 = [_b2(nc.sbuf_tensor(f"zacc{i}", [128, 1], F32))[:]
                for i in range(2)]
        zc = _b2(nc.sbuf_tensor("zc", [128, 1], F32))[:]
        nmx = [_b2(nc.sbuf_tensor(f"nmx{i}", [128, 1], F32))[:] for i in range(4)]
        nmc = _b2(nc.sbuf_tensor("nmc", [128, 1], F32))[:]
        dal = [_b2(nc.sbuf_tensor(f"dal{i}", [128, 1], F32))[:] for i in range(2)]
        alp2 = [_b2(nc.sbuf_tensor(f"alp{i}", [128, 1], F32))[:]
                for i in range(4)]
        rz = _b2(nc.sbuf_tensor("rz", [128, 1], F32))[:]


        BIG = 65536.0
        ixs = contextlib.ExitStack()
        _ix = ixs.enter_context
        inclt2 = [_ix(nc.sbuf_tensor(f"inclt{i}", [128, NT], F32))[:]
                  for i in range(2)]
        inclt = inclt2[0]
        cnt_all = _ix(nc.sbuf_tensor("cnt_all", [128, NEXP], F32))[:]
        cntT = _ix(nc.sbuf_tensor("cntT", [NEXP, 128], F32))[:]
        c_incl = _ix(nc.sbuf_tensor("c_incl", [NEXP, 128], F32))[:]
        c_excl = _ix(nc.sbuf_tensor("c_excl", [NEXP, 128], F32))[:]
        tot8 = _ix(nc.sbuf_tensor("tot8", [NEXP, 1], F32))[:]
        ccnt = _ix(nc.sbuf_tensor("ccnt", [NEXP, 1], F32))[:]
        tfr = _ix(nc.sbuf_tensor("tfr", [NEXP, 1], F32))[:]
        cc_row = _ix(nc.sbuf_tensor("cc_row", [1, NEXP], F32))[:]
        ti_row = _ix(nc.sbuf_tensor("ti_row", [1, NEXP], F32))[:]
        S_row = _ix(nc.sbuf_tensor("S_row", [1, NEXP], F32))[:]
        S_col = _ix(nc.sbuf_tensor("S_col", [NEXP, 1], F32))[:]
        rbase = _ix(nc.sbuf_tensor("rbase", [NEXP, 1], F32))[:]
        off_pg = _ix(nc.sbuf_tensor("off_pg", [NEXP, 128], F32))[:]
        off_tok = _ix(nc.sbuf_tensor("off_tok", [128, NEXP], F32))[:]
        posb2 = [_ix(nc.sbuf_tensor(f"posb{i}", [128, NT], F32))[:]
                 for i in range(2)]
        pos22 = [_ix(nc.sbuf_tensor(f"pos2{i}", [128, NT], F32))[:]
                 for i in range(2)]
        posm2 = [_ix(nc.sbuf_tensor(f"posm{i}", [128, NT], I32))[:]
                 for i in range(2)]
        stg2 = [_ix(nc.sbuf_tensor(f"stg{i}", [128, NT, 2], I32))[:]
                for i in range(2)]
        iota_id = _ix(nc.sbuf_tensor("iota_id", [128, NT], I32))[:]
        jj8 = _ix(nc.sbuf_tensor("jj8", [NEXP, NCNK], I32))[:]
        jjf = _ix(nc.sbuf_tensor("jjf", [NEXP, NCNK], F32))[:]
        geb = _ix(nc.sbuf_tensor("geb", [NEXP, NCNK], F32))[:]
        ones8 = _ix(nc.sbuf_tensor("ones8", [NEXP, 1], F32))[:]
        expf = _ix(nc.sbuf_tensor("expf", [1, NCNK], F32))[:]
        expbc = _ix(nc.sbuf_tensor("expbc", [128, NCNK], F32))[:]
        iotap_i = _ix(nc.sbuf_tensor("iotap_i", [128, NCNK], I32))[:]
        iotap_f = _ix(nc.sbuf_tensor("iotap_f", [128, NCNK], F32))[:]
        idxgw_f = _ix(nc.sbuf_tensor("idxgw_f", [128, NCNK], F32))[:]
        init_sb = _ix(nc.sbuf_tensor("init_sb", [128, CAPR * 2 // 128], I32))[:]

        m_ones = b.emit("vector", lambda e: e.memset(ones8, 1.0),
                        w=[ones8], inc="vm")
        m_init = b.emit("vector", lambda e: e.memset(init_sb, S),
                        w=[init_sb], inc="vm")
        b.wait("gpsimd", "vm", m_init)
        b.emit("gpsimd", lambda e: e.iota(iota_id, pattern=[[128, NT]], base=0,
                                          channel_multiplier=1))
        b.emit("gpsimd", lambda e: e.iota(jj8, pattern=[[1, NCNK]], base=0,
                                          channel_multiplier=0))
        m_iot = b.emit("gpsimd", lambda e: e.iota(
            iotap_i, pattern=[[0, NCNK]], base=0, channel_multiplier=1),
            inc="gm")
        b.emit("gpsimd", lambda e: e.dma_start(out=idx_h, in_=init_sb),
               inc="g_ix", n=16)
        ix_init = b.cnt["g_ix"]
        b.wait("vector", "gm", m_iot)
        b.emit("vector", lambda e: e.tensor_copy(jjf, jj8), r=[jj8], w=[jjf])
        b.emit("vector", lambda e: e.tensor_copy(iotap_f, iotap_i),
               r=[iotap_i], w=[iotap_f])
        for g in range(NEXP):
            b.emit("vector", lambda e, mk=mask_all[:, :, g]:
                   e.tensor_tensor_scan(inclt, mk, mk, 0.0, ALU.add, ALU.bypass),
                   r=[mask_all], w=[inclt])
            b.emit("vector", lambda e, d=cnt_all[:, g:g + 1]:
                   e.tensor_copy(d, inclt[:, NT - 1:NT]),
                   r=[inclt], w=[cnt_all])
        m1 = b.emit("vector", lambda e: e.engine_nop(), inc="vm")
        b.wait("gpsimd", "vm", m1)
        b.emit("gpsimd", lambda e: e.dma_start(
            out=scr_h[0:NEXP, 0:128].rearrange("g p -> p g"), in_=cnt_all,
            allow_slow_non_contiguous=True),
            inc="g_ix", n=16)
        b.wait("gpsimd", "g_ix", b.cnt["g_ix"])
        b.emit("gpsimd", lambda e: e.dma_start(out=cntT, in_=scr_h[0:NEXP, 0:128]),
               inc="g_ix", n=16)
        b.wait("vector", "g_ix", b.cnt["g_ix"])
        b.emit("vector", lambda e: e.tensor_tensor_scan(
            c_incl, cntT, cntT, 0.0, ALU.add, ALU.bypass), r=[cntT], w=[c_incl])
        b.emit("vector", lambda e: e.tensor_sub(c_excl, c_incl, cntT),
               r=[c_incl, cntT], w=[c_excl])
        b.emit("vector", lambda e: e.tensor_copy(tot8, c_incl[:, 127:128]),
               r=[c_incl], w=[tot8])
        b.emit("vector", lambda e: e.tensor_scalar_add(ccnt, tot8, 511.0),
               r=[tot8], w=[ccnt])
        b.emit("vector", lambda e: e.tensor_scalar_mul(ccnt, ccnt, 1.0 / 512.0),
               r=[ccnt], w=[ccnt])
        b.emit("vector", lambda e: e.tensor_scalar_add(tfr, ccnt, 8388608.0),
               r=[ccnt], w=[tfr])
        b.emit("vector", lambda e: e.tensor_scalar_add(tfr, tfr, -8388608.0),
               r=[tfr], w=[tfr])
        b.emit("vector", lambda e: e.tensor_tensor(tot8, tfr, ccnt, ALU.is_gt),
               r=[tfr, ccnt], w=[tot8])
        m2 = b.emit("vector", lambda e: e.tensor_sub(ccnt, tfr, tot8),
                    r=[tfr, tot8], w=[ccnt], inc="vm")
        b.wait("gpsimd", "vm", m2)
        b.emit("gpsimd", lambda e: e.dma_start(
            out=scr_h[8:9, 0:NEXP].rearrange("o g -> g o"), in_=ccnt,
            allow_slow_non_contiguous=True),
            inc="g_ix", n=16)
        b.wait("gpsimd", "g_ix", b.cnt["g_ix"])
        b.emit("gpsimd", lambda e: e.dma_start(out=cc_row, in_=scr_h[8:9, 0:NEXP]),
               inc="g_ix", n=16)
        b.wait("vector", "g_ix", b.cnt["g_ix"])
        b.emit("vector", lambda e: e.tensor_tensor_scan(
            ti_row, cc_row, cc_row, 0.0, ALU.add, ALU.bypass),
            r=[cc_row], w=[ti_row])
        m3 = b.emit("vector", lambda e: e.tensor_sub(S_row, ti_row, cc_row),
                    r=[ti_row, cc_row], w=[S_row], inc="vm")
        b.wait("gpsimd", "vm", m3)
        b.emit("gpsimd", lambda e: e.dma_start(
            out=scr_h[9:10, 0:NEXP], in_=S_row), inc="g_ix", n=16)
        b.wait("gpsimd", "g_ix", b.cnt["g_ix"])
        b.emit("gpsimd", lambda e: e.dma_start(
            out=S_col, in_=scr_h[9:10, 0:NEXP].rearrange("o g -> g o"),
            allow_slow_non_contiguous=True),
            inc="g_ix", n=16)
        b.wait("vector", "g_ix", b.cnt["g_ix"])
        b.emit("vector", lambda e: e.tensor_scalar_mul(rbase, S_col, 512.0),
               r=[S_col], w=[rbase])
        m4 = b.emit("vector", lambda e: e.tensor_scalar(
            off_pg, c_excl, rbase, None, ALU.add),
            r=[c_excl, rbase], w=[off_pg], inc="vm")
        b.wait("gpsimd", "vm", m4)
        b.emit("gpsimd", lambda e: e.dma_start(
            out=scr_h[16:16 + NEXP, 0:128], in_=off_pg), inc="g_ix", n=16)
        b.wait("gpsimd", "g_ix", b.cnt["g_ix"])
        b.emit("gpsimd", lambda e: e.dma_start(
            out=off_tok, in_=scr_h[16:16 + NEXP, 0:128].rearrange("g p -> p g"),
            allow_slow_non_contiguous=True),
            inc="g_ix", n=16)
        ix_pre = b.cnt["g_ix"]

        def ix_expert(g):
            gp = g % 2
            inc_, pb, p2_, pm, sg2 = (inclt2[gp], posb2[gp], pos22[gp],
                                      posm2[gp], stg2[gp])
            if g == 0:
                b.wait("vector", "g_ix", ix_pre)
            b.emit("vector", lambda e, mk=mask_all[:, :, g]:
                   e.tensor_tensor_scan(inc_, mk, mk, 0.0, ALU.add, ALU.bypass),
                   r=[mask_all], w=[inc_])
            b.emit("vector", lambda e, mk=mask_all[:, :, g]:
                   e.scalar_tensor_tensor(pb, mk, -1.0, inc_, ALU.mult, ALU.add),
                   r=[mask_all, inc_], w=[pb])
            b.emit("vector", lambda e, oc=off_tok[:, g:g + 1]:
                   e.tensor_scalar(pb, pb, oc, None, ALU.add),
                   r=[pb, off_tok], w=[pb])
            b.emit("vector", lambda e: e.tensor_scalar_add(p2_, pb, BIG),
                   r=[pb], w=[p2_])
            b.emit("vector", lambda e, mk=mask_all[:, :, g]:
                   e.scalar_tensor_tensor(pb, mk, -BIG, p2_, ALU.mult, ALU.add),
                   r=[mask_all, p2_], w=[pb])
            b.emit("vector", lambda e: e.tensor_copy(pm, pb),
                   r=[pb], w=[pm])
            b.emit("vector", lambda e: e.tensor_copy(
                sg2[:, :, 0:1].rearrange("p t o -> p (t o)"), iota_id),
                r=[iota_id], w=[sg2])
            mg = b.emit("vector", lambda e, s2=s2tok[:, :, g:g + 1].bitcast(I32):
                        e.tensor_copy(sg2[:, :, 1:2].rearrange("p t o -> p (t o)"),
                                      s2.rearrange("p t o -> p (t o)")),
                        r=[s2tok], w=[sg2], inc="vm")
            ix_mg[g] = mg

        def ix_scatters(g, c0, c1):
            gp = g % 2
            pm, sg2 = posm2[gp], stg2[gp]
            b.wait("gpsimd", "vm", ix_mg[g])
            b.wait("gpsimd", "g_ix", b.cnt["g_ix"])
            for c in range(c0, c1):
                b.emit("gpsimd", lambda e, po=pm[:, c:c + 1],
                       sg_=sg2[:, c, :]:
                       e.indirect_dma_start(
                           out=idx_h, out_offset=IndirectOffsetOnAxis(po, 0),
                           in_=sg_, in_offset=None,
                           bounds_check=regs["capr"], oob_is_err=False),
                       inc="g_ix", n=16)
            return b.cnt["g_ix"]

        ix_scat = {}
        ix_mg = {}

        pairg = {0: None, 1: None}
        avg_g = {"c": None}
        ptg = {0: None, 1: None}
        sm_exp = {}
        sm_pt = {}
        m_s_d = {}
        m_av_d = {}

        def p2_scores_mm(t, u):
            seq = t * NKC + u
            pr = pairs[seq % 2]
            b.wait("tensor", f"s_qt{t % 2}", 16 * (t // 2 + 1))
            if t == 0 and u == 0:
                b.wait("tensor", "s_kt", kt_final)
            b.wait_g("tensor", pairg[seq % 2])
            for j in range(ND):
                for hf in range(2):
                    fn = (lambda e, o=pr[:, hf * 512:(hf + 1) * 512],
                          l=qT_ld[t % 2][:, j * 128:(j + 1) * 128],
                          r_=kT[:, j, u * KC + hf * 512:u * KC + (hf + 1) * 512],
                          st=(j == 0), sp=(j == ND - 1):
                          e.matmul(o, l, r_, start=st, stop=sp))
                    if j == ND - 1 and hf == 1:
                        m_s = b.emit("tensor", fn, inc="tm")
                    else:
                        b.emit("tensor", fn)
            if u == NKC - 1:
                tm_smm[t] = m_s
            m_s_d[(t, u)] = m_s

        def p2_scores_max(t, u):
            seq = t * NKC + u
            pr = pairs[seq % 2]
            b.wait("vector", "tm", m_s_d[(t, u)])
            b.emit("vector", lambda e, s=pr: e.reduce_max(
                nmc[:], s, axis=AX.X, negate=True), w=[nmc])
            if u == 0:
                m_mx = b.emit("vector", lambda e, d=nmx[seq % 4]:
                              e.tensor_copy(d, nmc[:]), r=[nmc],
                              w=[nmx[seq % 4]], inc="vm")
            else:
                b.emit("vector", lambda e, d=nmx[seq % 4], a=nmx[(seq - 1) % 4]:
                       e.tensor_tensor(d, a, nmc[:], ALU.min),
                       r=[nmx[(seq - 1) % 4], nmc], w=[nmx[seq % 4]])
                m_mx = b.emit("vector", lambda e, a=nmx[(seq - 1) % 4],
                              n_=nmx[seq % 4], dd=dal[seq % 2]:
                              e.tensor_sub(dd, n_, a),
                              r=[nmx[seq % 4], nmx[(seq - 1) % 4]],
                              w=[dal[seq % 2]], inc="vm")
            return m_mx

        def p2_scores_exp(t, u, m_mx):
            seq = t * NKC + u
            pr = pairs[seq % 2]
            al = alp2[seq % 4]
            b.wait("scalar", "vm", m_mx)
            if u > 0:
                b.emit("scalar", lambda e, a=al, dd=dal[seq % 2]:
                       e.activation(a, dd, AF.Exp), r=[dal[seq % 2]], w=[al])
            sm_exp[(t, u)] = b.emit(
                "scalar", lambda e, o=P_sb[seq % 2], s=pr, n_=nmx[seq % 4]:
                e.activation(o, s, AF.Exp, bias=n_, accum_out=zc[:]),
                r=[nmx[seq % 4]], w=[P_sb[seq % 2], zc], inc="sm")
            pairg[seq % 2] = ("sm", sm_exp[(t, u)])

        def p2_scores_z(t, u):
            seq = t * NKC + u
            al = alp2[seq % 4]
            za = zac2[t % 2]
            b.wait("vector", "sm", sm_exp[(t, u)])
            if u == 0:
                b.emit("vector", lambda e, z=za: e.tensor_copy(z, zc[:]),
                       r=[zc], w=[za])
            else:
                b.emit("vector", lambda e, z=za, a=al: e.scalar_tensor_tensor(
                    z, z, a, zc[:], ALU.mult, ALU.add),
                    r=[za, al, zc], w=[za])

        def p2_pt_tr(t, u):
            seq = t * NKC + u
            b.wait("tensor", "sm", sm_exp[(t, u)])
            b.wait_g("tensor", ptg[seq % 2])
            for i in range(8):
                fn = (lambda e, o=pT[seq % 2][:, i * 128:(i + 1) * 128],
                      i_=P_sb[seq % 2][:, i * 128:(i + 1) * 128]:
                      e.transpose(o, i_, idb))
                if i == 7:
                    m_pt = b.emit("tensor", fn, inc="tm")
                else:
                    b.emit("tensor", fn)
            return m_pt

        def p2_pt_evac(t, u, m_pt):
            seq = t * NKC + u
            b.wait("scalar", "tm", m_pt)
            sm_pt[(t, u)] = b.emit(
                "scalar", lambda e, d=PT_sb[seq % 2],
                s=pT[seq % 2].rearrange("p (i t) -> p i t", i=8):
                e.activation(d, s, AF.Identity), w=[PT_sb[seq % 2]], inc="sm")
            ptg[seq % 2] = ("sm", sm_pt[(t, u)])

        def p2_av(t, u):
            seq = t * NKC + u
            b.wait("tensor", "sm", sm_pt[(t, u)])
            b.wait_g("tensor", avg_g["c"])
            for i in range(8):
                for hf in range(2):
                    fn = (lambda e, o=psC[:, hf * 512:(hf + 1) * 512],
                          l=PT_sb[seq % 2][:, i, :],
                          r_=xn[:, u * 8 + i, hf * 512:(hf + 1) * 512],
                          st=(i == 0), sp=(i == 7):
                          e.matmul(o, l, r_, start=st, stop=sp))
                    if i == 7 and hf == 1:
                        m_av = b.emit("tensor", fn, inc="tm")
                    else:
                        b.emit("tensor", fn)
            m_av_d[(t, u)] = m_av

        def p2_oacc(t, u):
            seq = t * NKC + u
            al = alp2[seq % 4]
            b.wait("vector", "tm", m_av_d[(t, u)])
            if u == 0:
                m_oe = b.emit("vector", lambda e: e.tensor_copy(oacc[:], psC[:]),
                              w=[oacc], inc="vm")
            else:
                m_oe = b.emit("vector", lambda e, a=al: e.scalar_tensor_tensor(
                    oacc[:], oacc[:], a, psC[:], ALU.mult, ALU.add),
                    r=[oacc, al], w=[oacc], inc="vm")
            avg_g["c"] = ("vm", m_oe)

        def qt_load(t):
            if t >= NT:
                return
            b.wait("sync", "tm", tm_smm.get(t - 2))
            b.emit("sync", lambda e, d=qT_ld[t % 2],
                   s=qT_h[t * 128:(t + 1) * 128, :]:
                   e.dma_start(out=d, in_=s), inc=f"s_qt{t % 2}", n=16)

        b.wait("vector", "vm", m_ones)
        mge = b.emit("vector", lambda e: e.tensor_scalar(
            geb, jjf, S_col, None, ALU.is_ge), r=[jjf, S_col], w=[geb], inc="vm")
        b.wait("tensor", "vm", mge)
        b.wait_g("tensor", avg_g["c"])
        m_ec = b.emit("tensor", lambda e: e.matmul(
            psC[0:1, 0:NCNK], ones8, geb, start=True, stop=True), inc="tm")
        b.wait("vector", "tm", m_ec)
        m_ec2 = b.emit("vector", lambda e: e.tensor_scalar_add(
            expf, psC[0:1, 0:NCNK], -1.0), w=[expf], inc="vm")
        avg_g["c"] = ("vm", m_ec2)
        b.wait("gpsimd", "vm", m_ec2)
        b.emit("gpsimd", lambda e: e.dma_start(
            out=scr_h[10:11, 0:NCNK], in_=expf), inc="g_ib", n=16)
        b.wait("gpsimd", "g_ib", b.cnt["g_ib"])
        b.emit("gpsimd", lambda e: e.dma_start(
            out=expbc, in_=scr_h[10:11, 0:NCNK].broadcast_to([128, NCNK])),
            inc="g_ib", n=16)
        b.wait("vector", "g_ib", b.cnt["g_ib"])
        b.emit("vector", lambda e: e.scalar_tensor_tensor(
            idxgw_f, expbc, 128.0, iotap_f, ALU.mult, ALU.add),
            r=[expbc, iotap_f], w=[idxgw_f])
        m_wix = b.emit("vector", lambda e: e.tensor_copy(idxgw_i, idxgw_f),
                       r=[idxgw_f], w=[idxgw_i], inc="vm")


        qt_load(0)
        qt_load(1)
        for bseq in range(2):
            bt, bu = divmod(bseq, NKC)
            p2_scores_mm(bt, bu)
            m_mxb = p2_scores_max(bt, bu)
            p2_scores_exp(bt, bu, m_mxb)
            p2_scores_z(bt, bu)
        IXSPL = 2 if NT >= 2 * NEXP else 1

        def tail_v(t):
            b.emit("vector", lambda e, z=zac2[t % 2]: e.reciprocal(rz[:], z),
                   r=[zac2[t % 2]], w=[rz])
            return b.emit("vector", lambda e: e.tensor_scalar(
                outp[:], oacc[:], rz[:], None, ALU.mult),
                r=[oacc, rz], w=[outp], inc="vm")

        def tail_pe(t, m_op):
            seqlast = t * NKC + NKC - 1
            b.wait("tensor", "vm", m_op)
            b.wait_g("tensor", ptg[seqlast % 2])
            for j in range(ND):
                fn = (lambda e, o=pT[seqlast % 2][:, j * 128:(j + 1) * 128],
                      i_=outp[:, j * 128:(j + 1) * 128]: e.transpose(o, i_, idb))
                if j == ND - 1:
                    m_t1 = b.emit("tensor", fn, inc="tm")
                else:
                    b.emit("tensor", fn)
            b.wait("scalar", "tm", m_t1)
            m_ev1 = b.emit("scalar", lambda e, s=pT[seqlast % 2].rearrange(
                "p (j t) -> p j t", j=ND):
                e.activation(outpT[:], s, AF.Identity), w=[outpT], inc="sm")
            ptg[seqlast % 2] = ("sm", m_ev1)
            b.wait("tensor", "sm", m_ev1)
            b.wait_g("tensor", avg_g["c"])
            for j in range(ND):
                for hf in range(2):
                    fn = (lambda e, o=psC[:, hf * 512:(hf + 1) * 512],
                          l=outpT[:, j, :],
                          r_=wc_sb[:, j, hf * 512:(hf + 1) * 512],
                          st=(j == 0), sp=(j == ND - 1):
                          e.matmul(o, l, r_, start=st, stop=sp))
                    if j == ND - 1 and hf == 1:
                        m_pj = b.emit("tensor", fn, inc="tm")
                    else:
                        b.emit("tensor", fn)
            b.wait("vector", "tm", m_pj)
            b.wait("vector", "s_bp", bp_final)
            if t >= 2:
                b.wait("vector", f"s_ast{t % 2}", 16 * (t // 2))
            m_at = b.emit("vector", lambda e, d=atile[t % 2]:
                          e.tensor_add(d, psC[:], bpbc[:]),
                          r=[bpbc], w=[atile[t % 2]], inc="vm")
            avg_g["c"] = ("vm", m_at)
            b.wait("sync", "vm", m_at)
            b.emit("sync", lambda e, d=attn_h[t * 128:(t + 1) * 128, :],
                   s=atile[t % 2]: e.dma_start(out=d, in_=s),
                   inc=f"s_ast{t % 2}", n=16)

        for t in range(NT):
            if t < IXSPL * NEXP:
                g, h = divmod(t, IXSPL)
                if h == 0:
                    if g >= 2:
                        b.wait("vector", "g_ix", ix_scat[g - 2])
                    ix_expert(g)
                nsc = -(-NT // IXSPL)
                ix_scat[g] = ix_scatters(g, h * nsc, min((h + 1) * nsc, NT))
                if t == IXSPL * NEXP - 1:
                    precast_weights()
            for u in range(NKC):
                m_pt = p2_pt_tr(t, u)
                p2_pt_evac(t, u, m_pt)
                nseq = t * NKC + u + 2
                nxt = (nseq // NKC, nseq % NKC) if nseq < NT * NKC else None
                if nxt:
                    p2_scores_mm(*nxt)
                    m_mx = p2_scores_max(*nxt)
                p2_av(t, u)
                if u == 0 and t >= 1:
                    m_opd = tail_v(t - 1)
                if nxt:
                    p2_scores_exp(*nxt, m_mx)
                    p2_scores_z(*nxt)
                p2_oacc(t, u)
                if u == 0 and t >= 1:
                    tail_pe(t - 1, m_opd)
            qt_load(t + 2)
        m_opd = tail_v(NT - 1)
        tail_pe(NT - 1, m_opd)

        b.barrier()
        ixs.close()
        phb.close()
        xn_stack.close()

        phc = contextlib.ExitStack()
        _c = phc.enter_context
        wupg = [_c(nc.sbuf_tensor(f"wupg{i}", [128, ND, HID], BF16))[:]
                for i in range(2)]
        wdng = [_c(nc.sbuf_tensor(f"wdng{i}", [128, NH, D], BF16))[:]
                for i in range(2)]
        xnTm_u = _c(nc.sbuf_tensor("xnTm_u", [128, 2 * ND * CH], BF16))[:]
        xnTm = [xnTm_u[:, i * ND * CH:(i + 1) * ND * CH].rearrange(
            "p (j t) -> p j t", j=ND) for i in range(2)]
        h_sb = _c(nc.sbuf_tensor("h_sb", [128, NH, CH], BF16))[:]
        rbf = [_c(nc.sbuf_tensor(f"rbf{i}", [128, CH], BF16))[:] for i in range(2)]
        osh = _c(nc.sbuf_tensor("osh", [128, D], F32))[:]
        gbuf = [_c(nc.sbuf_tensor(f"gbuf{i}", [128, TPC, D], BF16))[:]
                for i in range(2)]
        gxT = _c(nc.sbuf_tensor("gxT", [128, ND, CH], BF16))[:]
        obuf = xnTm_u.bitcast(F32).rearrange("p (u d) -> p u d", u=TPC)
        idxg_sb = _c(nc.sbuf_tensor("idxg_sb", [128, NGT, 2], I32))[:]

        upslice = [psA[:, 0:512], psA[:, 512:1024]]
        dnpair = [psB, psC]

        b.emit("scalar", lambda e: e.dma_start(
            out=wupg[0], in_=wupb_h[8 * 128:9 * 128, :].rearrange(
                "p (jd h) -> p jd h", jd=ND)), inc="s_ig", n=16)
        b.emit("gpsimd", lambda e: e.dma_start(
            out=wdng[0], in_=wdnb_h[8 * 128:9 * 128, :].rearrange(
                "p (h d) -> p h d", h=NH)), inc="g_wb", n=16)
        b.emit("scalar", lambda e: e.dma_start(
            out=idxg_sb, in_=idx_h.rearrange("(i p) v -> p i v", p=128)),
            inc="s_mf1", n=16)
        ig_final = b.cnt["s_ig"]
        ig2_final = b.cnt["g_wb"]
        mf1_final = b.cnt["s_mf1"]

        tm_up_done = {}
        tm_dn_done = {}
        tm_tr_done = {}
        vm_hsq = {}
        vm_ev = {}
        up_bank_free = {}
        dn_bank_free = {}
        sh_store = 0

        def moe_up(k, wsrc, rhs_buf, width, gwait=None):
            if gwait:
                b.wait("tensor", gwait[0], gwait[1])
            for h in range(NH):
                bank = h % 2
                b.wait_g("tensor", up_bank_free.get(bank))
                for j in range(ND):
                    fn = (lambda e, o=upslice[bank][:, 0:width],
                          l=wsrc[:, j, h * 128:(h + 1) * 128],
                          r_=rhs_buf[:, j, 0:width], st=(j == 0),
                          sp=(j == ND - 1):
                          e.matmul(o, l, r_, start=st, stop=sp))
                    if j == ND - 1:
                        m_up = b.emit("tensor", fn, inc="tm")
                    else:
                        b.emit("tensor", fn)
                if h == NH - 1:
                    tm_up_done[k] = m_up
                b.wait("scalar", "tm", m_up)
                prev_rbf = (k, h - 2) if h >= 2 else (k - 1, NH - 2 + h)
                b.wait("scalar", "vm", vm_hsq.get(prev_rbf))
                m_rl = b.emit(
                    "scalar", lambda e, o=rbf[h % 2][:, 0:width],
                    s=upslice[bank][:, 0:width]:
                    e.activation(o, s, AF.Relu), w=[rbf[h % 2]], inc="sm")
                up_bank_free[bank] = ("sm", m_rl)
                b.wait("vector", "sm", m_rl)
                b.wait("vector", "tm", tm_dn_done.get((k - 1, TPC - 1)))
                vm_hsq[(k, h)] = b.emit(
                    "vector", lambda e, d=h_sb[:, h, 0:width],
                    s=rbf[h % 2][:, 0:width]:
                    e.tensor_mul(d, s, s), r=[rbf[h % 2]], w=[h_sb], inc="vm")

        def moe_down_mm(k, u2, wsrc):
            pr = dnpair[u2 % 2]
            b.wait("tensor", "vm", vm_ev.get(dn_bank_free.get(u2 % 2)))
            if u2 == 0:
                b.wait("tensor", "vm", vm_hsq[(k, NH - 1)])
            for h in range(NH):
                for hf in range(2):
                    fn = (lambda e, o=pr[:, hf * 512:(hf + 1) * 512],
                          l=h_sb[:, h, u2 * 128:(u2 + 1) * 128],
                          r_=wsrc[:, h, hf * 512:(hf + 1) * 512],
                          st=(h == 0), sp=(h == NH - 1):
                          e.matmul(o, l, r_, start=st, stop=sp))
                    if h == NH - 1 and hf == 1:
                        m_dn = b.emit("tensor", fn, inc="tm")
                    else:
                        b.emit("tensor", fn)
            tm_dn_done[(k, u2)] = m_dn
            dn_bank_free[u2 % 2] = (k, u2)
            return m_dn, pr

        KOFF = NCH

        def routed_gathers(j):
            if j >= NCNK:
                return
            k = KOFF + j
            jp = (j + 1) % 2
            if j >= 2:
                b.wait("gpsimd", "tm", tm_up_done.get(k - 2))
            elif j == 1:
                b.wait("gpsimd", "tm", tm_up_done.get(KOFF - 1))
            else:
                b.wait("gpsimd", "tm", tm_up_done.get(2))
            b.emit("gpsimd", lambda e,
                   o=wupg[jp].rearrange("p jd h -> p (jd h)"),
                   ix=idxgw_i[:, j:j + 1]:
                   e.indirect_dma_start(
                       out=o, out_offset=None, in_=wupb_h,
                       in_offset=IndirectOffsetOnAxis(ix, 0)),
                   inc=f"g_wg{jp}", n=16)
            if j >= 2:
                b.wait("gpsimd", "tm", tm_dn_done.get((k - 2, TPC - 1)))
            elif j == 1:
                b.wait("gpsimd", "tm", tm_dn_done.get((KOFF - 1, TPC - 1)))
            b.emit("gpsimd", lambda e,
                   o=wdng[jp].rearrange("p h d -> p (h d)"),
                   ix=idxgw_i[:, j:j + 1]:
                   e.indirect_dma_start(
                       out=o, out_offset=None, in_=wdnb_h,
                       in_offset=IndirectOffsetOnAxis(ix, 0)),
                   inc=f"g_wg{jp}", n=16)
            if j <= 1:
                b.wait("gpsimd", "s_mf1", mf1_final)
            b.wait("gpsimd", "tm", tm_tr_done.get(k - 2))
            for i in range(TPC):
                b.emit("gpsimd", lambda e, o=gbuf[jp][:, i, :],
                       ix=idxg_sb[:, j * TPC + i, 0:1]:
                       e.indirect_dma_start(
                           out=o, out_offset=None, in_=xn_h,
                           in_offset=IndirectOffsetOnAxis(ix, 0),
                           bounds_check=regs["sm1"], oob_is_err=False),
                       inc=f"g_tg{jp}", n=16)

        def xnm_load(ch):
            if ch >= NCH:
                return
            b.wait("sync", "tm", tm_up_done.get(ch - 2))
            b.emit("sync", lambda e, d=xnTm[ch % 2],
                   s=xnT_h[ch * 128:(ch + 1) * 128, :].rearrange(
                       "p (j t) -> p j t", j=ND):
                   e.dma_start(out=d, in_=s), inc=f"s_xnt{ch % 2}", n=16)

        xnm_load(0)
        for ch in range(NCH):
            k = ch
            xnm_load(ch + 1)
            if ch == NCH - 2:
                routed_gathers(0)
            b.wait("tensor", f"s_xnt{ch % 2}", 16 * (ch // 2 + 1))
            if ch == 0:
                b.wait("tensor", "s_ig", ig_final)
                b.wait("tensor", "g_wb", ig2_final)
            moe_up(k, wupg[0], xnTm[ch % 2], CH)
            for u2 in range(TPC):
                tt = ch * TPC + u2
                m_dn, pr = moe_down_mm(k, u2, wdng[0])
                b.wait("vector", "tm", m_dn)
                if sh_store >= 1:
                    b.wait("vector", "s_sh", 16 * sh_store)
                vm_ev[(k, u2)] = b.emit(
                    "vector", lambda e, s=pr: e.tensor_copy(osh[:], s),
                    w=[osh], inc="vm")
                b.wait("sync", "vm", vm_ev[(k, u2)])
                b.emit("sync", lambda e, d=moe_h[tt * 128:(tt + 1) * 128, :]:
                       e.dma_start(out=d, in_=osh[:]), inc="s_sh", n=16)
                sh_store += 1
        sh_final = 16 * sh_store

        routed_gathers(1)
        n_sca = 0
        for j in range(NCNK):
            k = KOFF + j
            jp = (j + 1) % 2
            b.wait("tensor", f"g_wg{jp}", 16 * 2 * (j // 2 + 1))
            b.wait("tensor", f"g_tg{jp}", 16 * TPC * (j // 2 + 1))
            for i in range(TPC):
                b.wait_g("tensor", ptg[i % 2])
                for jd in range(ND):
                    fn = (lambda e, o=pT[i % 2][:, jd * 128:(jd + 1) * 128],
                          i_=gbuf[jp][:, i, jd * 128:(jd + 1) * 128]:
                          e.transpose(o, i_, idb))
                    if jd == ND - 1:
                        m_tr = b.emit("tensor", fn, inc="tm")
                    else:
                        b.emit("tensor", fn)
                b.wait("scalar", "tm", m_tr)
                m_te = b.emit("scalar", lambda e,
                              d=gxT[:, :, i * 128:(i + 1) * 128],
                              s=pT[i % 2].rearrange("p (jd t) -> p jd t", jd=ND):
                              e.activation(d, s, AF.Identity), w=[gxT], inc="sm")
                ptg[i % 2] = ("sm", m_te)
                if i == TPC - 1:
                    tm_tr_done[k] = m_tr
            moe_up(k, wupg[jp], gxT, CH, gwait=("sm", m_te))
            for u2 in range(TPC):
                gt = j * TPC + u2
                m_dn, pr = moe_down_mm(k, u2, wdng[jp])
                b.wait("vector", "tm", m_dn)
                if j >= 1:
                    pk = (j - 1) * TPC + u2
                    b.wait("vector", f"g_sa{pk % 2}", 16 * (pk // 2 + 1))
                if j == 0 and u2 == 0:
                    b.wait("vector", "s_mf1", mf1_final)
                vm_ev[(k, u2)] = b.emit(
                    "vector", lambda e, d=obuf[:, u2, :], s=pr,
                    s2=idxg_sb[:, gt, 1:2].bitcast(F32):
                    e.tensor_scalar(d, s, s2, None, ALU.mult),
                    r=[idxg_sb], w=[obuf], inc="vm")
                b.wait("gpsimd", "vm", vm_ev[(k, u2)])
                if n_sca == 0:
                    b.wait("gpsimd", "s_sh", sh_final)
                else:
                    p_ = (n_sca - 1) % 2
                    b.wait("gpsimd", f"g_sa{p_}", 16 * ((n_sca - 1) // 2 + 1))
                b.emit("gpsimd", lambda e, ix=idxg_sb[:, gt, 0:1],
                       s=obuf[:, u2, :]:
                       e.indirect_dma_start(
                           out=moe_h, out_offset=IndirectOffsetOnAxis(ix, 0),
                           in_=s, in_offset=None,
                           bounds_check=regs["sm1"], oob_is_err=False,
                           compute_op=ALU.add),
                       inc=f"g_sa{n_sca % 2}", n=16)
                n_sca += 1
            routed_gathers(j + 2)
        sa0_final = b.cnt["g_sa0"]
        sa1_final = b.cnt["g_sa1"]

        asmst = contextlib.ExitStack()
        _as = asmst.enter_context
        xfin = [_as(nc.sbuf_tensor(f"xfin{i}", [128, D], F32))[:] for i in range(2)]
        afin = [_as(nc.sbuf_tensor(f"afin{i}", [128, D], BF16))[:] for i in range(2)]
        mfin = [_as(nc.sbuf_tensor(f"mfin{i}", [128, D], F32))[:] for i in range(2)]
        t1f = _as(nc.sbuf_tensor("t1f", [128, D], F32))[:]
        otile = [_as(nc.sbuf_tensor("otile0", [128, D], F32))[:], osh]
        b.wait("vector", "s_sh", sh_final)
        b.wait("sync", "g_sa0", sa0_final)
        b.wait("sync", "g_sa1", sa1_final)
        for tt in range(NT):
            b.wait("sync", "vm", vm_fin.get(tt - 2))
            b.emit("sync", lambda e, d=xfin[tt % 2],
                   s=x_d[tt * 128:(tt + 1) * 128, :]:
                   e.dma_start(out=d, in_=s), inc=f"s_fin{tt % 2}", n=16)
            b.emit("sync", lambda e, d=afin[tt % 2],
                   s=attn_h[tt * 128:(tt + 1) * 128, :]:
                   e.dma_start(out=d, in_=s), inc=f"s_fin{tt % 2}", n=16)
            b.emit("sync", lambda e, d=mfin[tt % 2],
                   s=moe_h[tt * 128:(tt + 1) * 128, :]:
                   e.dma_start(out=d, in_=s), inc=f"s_fin{tt % 2}", n=16)
            b.wait("vector", f"s_fin{tt % 2}", 48 * (tt // 2 + 1))
            if tt >= 2:
                b.wait("vector", f"s_out{tt % 2}", 16 * (tt // 2))
            b.emit("vector", lambda e, xf=xfin[tt % 2], af=afin[tt % 2]:
                   e.tensor_add(t1f[:], xf, af),
                   r=[xfin[tt % 2], afin[tt % 2]], w=[t1f])
            vm_fin[tt] = b.emit("vector", lambda e, d=otile[tt % 2],
                                m=mfin[tt % 2]: e.tensor_add(d, t1f[:], m),
                                r=[t1f, mfin[tt % 2]], w=[otile[tt % 2]],
                                inc="vm")
            b.wait("sync", "vm", vm_fin[tt])
            b.emit("sync", lambda e, d=out_d[tt * 128:(tt + 1) * 128, :],
                   s=otile[tt % 2]: e.dma_start(out=d, in_=s),
                   inc=f"s_out{tt % 2}", n=16)
        asmst.close()

        finals = dict(b.cnt)
        for s, cfin in finals.items():
            b.wait("sync", s, cfin)
        phc.close()

        with nc.Block() as blk:
            @blk.sync
            def _(e):
                for f in b.th["sync"]:
                    f(e)

            @blk.gpsimd
            def _(e):
                for f in b.th["gpsimd"]:
                    f(e)

            @blk.vector
            def _(e):
                for f in b.th["vector"]:
                    f(e)

            @blk.scalar
            def _(e):
                for f in b.th["scalar"]:
                    f(e)

            @blk.tensor
            def _(e):
                for f in b.th["tensor"]:
                    f(e)

    return nc


def stage_inputs(S, L0_attn_x, L0_attn_W_qkv, L0_attn_b_qkv, L0_attn_W_proj,
                 L0_attn_b_proj, L0_ffn_gate_w, L0_ffn_expert_bias, L0_ffn_w_up,
                 L0_ffn_w_down, L0_ffn_shared_w_up, L0_ffn_shared_w_down):
    asc = np.ascontiguousarray
    f32 = np.float32
    Wqkv = np.asarray(L0_attn_W_qkv, f32)
    bqkv = np.asarray(L0_attn_b_qkv, f32)
    shared = {
        "wqT": asc(Wqkv[0:D].T),
        "wkT": asc(Wqkv[D:2 * D].T),
        "wv_e": asc(Wqkv[2 * D:3 * D]),
        "wpT": asc(np.asarray(L0_attn_W_proj, f32).T),
        "bq_pe": asc(bqkv[0:D].reshape(ND, 128).T),
        "bk_pe": asc(bqkv[D:2 * D].reshape(ND, 128).T),
        "bv_pe": asc(bqkv[2 * D:3 * D].reshape(ND, 128).T),
        "bp_row": asc(np.asarray(L0_attn_b_proj, f32).reshape(1, D)),
        "gwT": asc(np.asarray(L0_ffn_gate_w, f32).T),
        "eb_row": asc(np.asarray(L0_ffn_expert_bias, f32).reshape(1, NEXP)),
        "wupT": asc(np.concatenate(
            [np.asarray(L0_ffn_w_up, f32).transpose(0, 2, 1),
             np.asarray(L0_ffn_shared_w_up, f32).T[None]], axis=0
        ).reshape(9 * D, HID)),
        "wdnT": asc(np.concatenate(
            [np.asarray(L0_ffn_w_down, f32).transpose(0, 2, 1),
             np.asarray(L0_ffn_shared_w_down, f32).T[None]], axis=0
        ).reshape(9 * HID, D)),
        "identb": np.eye(128).astype(ml_dtypes.bfloat16),
    }
    x = np.asarray(L0_attn_x, f32)
    in_maps = []
    for c in range(x.shape[0]):
        m = dict(shared)
        m["x"] = asc(x[c][:S])
        in_maps.append(m)
    return in_maps


def kernel(**inputs):
    S = 4096
    in_maps = stage_inputs(S, **inputs)
    nc = build_nc(S)
    res = run_bass_kernel_spmd(nc, in_maps, core_ids=list(range(8)))
    out = np.stack([np.asarray(res.results[c]["out"]) for c in range(8)], axis=0)
    return out.astype(np.float32)
